# version 01
# baseline (speedup 1.0000x reference)
import os
import numpy as np
import ml_dtypes
import concourse.bass as bass
import concourse.mybir as mybir
from concourse.alu_op_type import AluOpType
from concourse.bass_utils import run_bass_kernel_spmd
from contextlib import ExitStack

F32 = mybir.dt.float32
BF16 = mybir.dt.bfloat16
I32 = mybir.dt.int32
AF = mybir.ActivationFunctionType
OP = AluOpType

H = 4096
DIN = 8192
DH = 4096
NDB = 32
NST = 16
R = 256
L = 4096
TCH = 1024
NCH = L // TCH
THF = 512
TH = 2048
II = 11008
NKB_D = 86
EPS_RMS = 1e-6
EPS_LN = 1e-5

PAIRS = [[0, 1], [2, 3], [4, 5], [6, 7]]
DEBUG = os.environ.get("MK_DEBUG", "") == "1"


class Ctr:
    def __init__(self, stack, nc, name):
        self.sem = stack.enter_context(nc.semaphore(name))
        self.n = 0

    def inc(self, instr, k=1):
        instr.then_inc(self.sem, k)
        self.n += k
        return self.n


def build_nc():
    nc = bass.Bass("TRN2", target_bir_lowering=False)

    def din(name, shape, dt):
        return nc.dram_tensor(name, list(shape), dt, kind="ExternalInput").ap()

    xT = din("xT", (H, L), F32)
    xT_my = din("xT_my", (H, TH), F32)
    winT = din("winT", (H, DIN), F32)
    convw = din("convw", (128, NDB * 4), F32)
    convb = din("convb", (128, NDB), F32)
    bdt = din("bdt", (128, NDB), F32)
    Dv = din("Dv", (128, NDB), F32)
    alog = din("alog", (128, NDB * NST), F32)
    wxT = din("wxT", (DH, 288), F32)
    wdtT = din("wdtT", (R, DH), F32)
    woutT = din("woutT", (DH, H), F32)
    gq = din("gq", (H, II // 8), I32)
    gs = din("gs", (32, II), F32)
    gz = din("gz", (32, II // 8), I32)
    uq = din("uq", (H, II // 8), I32)
    us = din("us", (32, II), F32)
    uz = din("uz", (32, II // 8), I32)
    dq = din("dq", (II, H // 8), I32)
    ds = din("ds", (NKB_D, H), F32)
    dz = din("dz", (NKB_D, H // 8), I32)
    ones_bf = din("ones_bf", (128, 1), BF16)
    oneh32 = din("oneh32", (128, 32 * 32), BF16)
    oneh86 = din("oneh86", (128, 86 * 86), BF16)
    shifts = din("shifts", (128, 8), I32)
    mask15 = din("mask15", (128, 1), I32)
    shifts32 = din("shifts32", (32, 8), I32)
    mask15_32 = din("mask15_32", (32, 1), I32)
    shifts86 = din("shifts86", (86, 8), I32)
    mask15_86 = din("mask15_86", (86, 1), I32)
    constv = din("constv", (128, 8), F32)

    out = nc.dram_tensor("out", [H, TH], F32, kind="ExternalOutput").ap()

    def scratch(name, shape, dt):
        kind = "ExternalOutput" if DEBUG else "Internal"
        return nc.dram_tensor(name, list(shape), dt, kind=kind).ap()

    proj_s = scratch("proj_s", (DIN, L), BF16)
    u_s = scratch("u_s", (DH, L), BF16)
    sg_s = scratch("sg_s", (DH, L), BF16)
    y_s = scratch("y_s", (DH, L), BF16)
    wg_s = scratch("wg_s", (H, II), BF16)
    wu_s = scratch("wu_s", (H, II), BF16)
    wd_s = scratch("wd_s", (II, H), BF16)
    zsg_s = scratch("zsg_s", (32, II), BF16)
    zsu_s = scratch("zsu_s", (32, II), BF16)
    zsd_s = scratch("zsd_s", (86, H), BF16)
    xf_s = scratch("xf_s", (H, TH), BF16)
    r1_s = scratch("r1_s", (H, TH), F32)
    r2_s = scratch("r2_s", (H, TH), F32)
    h_s = scratch("h_s", (II, TH), BF16)
    ssm_dbg = scratch("ssm_dbg", (NCH * 288, TCH), F32) if DEBUG else None
    dt_dbg = scratch("dt_dbg", (DH, TCH), F32) if DEBUG else None

    ar_in = nc.dram_tensor("ar_in", [288, TCH], F32, kind="Internal").ap()
    ar_out = nc.dram_tensor("ar_out", [288, TCH], F32, kind="Internal").ap()
    rs_in = nc.dram_tensor("rs_in", [4 * H, TCH], BF16, kind="Internal").ap()
    rs_out = nc.dram_tensor("rs_out", [2 * H, TCH], BF16, kind="Internal").ap()
    row_bounce = nc.dram_tensor("row_bounce", [64, TCH], F32, kind="Internal").ap()

    es = ExitStack()
    with es:
        def sb(name, shape, dt):
            return es.enter_context(nc.sbuf_tensor(name, list(shape), dt))

        _psum_stack = [es]

        def psum(name, shape):
            return _psum_stack[-1].enter_context(nc.psum_tensor(name, list(shape), F32))

        qS = Ctr(es, nc, "qS")
        qG = Ctr(es, nc, "qG")
        cT = Ctr(es, nc, "cT")
        cV = Ctr(es, nc, "cV")
        cA = Ctr(es, nc, "cA")
        cC = Ctr(es, nc, "cC")

        block = es.enter_context(nc.Block())

        def on(engine_section):
            def run(fn):
                engine_section(fn)
            return run

        s_convw = sb("s_convw", (128, NDB * 4), F32)
        s_convb = sb("s_convb", (128, NDB), F32)
        s_bdt = sb("s_bdt", (128, NDB), F32)
        s_D = sb("s_D", (128, NDB), F32)
        s_A = sb("s_A", (128, NDB * NST), F32)
        s_ones = sb("s_ones", (128, 1), BF16)
        s_shifts = sb("s_shifts", (128, 8), I32)
        s_mask = sb("s_mask", (128, 1), I32)
        s_carry = sb("s_carry", (128, NDB * NST), F32)
        s_cst = sb("s_cst", (128, 8), F32)

        p0s = ExitStack()
        with p0s:
            def sb0(n, s, d):
                return p0s.enter_context(nc.sbuf_tensor(n, list(s), d))
            s_alog = sb0("s_alog", (128, NDB * NST), F32)
            s_shifts32 = sb0("s_shifts32", (32, 8), I32)
            s_mask32 = sb0("s_mask32", (32, 1), I32)
            s_shifts86 = sb0("s_shifts86", (86, 8), I32)
            s_mask86 = sb0("s_mask86", (86, 1), I32)
            s_sflt = sb0("s_sflt", (86, II), F32)
            s_zq = sb0("s_zq", (86, II // 8), I32)
            s_zi = sb0("s_zi", (86, II), I32)
            s_zs = sb0("s_zs", (86, II), BF16)

            @on(block.sync)
            def _(q):
                for dst, src in [(s_convw, convw), (s_convb, convb), (s_bdt, bdt),
                                 (s_D, Dv), (s_ones, ones_bf),
                                 (s_shifts, shifts), (s_mask, mask15),
                                 (s_shifts32, shifts32), (s_mask32, mask15_32),
                                 (s_shifts86, shifts86), (s_mask86, mask15_86),
                                 (s_alog, alog), (s_cst, constv)]:
                    qS.inc(q.dma_start(out=dst[:], in_=src), 16)
            p0_loads = qS.n

            @on(block.scalar)
            def _(a):
                a.wait_ge(qS.sem, p0_loads)
                cA.inc(a.activation(s_A[:], s_alog[:], AF.Exp))
            p0_exp = cA.n

            @on(block.vector)
            def _(v):
                v.wait_ge(cA.sem, p0_exp)
                v.tensor_scalar(s_A[:], s_A[:], -1.0, None, OP.mult)
                cV.inc(v.memset(s_carry[:], 0.0))
            p0_A = cV.n

            def emit_zs(s_in, z_in, zdst, P, N, shp, msk, prev_st):
                @on(block.sync)
                def _(q):
                    if prev_st:
                        q.wait_ge(qS.sem, prev_st)
                    qS.inc(q.dma_start(out=s_sflt[0:P, 0:N], in_=s_in), 16)
                    qS.inc(q.dma_start(out=s_zq[0:P, 0:N // 8], in_=z_in), 16)
                marker = qS.n

                @on(block.vector)
                def _(v):
                    v.wait_ge(qS.sem, marker)
                    v.wait_ge(qS.sem, p0_loads)
                    for j in range(8):
                        v.tensor_scalar(s_zi[0:P, j:N:8], s_zq[0:P, 0:N // 8],
                                        shp[:, j:j + 1], msk[:],
                                        OP.logical_shift_right, OP.bitwise_and)
                    v.tensor_tensor(s_zs[0:P, 0:N], s_zi[0:P, 0:N],
                                    s_sflt[0:P, 0:N], OP.mult)
                    cV.inc(v.tensor_scalar(s_zs[0:P, 0:N], s_zs[0:P, 0:N],
                                           -1.0, None, OP.mult))
                zz = cV.n

                @on(block.sync)
                def _(q):
                    q.wait_ge(cV.sem, zz)
                    qS.inc(q.dma_start(out=zdst, in_=s_zs[0:P, 0:N]), 16)
                return qS.n

            st1 = emit_zs(gs, gz, zsg_s, 32, II, s_shifts32, s_mask32, None)
            st2 = emit_zs(us, uz, zsu_s, 32, II, s_shifts32, s_mask32, st1)
            zs_store = emit_zs(ds, dz, zsd_s, 86, H, s_shifts86, s_mask86, st2)
        p0_vec_done = cV.n

        p1 = ExitStack()
        with p1:
            _psum_stack.append(p1); p1.callback(_psum_stack.pop)
            def sb1(n, s, d):
                return p1.enter_context(nc.sbuf_tensor(n, list(s), d))
            xbf = sb1("p1_xbf", (128, 32 * TH), BF16)
            swin = [sb1(f"p1_swin{i}", (128, 32 * 128), BF16) for i in range(2)]
            pbf = [sb1(f"p1_pbf{i}", (128, TH), BF16) for i in range(2)]
            pp = [psum("p1_ps0", (128, TH)), psum("p1_ps1", (128, TH))]

            mm_done = {}
            evac_done = {}
            store_done = {}

            for th in range(2):
                t0 = th * TH

                @on(block.gpsimd)
                def _(g, th=th, t0=t0):
                    if th == 0:
                        g.wait_ge(qS.sem, zs_store)
                        g.wait_ge(cV.sem, p0_vec_done)
                        g.wait_ge(cA.sem, p0_exp)
                    else:
                        g.wait_ge(cT.sem, mm_done[(th - 1, 63)])
                    for kb in range(32):
                        qG.inc(g.dma_start(
                            out=xbf[:, kb * TH:(kb + 1) * TH],
                            in_=xT[kb * 128:(kb + 1) * 128, t0:t0 + TH]), 16)

                for fb in range(64):
                    buf = fb % 2

                    @on(block.gpsimd)
                    def _(g, th=th, fb=fb, buf=buf):
                        if fb >= 2:
                            g.wait_ge(cT.sem, mm_done[(th, fb - 2)])
                        qG.inc(g.dma_start(
                            out=swin[buf][:],
                            in_=winT[:, fb * 128:(fb + 1) * 128]
                            .rearrange("(kb p) c -> p kb c", p=128)), 16)
                    win_dma = qG.n

                    @on(block.tensor)
                    def _(t, th=th, fb=fb, buf=buf, win_dma=win_dma):
                        t.wait_ge(qG.sem, win_dma)
                        if fb >= 2:
                            t.wait_ge(cA.sem, evac_done[(th, fb - 2)])
                        last = None
                        for kb in range(32):
                            for nn in range(4):
                                last = t.matmul(
                                    pp[buf][:, nn * 512:(nn + 1) * 512],
                                    swin[buf][:, kb * 128:(kb + 1) * 128],
                                    xbf[:, kb * TH + nn * 512:kb * TH + (nn + 1) * 512],
                                    start=(kb == 0), stop=(kb == 31))
                        cT.inc(last)
                        mm_done[(th, fb)] = cT.n

                    @on(block.scalar)
                    def _(a, th=th, fb=fb, buf=buf):
                        a.wait_ge(cT.sem, mm_done[(th, fb)])
                        if fb >= 2:
                            a.wait_ge(qS.sem, store_done[(th, fb - 2)])
                        cA.inc(a.copy(pbf[buf][:], pp[buf][:]))
                        evac_done[(th, fb)] = cA.n

                    @on(block.sync)
                    def _(q, th=th, fb=fb, buf=buf, t0=t0):
                        q.wait_ge(cA.sem, evac_done[(th, fb)])
                        qS.inc(q.dma_start(
                            out=proj_s[fb * 128:(fb + 1) * 128, t0:t0 + TH],
                            in_=pbf[buf][:]), 16)
                        store_done[(th, fb)] = qS.n
            p1_done = qS.n

        p1b = ExitStack()
        with p1b:
            def sb2(n, s, d):
                return p1b.enter_context(nc.sbuf_tensor(n, list(s), d))
            qtile = [sb2(f"dq_q{i}", (128, II // 8), I32) for i in range(2)]
            sbc = [sb2(f"dq_s{i}", (128, II), BF16) for i in range(2)]
            wint = sb2("dq_wi", (128, II), I32)
            wbf = [sb2(f"dq_w{i}", (128, II), BF16) for i in range(2)]

            steps = ([("g", kb) for kb in range(32)] +
                     [("u", kb) for kb in range(32)] +
                     [("d", kb) for kb in range(NKB_D)])
            dq_v = {}
            dq_st = {}
            for i, (w, kb) in enumerate(steps):
                buf = i % 2
                src = {"g": gq, "u": uq, "d": dq}[w]
                srow = {"g": gs, "u": us, "d": ds}[w]
                dstt = {"g": wg_s, "u": wu_s, "d": wd_s}[w]
                ncols = II if w != "d" else H
                nq = ncols // 8

                @on(block.sync)
                def _(q, i=i, buf=buf, src=src, kb=kb, nq=nq):
                    if i == 0:
                        q.wait_ge(cT.sem, mm_done[(1, 63)])
                    if i >= 2:
                        q.wait_ge(cV.sem, dq_v[i - 2])
                    qS.inc(q.dma_start(out=qtile[buf][:, 0:nq],
                                       in_=src[kb * 128:(kb + 1) * 128, :]), 16)
                dq_q = qS.n

                @on(block.gpsimd)
                def _(g, i=i, buf=buf, srow=srow, kb=kb, ncols=ncols):
                    if i == 0:
                        g.wait_ge(cT.sem, mm_done[(1, 63)])
                    if i >= 2:
                        g.wait_ge(cV.sem, dq_v[i - 2])
                    qG.inc(g.dma_start(
                        out=sbc[buf][:, 0:ncols],
                        in_=srow[kb:kb + 1, :].partition_broadcast(128).squeeze(1)), 16)
                dq_s = qG.n

                @on(block.vector)
                def _(v, i=i, buf=buf, ncols=ncols, nq=nq, dq_q=dq_q, dq_s=dq_s):
                    if i == 0:
                        v.wait_ge(qS.sem, p0_loads)
                    v.wait_ge(qS.sem, dq_q)
                    v.wait_ge(qG.sem, dq_s)
                    if i >= 2:
                        v.wait_ge(qS.sem, dq_st[i - 2])
                    for j in range(8):
                        v.tensor_scalar(wint[:, j:ncols:8], qtile[buf][:, 0:nq],
                                        s_shifts[:, j:j + 1], s_mask[:],
                                        OP.logical_shift_right, OP.bitwise_and)
                    cV.inc(v.tensor_tensor(wbf[buf][:, 0:ncols], wint[:, 0:ncols],
                                           sbc[buf][:, 0:ncols], OP.mult))
                    dq_v[i] = cV.n

                @on(block.sync)
                def _(q, i=i, buf=buf, dstt=dstt, kb=kb, ncols=ncols):
                    q.wait_ge(cV.sem, dq_v[i])
                    qS.inc(q.dma_start(out=dstt[kb * 128:(kb + 1) * 128, :],
                                       in_=wbf[buf][:, 0:ncols]), 16)
                    dq_st[i] = qS.n
            p1b_vec_done = cV.n
            p1b_done = qS.n

        p2 = ExitStack()
        with p2:
            _psum_stack.append(p2); p2.callback(_psum_stack.pop)
            def sb3(n, s, d):
                return p2.enter_context(nc.sbuf_tensor(n, list(s), d))
            u_slab = sb3("p2_u", (128, NDB * THF), BF16)
            xp = sb3("p2_xp", (128, THF + 3), BF16)
            xc = sb3("p2_xc", (128, THF), F32)
            gt2 = sb3("p2_gt", (128, THF), BF16)
            sgt2 = sb3("p2_sgt", (128, THF), BF16)
            s_wx = [sb3(f"p2_wx{i}", (128, 288), BF16) for i in range(2)]
            s_wdt = [sb3(f"p2_wdt{i}", (128, 256), BF16) for i in range(2)]
            ts_sb = [sb3(f"p2_ts{i}", (128, TCH), F32) for i in range(2)]
            bc_sb = [sb3(f"p2_bc{i}", (16, TCH), F32) for i in range(2)]
            sq_bf = sb3("p2_sq", (128, TCH), BF16)
            row_a = sb3("p2_rowa", (1, TCH), F32)
            row_b = sb3("p2_rowb", (1, TCH), F32)
            rstd_b = sb3("p2_rstdb", (128, TCH), F32)
            tsn = [sb3(f"p2_tsn{i}", (128, TCH), BF16) for i in range(2)]
            bcn = [sb3(f"p2_bcn{i}", (16, TCH), F32) for i in range(2)]
            Bb = sb3("p2_Bb", (128, NST * TCH), BF16)
            Cb = sb3("p2_Cb", (128, NST * TCH), BF16)
            dt_f = sb3("p2_dt", (128, TCH), F32)
            sp_t = sb3("p2_spt", (128, TCH), F32)
            dtu = sb3("p2_dtu", (128, TCH), F32)
            u_db = sb3("p2_udb", (128, TCH), BF16)
            sg_db = sb3("p2_sgdb", (128, TCH), BF16)
            dA = sb3("p2_dA", (128, 2 * TCH), F32)
            Xb = sb3("p2_X", (128, 2 * TCH), F32)
            st_b = sb3("p2_st", (128, 2 * TCH), F32)
            fix = sb3("p2_fix", (128, 2), F32)
            yacc = sb3("p2_ys", (128, TCH), F32)
            y2 = sb3("p2_y2", (128, TCH), F32)
            y3 = sb3("p2_y3", (128, TCH), BF16)
            ps_ts0 = psum("p2_pts0", (128, THF))
            ps_ts1 = psum("p2_pts1", (128, THF))
            ps_B = psum("p2_pB", (16, THF))
            ps_C = psum("p2_pC", (16, THF))
            ps_row = psum("p2_prow", (1, TCH))
            ps_dt = psum("p2_pdt", (128, TCH))

            conv_v = {}
            silu_a = {}
            wx_mm = {}
            ssm_evac = {}
            scan_fin = {}
            scan_sp = {}
            udma = {}
            prev_fin = {}

            NT2 = TCH // THF

            for ch in range(NCH):
                t0 = ch * TCH
                for th2 in range(NT2):
                    tt0 = t0 + th2 * THF
                    for db in range(NDB):
                        def prev_key(d, ch=ch, th2=th2, db=db):
                            for k in [(ch, th2, db - 1), (ch, th2 - 1, NDB - 1),
                                      (ch - 1, NT2 - 1, NDB - 1)]:
                                if k in d:
                                    return k
                            return None

                        @on(block.sync)
                        def _(q, ch=ch, th2=th2, db=db, tt0=tt0):
                            if ch == 0 and th2 == 0 and db == 0:
                                q.wait_ge(qS.sem, p1_done)
                            pk = prev_key(conv_v)
                            if pk:
                                q.wait_ge(cV.sem, conv_v[pk])
                            lo = tt0 - 3
                            if lo < 0:
                                qS.inc(q.dma_start(
                                    out=xp[:, 3:THF + 3],
                                    in_=proj_s[db * 128:(db + 1) * 128, 0:THF]), 16)
                            else:
                                qS.inc(q.dma_start(
                                    out=xp[:, 0:THF + 3],
                                    in_=proj_s[db * 128:(db + 1) * 128, lo:lo + THF + 3]), 16)
                            pk = prev_key(silu_a)
                            if pk:
                                q.wait_ge(cA.sem, silu_a[pk])
                            qS.inc(q.dma_start(
                                out=gt2[:],
                                in_=proj_s[DH + db * 128:DH + (db + 1) * 128,
                                           tt0:tt0 + THF]), 16)
                        conv_dma = qS.n

                        @on(block.vector)
                        def _(v, ch=ch, th2=th2, db=db, tt0=tt0, conv_dma=conv_dma):
                            v.wait_ge(qS.sem, conv_dma)
                            if ch == 0 and th2 == 0 and db == 0:
                                v.wait_ge(qS.sem, p0_loads)
                            pk = prev_key(silu_a)
                            if pk:
                                v.wait_ge(cA.sem, silu_a[pk])
                            if tt0 - 3 < 0:
                                v.memset(xp[:, 0:3], 0.0)
                            v.tensor_scalar(xc[:], xp[:, 0:THF],
                                            s_convw[:, db * 4:db * 4 + 1],
                                            s_convb[:, db:db + 1], OP.mult, OP.add)
                            for j in range(1, 3):
                                v.scalar_tensor_tensor(
                                    xc[:], xp[:, j:j + THF],
                                    s_convw[:, db * 4 + j:db * 4 + j + 1],
                                    xc[:], OP.mult, OP.add)
                            cV.inc(v.scalar_tensor_tensor(
                                xc[:], xp[:, 3:3 + THF],
                                s_convw[:, db * 4 + 3:db * 4 + 4],
                                xc[:], OP.mult, OP.add))
                            conv_v[(ch, th2, db)] = cV.n

                        @on(block.scalar)
                        def _(a, ch=ch, th2=th2, db=db):
                            a.wait_ge(cV.sem, conv_v[(ch, th2, db)])
                            for k in [(ch, th2 - 1, db), (ch - 1, NT2 - 1, db)]:
                                if k in wx_mm:
                                    a.wait_ge(cT.sem, wx_mm[k])
                                    break
                            a.activation(u_slab[:, db * THF:(db + 1) * THF],
                                         xc[:], AF.Silu)
                            cA.inc(a.activation(sgt2[:], gt2[:], AF.Silu))
                            silu_a[(ch, th2, db)] = cA.n

                        @on(block.sync)
                        def _(q, ch=ch, th2=th2, db=db, tt0=tt0):
                            q.wait_ge(cA.sem, silu_a[(ch, th2, db)])
                            qS.inc(q.dma_start(
                                out=u_s[db * 128:(db + 1) * 128, tt0:tt0 + THF],
                                in_=u_slab[:, db * THF:(db + 1) * THF]), 16)
                            qS.inc(q.dma_start(
                                out=sg_s[db * 128:(db + 1) * 128, tt0:tt0 + THF],
                                in_=sgt2[:]), 16)
                            udma[(ch, th2, db)] = qS.n

                        wxbuf = db % 2

                        @on(block.gpsimd)
                        def _(g, ch=ch, th2=th2, db=db, wxbuf=wxbuf):
                            if ch == 0 and th2 == 0 and db == 0:
                                g.wait_ge(cV.sem, p1b_vec_done)
                            for k in [(ch, th2, db - 2), (ch, th2 - 1, NDB - 2 + db),
                                      (ch - 1, NT2 - 1, NDB - 2 + db)]:
                                if k in wx_mm:
                                    g.wait_ge(cT.sem, wx_mm[k])
                                    break
                            qG.inc(g.dma_start(
                                out=s_wx[wxbuf][:],
                                in_=wxT[db * 128:(db + 1) * 128, :]), 16)
                        wx_dma = qG.n

                        @on(block.tensor)
                        def _(t, ch=ch, th2=th2, db=db, wxbuf=wxbuf, wx_dma=wx_dma):
                            t.wait_ge(cA.sem, silu_a[(ch, th2, db)])
                            t.wait_ge(qG.sem, wx_dma)
                            if db == 0:
                                for k in [(ch, th2 - 1), (ch - 1, NT2 - 1)]:
                                    if k in ssm_evac:
                                        t.wait_ge(cV.sem, ssm_evac[k])
                                        break
                            st = (db == 0)
                            sp = (db == NDB - 1)
                            mv = u_slab[:, db * THF:(db + 1) * THF]
                            t.matmul(ps_ts0[:], s_wx[wxbuf][:, 0:128], mv, start=st, stop=sp)
                            t.matmul(ps_ts1[:], s_wx[wxbuf][:, 128:256], mv, start=st, stop=sp)
                            t.matmul(ps_B[:], s_wx[wxbuf][:, 256:272], mv, start=st, stop=sp)
                            last = t.matmul(ps_C[:], s_wx[wxbuf][:, 272:288], mv,
                                            start=st, stop=sp)
                            cT.inc(last)
                            wx_mm[(ch, th2, db)] = cT.n

                    @on(block.vector)
                    def _(v, ch=ch, th2=th2):
                        v.wait_ge(cT.sem, wx_mm[(ch, th2, NDB - 1)])
                        c0 = th2 * THF
                        v.tensor_copy(ts_sb[0][:, c0:c0 + THF], ps_ts0[:])
                        v.tensor_copy(ts_sb[1][:, c0:c0 + THF], ps_ts1[:])
                        v.tensor_copy(bc_sb[0][:, c0:c0 + THF], ps_B[:])
                        cV.inc(v.tensor_copy(bc_sb[1][:, c0:c0 + THF], ps_C[:]))
                        ssm_evac[(ch, th2)] = cV.n

                @on(block.sync)
                def _(q, ch=ch):
                    q.wait_ge(cV.sem, ssm_evac[(ch, NT2 - 1)])
                    qS.inc(q.dma_start(out=ar_in[0:128, :], in_=ts_sb[0][:]), 16)
                    qS.inc(q.dma_start(out=ar_in[128:256, :], in_=ts_sb[1][:]), 16)
                    qS.inc(q.dma_start(out=ar_in[256:272, :], in_=bc_sb[0][:]), 16)
                    qS.inc(q.dma_start(out=ar_in[272:288, :], in_=bc_sb[1][:]), 16)
                ar_staged = qS.n

                @on(block.gpsimd)
                def _(g, ch=ch, ar_staged=ar_staged):
                    g.wait_ge(qS.sem, ar_staged)
                    cC.inc(g.collective_compute(
                        "AllReduce", OP.add, replica_groups=PAIRS,
                        ins=[ar_in.opt()], outs=[ar_out.opt()]))
                ar_done = cC.n

                @on(block.sync)
                def _(q, ch=ch, ar_done=ar_done):
                    q.wait_ge(cC.sem, ar_done)
                    qS.inc(q.dma_start(out=ts_sb[0][:], in_=ar_out[0:128, :]), 16)
                    qS.inc(q.dma_start(out=ts_sb[1][:], in_=ar_out[128:256, :]), 16)
                    qS.inc(q.dma_start(out=bc_sb[0][:], in_=ar_out[256:272, :]), 16)
                    qS.inc(q.dma_start(out=bc_sb[1][:], in_=ar_out[272:288, :]), 16)
                    if DEBUG:
                        qS.inc(q.dma_start(out=ssm_dbg[ch * 288:(ch + 1) * 288, :],
                                           in_=ar_out), 16)
                ssm_full = qS.n

                @on(block.scalar)
                def _(a, ch=ch, ssm_full=ssm_full):
                    a.wait_ge(qS.sem, ssm_full)
                    cA.inc(a.activation(sq_bf[:], ts_sb[0][:], AF.Square))
                sq0 = cA.n

                @on(block.tensor)
                def _(t, ch=ch, sq0=sq0):
                    t.wait_ge(cA.sem, sq0)
                    last = None
                    for nn in range(2):
                        last = t.matmul(ps_row[:, nn * 512:(nn + 1) * 512], s_ones[:],
                                        sq_bf[:, nn * 512:(nn + 1) * 512],
                                        start=True, stop=False)
                    cT.inc(last)
                mm0 = cT.n

                @on(block.scalar)
                def _(a, ch=ch, mm0=mm0):
                    a.wait_ge(cT.sem, mm0)
                    cA.inc(a.activation(sq_bf[:], ts_sb[1][:], AF.Square))
                sq1 = cA.n

                @on(block.tensor)
                def _(t, ch=ch, sq1=sq1):
                    t.wait_ge(cA.sem, sq1)
                    last = None
                    for nn in range(2):
                        last = t.matmul(ps_row[:, nn * 512:(nn + 1) * 512], s_ones[:],
                                        sq_bf[:, nn * 512:(nn + 1) * 512],
                                        start=False, stop=True)
                    cT.inc(last)
                mm1 = cT.n

                @on(block.scalar)
                def _(a, ch=ch, mm1=mm1):
                    a.wait_ge(cT.sem, mm1)
                    a.activation(row_a[:], ps_row[:], AF.Ln, scale=s_cst[0:1, 2:3], bias=s_cst[0:1, 0:1])
                    cA.inc(a.activation(row_a[:], row_a[:], AF.Exp, scale=s_cst[0:1, 4:5]))
                rstd_ts = cA.n

                @on(block.sync)
                def _(q, rstd_ts=rstd_ts):
                    q.wait_ge(cA.sem, rstd_ts)
                    qS.inc(q.dma_start(out=row_bounce[0:1, :], in_=row_a[:]), 16)
                rb1 = qS.n

                @on(block.gpsimd)
                def _(g, rb1=rb1, ch=ch):
                    g.wait_ge(qS.sem, rb1)
                    qG.inc(g.dma_start(
                        out=rstd_b[:],
                        in_=row_bounce[0:1, :].partition_broadcast(128).squeeze(1)), 16)
                rstd_bc = qG.n

                @on(block.vector)
                def _(v, ch=ch, rstd_bc=rstd_bc):
                    v.wait_ge(qG.sem, rstd_bc)
                    if (ch - 1, NDB - 1) in scan_sp:
                        v.wait_ge(cA.sem, scan_sp[(ch - 1, NDB - 1)])
                    v.tensor_tensor(tsn[0][:], ts_sb[0][:], rstd_b[:], OP.mult)
                    cV.inc(v.tensor_tensor(tsn[1][:], ts_sb[1][:], rstd_b[:], OP.mult))
                tsn_done = cV.n

                bcn_rows = {}
                for bi in range(2):
                    @on(block.scalar)
                    def _(a, ch=ch, bi=bi, tsn_done=tsn_done):
                        if bi == 0:
                            a.wait_ge(cV.sem, tsn_done)
                        cA.inc(a.activation(sq_bf[0:16, :], bc_sb[bi][:], AF.Square))
                    sqb = cA.n

                    @on(block.tensor)
                    def _(t, ch=ch, bi=bi, sqb=sqb):
                        t.wait_ge(cA.sem, sqb)
                        last = None
                        for nn in range(2):
                            last = t.matmul(ps_row[:, nn * 512:(nn + 1) * 512],
                                            s_ones[0:16, :],
                                            sq_bf[0:16, nn * 512:(nn + 1) * 512],
                                            start=True, stop=True)
                        cT.inc(last)
                    mb = cT.n

                    @on(block.scalar)
                    def _(a, ch=ch, bi=bi, mb=mb):
                        a.wait_ge(cT.sem, mb)
                        a.activation(row_b[:], ps_row[:], AF.Ln, scale=s_cst[0:1, 3:4], bias=s_cst[0:1, 0:1])
                        cA.inc(a.activation(row_b[:], row_b[:], AF.Exp, scale=s_cst[0:1, 4:5]))
                    rstdb = cA.n

                    @on(block.sync)
                    def _(q, rstdb=rstdb):
                        q.wait_ge(cA.sem, rstdb)
                        qS.inc(q.dma_start(out=row_bounce[1:2, :], in_=row_b[:]), 16)
                    rbx = qS.n

                    @on(block.gpsimd)
                    def _(g, rbx=rbx, tsn_done=tsn_done):
                        g.wait_ge(qS.sem, rbx)
                        g.wait_ge(cV.sem, tsn_done)
                        qG.inc(g.dma_start(
                            out=rstd_b[:],
                            in_=row_bounce[1:2, :].partition_broadcast(128).squeeze(1)), 16)
                    rbv = qG.n

                    @on(block.vector)
                    def _(v, ch=ch, bi=bi, rbv=rbv):
                        v.wait_ge(qG.sem, rbv)
                        cV.inc(v.tensor_tensor(bcn[bi][:], bc_sb[bi][:],
                                               rstd_b[0:16, :], OP.mult))
                    bcnv = cV.n

                    @on(block.sync)
                    def _(q, bi=bi, bcnv=bcnv):
                        q.wait_ge(cV.sem, bcnv)
                        qS.inc(q.dma_start(
                            out=row_bounce[16 * (1 + bi):16 * (1 + bi) + 16, :],
                            in_=bcn[bi][:]), 16)
                        bcn_rows[bi] = qS.n

                @on(block.gpsimd)
                def _(g, ch=ch):
                    g.wait_ge(qS.sem, bcn_rows[0])
                    g.wait_ge(qS.sem, bcn_rows[1])
                    if (ch - 1, NDB - 1) in prev_fin:
                        g.wait_ge(cV.sem, prev_fin[(ch - 1, NDB - 1)])
                    for n in range(NST):
                        qG.inc(g.dma_start(
                            out=Bb[:, n * TCH:(n + 1) * TCH],
                            in_=row_bounce[16 + n:17 + n, :]
                            .partition_broadcast(128).squeeze(1)), 16)
                        qG.inc(g.dma_start(
                            out=Cb[:, n * TCH:(n + 1) * TCH],
                            in_=row_bounce[32 + n:33 + n, :]
                            .partition_broadcast(128).squeeze(1)), 16)
                bc_slabs = qG.n

                for db in range(NDB):
                    @on(block.sync)
                    def _(q, ch=ch, db=db, t0=t0):
                        for k in [(ch, db - 1), (ch - 1, NDB - 1)]:
                            if k in prev_fin:
                                q.wait_ge(cV.sem, prev_fin[k])
                                break
                        q.wait_ge(qS.sem, udma[(ch, NT2 - 1, db)])
                        qS.inc(q.dma_start(out=u_db[:],
                                           in_=u_s[db * 128:(db + 1) * 128, t0:t0 + TCH]), 16)
                        qS.inc(q.dma_start(out=sg_db[:],
                                           in_=sg_s[db * 128:(db + 1) * 128, t0:t0 + TCH]), 16)
                    db_dma = qS.n

                    wdbuf = db % 2

                    @on(block.gpsimd)
                    def _(g, ch=ch, db=db, wdbuf=wdbuf):
                        for k in [(ch, db - 2), (ch - 1, NDB - 2 + db)]:
                            if k in scan_sp:
                                g.wait_ge(cA.sem, scan_sp[k])
                                break
                        qG.inc(g.dma_start(out=s_wdt[wdbuf][:, 0:128],
                                           in_=wdtT[0:128, db * 128:(db + 1) * 128]), 16)
                        qG.inc(g.dma_start(out=s_wdt[wdbuf][:, 128:256],
                                           in_=wdtT[128:256, db * 128:(db + 1) * 128]), 16)
                    wdt_dma = qG.n

                    @on(block.tensor)
                    def _(t, ch=ch, db=db, wdbuf=wdbuf, wdt_dma=wdt_dma):
                        t.wait_ge(cV.sem, tsn_done)
                        t.wait_ge(qG.sem, wdt_dma)
                        for k in [(ch, db - 1), (ch - 1, NDB - 1)]:
                            if k in scan_sp:
                                t.wait_ge(cA.sem, scan_sp[k])
                                break
                        last = None
                        for kt in range(2):
                            for nn in range(2):
                                last = t.matmul(
                                    ps_dt[:, nn * 512:(nn + 1) * 512],
                                    s_wdt[wdbuf][:, kt * 128:(kt + 1) * 128],
                                    tsn[kt][:, nn * 512:(nn + 1) * 512],
                                    start=(kt == 0), stop=(kt == 1))
                        cT.inc(last)
                    dt_mm = cT.n

                    @on(block.scalar)
                    def _(a, ch=ch, db=db, dt_mm=dt_mm):
                        a.wait_ge(cT.sem, dt_mm)
                        if ch == 0 and db == 0:
                            a.wait_ge(cV.sem, p0_A)
                        for k in [(ch, db - 1, "dtu"), (ch - 1, NDB - 1, "dtu")]:
                            if k in scan_fin:
                                a.wait_ge(cV.sem, scan_fin[k])
                                break
                        a.activation(dt_f[:], ps_dt[:], AF.Relu, bias=s_bdt[:, db:db + 1])
                        a.activation(sp_t[:], ps_dt[:], AF.Abs, bias=s_bdt[:, db:db + 1])
                        a.activation(sp_t[:], sp_t[:], AF.Exp, scale=s_cst[:, 5:6])
                        cA.inc(a.activation(sp_t[:], sp_t[:], AF.Ln, bias=s_cst[:, 6:7]))
                        scan_sp[(ch, db)] = cA.n

                    @on(block.vector)
                    def _(v, ch=ch, db=db, db_dma=db_dma):
                        v.wait_ge(cA.sem, scan_sp[(ch, db)])
                        v.tensor_tensor(dt_f[:], dt_f[:], sp_t[:], OP.add)
                        v.wait_ge(qS.sem, db_dma)
                        cV.inc(v.tensor_tensor(dtu[:], dt_f[:], u_db[:], OP.mult))
                        scan_fin[(ch, db, "dtu")] = cV.n

                    if DEBUG and ch == 0:
                        @on(block.sync)
                        def _(q, db=db):
                            q.wait_ge(cV.sem, scan_fin[(0, db, "dtu")])
                            qS.inc(q.dma_start(out=dt_dbg[db * 128:(db + 1) * 128, :],
                                               in_=dt_f[:]), 16)

                    for np_i in range(NST // 2):
                        @on(block.scalar)
                        def _(a, ch=ch, db=db, np_i=np_i):
                            a.wait_ge(cV.sem, scan_fin[(ch, db, "dtu")])
                            for k in [(ch, db, np_i - 1), (ch, db - 1, NST // 2 - 1),
                                      (ch - 1, NDB - 1, NST // 2 - 1)]:
                                if k in scan_fin:
                                    a.wait_ge(cV.sem, scan_fin[k])
                                    break
                            n0 = db * NST + 2 * np_i
                            a.activation(dA[:, 0:TCH], dt_f[:], AF.Exp,
                                         scale=s_A[:, n0:n0 + 1])
                            cA.inc(a.activation(dA[:, TCH:2 * TCH], dt_f[:], AF.Exp,
                                                scale=s_A[:, n0 + 1:n0 + 2]))
                        dA_a = cA.n

                        @on(block.vector)
                        def _(v, ch=ch, db=db, np_i=np_i, dA_a=dA_a):
                            n0 = 2 * np_i
                            v.wait_ge(cA.sem, dA_a)
                            if np_i == 0 and db == 0:
                                v.wait_ge(qG.sem, bc_slabs)
                            v.tensor_tensor(
                                Xb[:].rearrange("p (r c) -> p r c", r=2),
                                dtu[:].unsqueeze(1).broadcast_to((128, 2, TCH)),
                                Bb[:, n0 * TCH:(n0 + 2) * TCH]
                                .rearrange("p (r c) -> p r c", r=2), OP.mult)
                            cs = s_carry[:, db * NST + n0:db * NST + n0 + 2]
                            dA3 = dA[:].rearrange("p (r c) -> p r c", r=2)
                            Xb3 = Xb[:].rearrange("p (r c) -> p r c", r=2)
                            v.tensor_tensor(fix[:].unsqueeze(2), cs.unsqueeze(2),
                                            dA3[:, :, 0:1], OP.mult)
                            v.tensor_tensor(Xb3[:, :, 0:1], Xb3[:, :, 0:1],
                                            fix[:].unsqueeze(2), OP.add)
                            v.memset(dA3[:, :, 0:1], 0.0)
                            v.tensor_tensor_scan(st_b[:], dA[:], Xb[:], 0.0,
                                                 OP.mult, OP.add)
                            st3 = st_b[:].rearrange("p (r c) -> p r c", r=2)
                            v.tensor_copy(cs.unsqueeze(2), st3[:, :, TCH - 1:TCH])
                            v.tensor_tensor(
                                Xb3, st3,
                                Cb[:, n0 * TCH:(n0 + 2) * TCH]
                                .rearrange("p (r c) -> p r c", r=2), OP.mult)
                            v.tensor_tensor(Xb[:, 0:TCH], Xb[:, 0:TCH],
                                            Xb[:, TCH:2 * TCH], OP.add)
                            if np_i == 0:
                                cV.inc(v.tensor_copy(yacc[:], Xb[:, 0:TCH]))
                            else:
                                cV.inc(v.tensor_tensor(yacc[:], yacc[:],
                                                       Xb[:, 0:TCH], OP.add))
                            scan_fin[(ch, db, np_i)] = cV.n

                    @on(block.vector)
                    def _(v, ch=ch, db=db):
                        v.scalar_tensor_tensor(y2[:], u_db[:], s_D[:, db:db + 1],
                                               yacc[:], OP.mult, OP.add)
                        cV.inc(v.tensor_tensor(y3[:], y2[:], sg_db[:], OP.mult))
                        prev_fin[(ch, db)] = cV.n
                        scan_fin[(ch, db, NST // 2 - 1)] = cV.n

                    @on(block.sync)
                    def _(q, ch=ch, db=db, t0=t0):
                        q.wait_ge(cV.sem, prev_fin[(ch, db)])
                        qS.inc(q.dma_start(out=y_s[db * 128:(db + 1) * 128, t0:t0 + TCH],
                                           in_=y3[:]), 16)
            p2_vec_done = cV.n
            p2_y_done = qS.n

        p4 = ExitStack()
        with p4:
            _psum_stack.append(p4); p4.callback(_psum_stack.pop)
            def sb4(n, s, d):
                return p4.enter_context(nc.sbuf_tensor(n, list(s), d))
            p4a = ExitStack()
            p4a.callback(lambda: None)
            _psum_stack.append(p4a)
            yslab = p4a.enter_context(nc.sbuf_tensor("p4_y", [128, NDB * TH], BF16))
            swo = [p4a.enter_context(nc.sbuf_tensor(f"p4_w{i}", [128, 32 * 128], BF16)) for i in range(2)]
            pbf4 = [p4a.enter_context(nc.sbuf_tensor(f"p4_e{i}", [128, TH], BF16)) for i in range(2)]
            pp4 = [psum("p4_ps0", (128, TH)), psum("p4_ps1", (128, TH))]

            wo_mm = {}
            wo_ev = {}
            wo_st = {}
            for cp in range(2):
                t0 = cp * TH

                @on(block.sync)
                def _(q, cp=cp, t0=t0):
                    if cp == 0:
                        q.wait_ge(qS.sem, p2_y_done)
                    else:
                        q.wait_ge(cT.sem, wo_mm[(cp - 1, 31)])
                    for db in range(NDB):
                        qS.inc(q.dma_start(out=yslab[:, db * TH:(db + 1) * TH],
                                           in_=y_s[db * 128:(db + 1) * 128, t0:t0 + TH]), 16)
                yslab_dma = qS.n

                for fb in range(32):
                    buf = fb % 2

                    @on(block.gpsimd)
                    def _(g, cp=cp, fb=fb, buf=buf):
                        if cp == 0 and fb < 2:
                            g.wait_ge(cV.sem, p2_vec_done)
                        if (cp, fb - 2) in wo_mm:
                            g.wait_ge(cT.sem, wo_mm[(cp, fb - 2)])
                        elif cp > 0 and fb < 2:
                            g.wait_ge(cT.sem, wo_mm[(cp - 1, 30 + fb)])
                        qG.inc(g.dma_start(
                            out=swo[buf][:],
                            in_=woutT[:, fb * 128:(fb + 1) * 128]
                            .rearrange("(kb p) c -> p kb c", p=128)), 16)
                    wo_dma = qG.n

                    @on(block.tensor)
                    def _(t, cp=cp, fb=fb, buf=buf, wo_dma=wo_dma, yslab_dma=yslab_dma):
                        if fb == 0:
                            t.wait_ge(qS.sem, yslab_dma)
                        t.wait_ge(qG.sem, wo_dma)
                        if (cp, fb - 2) in wo_ev:
                            t.wait_ge(cA.sem, wo_ev[(cp, fb - 2)])
                        elif cp > 0 and fb < 2:
                            t.wait_ge(cA.sem, wo_ev[(cp - 1, 30 + fb)])
                        last = None
                        for kb in range(NDB):
                            for nn in range(4):
                                last = t.matmul(
                                    pp4[buf][:, nn * 512:(nn + 1) * 512],
                                    swo[buf][:, kb * 128:(kb + 1) * 128],
                                    yslab[:, kb * TH + nn * 512:kb * TH + (nn + 1) * 512],
                                    start=(kb == 0), stop=(kb == NDB - 1))
                        cT.inc(last)
                        wo_mm[(cp, fb)] = cT.n

                    @on(block.scalar)
                    def _(a, cp=cp, fb=fb, buf=buf):
                        a.wait_ge(cT.sem, wo_mm[(cp, fb)])
                        if (cp, fb - 2) in wo_st:
                            a.wait_ge(qS.sem, wo_st[(cp, fb - 2)])
                        elif cp > 0 and fb < 2:
                            a.wait_ge(qS.sem, wo_st[(cp - 1, 30 + fb)])
                        cA.inc(a.copy(pbf4[buf][:], pp4[buf][:]))
                        wo_ev[(cp, fb)] = cA.n

                    @on(block.sync)
                    def _(q, cp=cp, fb=fb, buf=buf):
                        q.wait_ge(cA.sem, wo_ev[(cp, fb)])
                        qS.inc(q.dma_start(
                            out=rs_in[cp * 2 * H + fb * 128:cp * 2 * H + (fb + 1) * 128, :],
                            in_=pbf4[buf][:, 0:TCH]), 16)
                        qS.inc(q.dma_start(
                            out=rs_in[cp * 2 * H + H + fb * 128:cp * 2 * H + H + (fb + 1) * 128, :],
                            in_=pbf4[buf][:, TCH:2 * TCH]), 16)
                        wo_st[(cp, fb)] = qS.n
            rs_staged = qS.n

            @on(block.gpsimd)
            def _(g):
                g.wait_ge(qS.sem, rs_staged)
                cC.inc(g.collective_compute(
                    "ReduceScatter", OP.add, replica_groups=PAIRS,
                    ins=[rs_in.opt()], outs=[rs_out.opt()]))
            rs_done = cC.n
            _psum_stack.pop()
            p4a.close()

            attn_bf = sb4("p4_attn", (128, TH), BF16)
            xres = sb4("p4_xres", (128, TH), F32)
            r1t = sb4("p4_r1", (128, TH), F32)
            r1bf = sb4("p4_r1bf", (128, TH), BF16)
            sq4 = sb4("p4_sq", (128, TH), BF16)
            rowm = sb4("p4_rowm", (1, TH), F32)
            rowv = sb4("p4_rowv", (1, TH), F32)
            rowt = sb4("p4_rowt", (1, TH), F32)
            sc_b = sb4("p4_scb", (128, TH), F32)
            mr_b = sb4("p4_mrb", (128, TH), F32)
            ps_m = psum("p4_pm", (1, TH))
            ps_v = psum("p4_pv", (1, TH))

            ln1 = {}
            for fb in range(32):
                @on(block.sync)
                def _(q, fb=fb, rs_done=rs_done):
                    if fb == 0:
                        q.wait_ge(cC.sem, rs_done)
                    if (fb - 1, "r1") in ln1:
                        q.wait_ge(cV.sem, ln1[(fb - 1, "r1")])
                    qS.inc(q.dma_start(out=attn_bf[:, 0:TCH],
                                       in_=rs_out[fb * 128:(fb + 1) * 128, :]), 16)
                    qS.inc(q.dma_start(out=attn_bf[:, TCH:2 * TCH],
                                       in_=rs_out[H + fb * 128:H + (fb + 1) * 128, :]), 16)
                    qS.inc(q.dma_start(out=xres[:],
                                       in_=xT_my[fb * 128:(fb + 1) * 128, :]), 16)
                    ln1[(fb, "dma")] = qS.n

                @on(block.vector)
                def _(v, fb=fb):
                    v.wait_ge(qS.sem, ln1[(fb, "dma")])
                    if (fb - 1, "cv") in ln1:
                        v.wait_ge(cA.sem, ln1[(fb - 1, "cv")])
                    cV.inc(v.tensor_tensor(r1t[:], attn_bf[:], xres[:], OP.add))
                    ln1[(fb, "r1")] = cV.n

                @on(block.sync)
                def _(q, fb=fb):
                    q.wait_ge(cV.sem, ln1[(fb, "r1")])
                    qS.inc(q.dma_start(out=r1_s[fb * 128:(fb + 1) * 128, :], in_=r1t[:]), 16)
                    ln1[(fb, "r1st")] = qS.n

                @on(block.scalar)
                def _(a, fb=fb):
                    a.wait_ge(cV.sem, ln1[(fb, "r1")])
                    if (fb - 1, "mm") in ln1:
                        a.wait_ge(cT.sem, ln1[(fb - 1, "mm")])
                    a.copy(r1bf[:], r1t[:])
                    cA.inc(a.activation(sq4[:], r1t[:], AF.Square))
                    ln1[(fb, "cv")] = cA.n

                @on(block.tensor)
                def _(t, fb=fb):
                    t.wait_ge(cA.sem, ln1[(fb, "cv")])
                    last = None
                    for nn in range(4):
                        t.matmul(ps_m[:, nn * 512:(nn + 1) * 512], s_ones[:],
                                 r1bf[:, nn * 512:(nn + 1) * 512],
                                 start=(fb == 0), stop=(fb == 31))
                        last = t.matmul(ps_v[:, nn * 512:(nn + 1) * 512], s_ones[:],
                                        sq4[:, nn * 512:(nn + 1) * 512],
                                        start=(fb == 0), stop=(fb == 31))
                    cT.inc(last)
                    ln1[(fb, "mm")] = cT.n

            @on(block.vector)
            def _(v):
                v.wait_ge(cT.sem, ln1[(31, "mm")])
                v.tensor_scalar(rowm[:], ps_m[:], 1.0 / H, None, OP.mult)
                v.tensor_scalar(rowv[:], ps_v[:], 1.0 / H, None, OP.mult)
                v.tensor_tensor(rowt[:], rowm[:], rowm[:], OP.mult)
                cV.inc(v.tensor_tensor(rowv[:], rowv[:], rowt[:], OP.subtract))
            ln1_rows0 = cV.n

            @on(block.scalar)
            def _(a, ln1_rows0=ln1_rows0):
                a.wait_ge(cV.sem, ln1_rows0)
                a.activation(rowv[:], rowv[:], AF.Ln, bias=s_cst[0:1, 1:2])
                cA.inc(a.activation(rowv[:], rowv[:], AF.Exp, scale=s_cst[0:1, 4:5]))
            ln1_rstd = cA.n

            @on(block.vector)
            def _(v, ln1_rstd=ln1_rstd):
                v.wait_ge(cA.sem, ln1_rstd)
                v.tensor_tensor(rowm[:], rowm[:], rowv[:], OP.mult)
                cV.inc(v.tensor_scalar(rowm[:], rowm[:], -1.0, None, OP.mult))
            ln1_rows = cV.n

            @on(block.sync)
            def _(q, ln1_rows=ln1_rows):
                q.wait_ge(cV.sem, ln1_rows)
                qS.inc(q.dma_start(out=row_bounce[0:1, 0:TCH], in_=rowv[0:1, 0:TCH]), 16)
                qS.inc(q.dma_start(out=row_bounce[1:2, 0:TCH], in_=rowv[0:1, TCH:2 * TCH]), 16)
                qS.inc(q.dma_start(out=row_bounce[2:3, 0:TCH], in_=rowm[0:1, 0:TCH]), 16)
                qS.inc(q.dma_start(out=row_bounce[3:4, 0:TCH], in_=rowm[0:1, TCH:2 * TCH]), 16)
            ln1_rows_st = qS.n

            @on(block.gpsimd)
            def _(g, ln1_rows_st=ln1_rows_st):
                g.wait_ge(qS.sem, ln1_rows_st)
                qG.inc(g.dma_start(out=sc_b[:, 0:TCH],
                                   in_=row_bounce[0:1, :].partition_broadcast(128).squeeze(1)), 16)
                qG.inc(g.dma_start(out=sc_b[:, TCH:2 * TCH],
                                   in_=row_bounce[1:2, :].partition_broadcast(128).squeeze(1)), 16)
                qG.inc(g.dma_start(out=mr_b[:, 0:TCH],
                                   in_=row_bounce[2:3, :].partition_broadcast(128).squeeze(1)), 16)
                qG.inc(g.dma_start(out=mr_b[:, TCH:2 * TCH],
                                   in_=row_bounce[3:4, :].partition_broadcast(128).squeeze(1)), 16)
            ln1_bc = qG.n

            for fb in range(32):
                @on(block.sync)
                def _(q, fb=fb):
                    if fb == 0:
                        q.wait_ge(cT.sem, ln1[(31, "mm")])
                    else:
                        q.wait_ge(cV.sem, ln1[(fb - 1, "xf")])
                    q.wait_ge(qS.sem, ln1[(fb, "r1st")])
                    qS.inc(q.dma_start(out=r1t[:], in_=r1_s[fb * 128:(fb + 1) * 128, :]), 16)
                    ln1[(fb, "r1ld")] = qS.n

                @on(block.vector)
                def _(v, fb=fb, ln1_bc=ln1_bc):
                    if fb == 0:
                        v.wait_ge(qG.sem, ln1_bc)
                    v.wait_ge(qS.sem, ln1[(fb, "r1ld")])
                    v.tensor_tensor(r1t[:], r1t[:], sc_b[:], OP.mult)
                    cV.inc(v.tensor_tensor(r1bf[:], r1t[:], mr_b[:], OP.add))
                    ln1[(fb, "xf")] = cV.n

                @on(block.sync)
                def _(q, fb=fb):
                    q.wait_ge(cV.sem, ln1[(fb, "xf")])
                    qS.inc(q.dma_start(out=xf_s[fb * 128:(fb + 1) * 128, :], in_=r1bf[:]), 16)
            p4_vec_done = cV.n
            xf_done = qS.n

        p5 = ExitStack()
        with p5:
            _psum_stack.append(p5); p5.callback(_psum_stack.pop)
            def sb5(n, s, d):
                return p5.enter_context(nc.sbuf_tensor(n, list(s), d))
            xf = sb5("p5_xf", (128, 32 * TH), BF16)
            wgt = [sb5(f"p5_wg{i}", (128, 32 * 128), BF16) for i in range(2)]
            wut = [sb5(f"p5_wu{i}", (128, 32 * 128), BF16) for i in range(2)]
            xfg = sb5("p5_xfg", (32, TH), BF16)
            sg5 = sb5("p5_sg", (128, TH), BF16)
            hbf = sb5("p5_h", (128, TH), BF16)
            s_oneh32 = sb5("p5_oh32", (128, 32 * 32), BF16)
            s_zsg = [sb5(f"p5_zsg{i}", (32, 128), BF16) for i in range(2)]
            s_zsu = [sb5(f"p5_zsu{i}", (32, 128), BF16) for i in range(2)]
            p5x = ExitStack()
            _psum_stack.append(p5x)
            ps_x = psum("p5_px", (32, TH))

            @on(block.sync)
            def _(q, xf_done=xf_done):
                q.wait_ge(qS.sem, xf_done)
                q.wait_ge(qS.sem, zs_store)
                qS.inc(q.dma_start(out=s_oneh32[:], in_=oneh32), 16)
                for kb in range(32):
                    qS.inc(q.dma_start(out=xf[:, kb * TH:(kb + 1) * TH],
                                       in_=xf_s[kb * 128:(kb + 1) * 128, :]), 16)
            xf_slab = qS.n

            @on(block.tensor)
            def _(t, xf_slab=xf_slab):
                t.wait_ge(qS.sem, xf_slab)
                last = None
                for kb in range(32):
                    for nn in range(4):
                        last = t.matmul(ps_x[:, nn * 512:(nn + 1) * 512],
                                        s_oneh32[:, kb * 32:(kb + 1) * 32],
                                        xf[:, kb * TH + nn * 512:kb * TH + (nn + 1) * 512],
                                        start=(kb == 0), stop=(kb == 31))
                cT.inc(last)
            xfg_mm = cT.n

            @on(block.vector)
            def _(v, xfg_mm=xfg_mm):
                v.wait_ge(cT.sem, xfg_mm)
                cV.inc(v.tensor_copy(xfg[:], ps_x[:]))
            xfg_ev = cV.n
            _psum_stack.pop()
            p5x.close()
            psG = psum("p5_pG", (128, TH))
            psU = psum("p5_pU", (128, TH))

            gu = {}
            for ib in range(NKB_D):
                buf = ib % 2

                @on(block.gpsimd)
                def _(g, ib=ib, buf=buf):
                    if (ib - 2, "mmU") in gu:
                        g.wait_ge(cT.sem, gu[(ib - 2, "mmU")])
                    if ib == 0:
                        g.wait_ge(qS.sem, p1b_done)
                        g.wait_ge(cV.sem, p4_vec_done)
                    qG.inc(g.dma_start(
                        out=wgt[buf][:],
                        in_=wg_s[:, ib * 128:(ib + 1) * 128]
                        .rearrange("(kb p) c -> p kb c", p=128)), 16)
                    qG.inc(g.dma_start(
                        out=wut[buf][:],
                        in_=wu_s[:, ib * 128:(ib + 1) * 128]
                        .rearrange("(kb p) c -> p kb c", p=128)), 16)
                    qG.inc(g.dma_start(out=s_zsg[buf][:],
                                       in_=zsg_s[:, ib * 128:(ib + 1) * 128]), 16)
                    qG.inc(g.dma_start(out=s_zsu[buf][:],
                                       in_=zsu_s[:, ib * 128:(ib + 1) * 128]), 16)
                wgu_dma = qG.n

                @on(block.tensor)
                def _(t, ib=ib, buf=buf, wgu_dma=wgu_dma):
                    t.wait_ge(qG.sem, wgu_dma)
                    if ib == 0:
                        t.wait_ge(cV.sem, xfg_ev)
                    if (ib - 1, "sg") in gu:
                        t.wait_ge(cA.sem, gu[(ib - 1, "sg")])
                    last = None
                    for kb in range(32):
                        for nn in range(4):
                            last = t.matmul(psG[:, nn * 512:(nn + 1) * 512],
                                            wgt[buf][:, kb * 128:(kb + 1) * 128],
                                            xf[:, kb * TH + nn * 512:kb * TH + (nn + 1) * 512],
                                            start=(kb == 0), stop=False)
                    for nn in range(4):
                        last = t.matmul(psG[:, nn * 512:(nn + 1) * 512],
                                        s_zsg[buf][:],
                                        xfg[:, nn * 512:(nn + 1) * 512],
                                        start=False, stop=True)
                    cT.inc(last)
                    gu[(ib, "mmG")] = cT.n
                    if (ib - 1, "h") in gu:
                        t.wait_ge(cV.sem, gu[(ib - 1, "h")])
                    for kb in range(32):
                        for nn in range(4):
                            last = t.matmul(psU[:, nn * 512:(nn + 1) * 512],
                                            wut[buf][:, kb * 128:(kb + 1) * 128],
                                            xf[:, kb * TH + nn * 512:kb * TH + (nn + 1) * 512],
                                            start=(kb == 0), stop=False)
                    for nn in range(4):
                        last = t.matmul(psU[:, nn * 512:(nn + 1) * 512],
                                        s_zsu[buf][:],
                                        xfg[:, nn * 512:(nn + 1) * 512],
                                        start=False, stop=True)
                    cT.inc(last)
                    gu[(ib, "mmU")] = cT.n

                @on(block.scalar)
                def _(a, ib=ib):
                    a.wait_ge(cT.sem, gu[(ib, "mmG")])
                    if (ib - 1, "h") in gu:
                        a.wait_ge(cV.sem, gu[(ib - 1, "h")])
                    cA.inc(a.activation(sg5[:], psG[:], AF.Silu))
                    gu[(ib, "sg")] = cA.n

                @on(block.vector)
                def _(v, ib=ib):
                    v.wait_ge(cA.sem, gu[(ib, "sg")])
                    v.wait_ge(cT.sem, gu[(ib, "mmU")])
                    if (ib - 1, "hst") in gu:
                        v.wait_ge(qS.sem, gu[(ib - 1, "hst")])
                    cV.inc(v.tensor_tensor(hbf[:], sg5[:], psU[:], OP.mult))
                    gu[(ib, "h")] = cV.n

                @on(block.sync)
                def _(q, ib=ib):
                    q.wait_ge(cV.sem, gu[(ib, "h")])
                    qS.inc(q.dma_start(out=h_s[ib * 128:(ib + 1) * 128, :], in_=hbf[:]), 16)
                    gu[(ib, "hst")] = qS.n
            h_done = qS.n

        p5b = ExitStack()
        with p5b:
            _psum_stack.append(p5b); p5b.callback(_psum_stack.pop)
            def sb6(n, s, d):
                return p5b.enter_context(nc.sbuf_tensor(n, list(s), d))
            hslab = sb6("p5_hs", (128, NKB_D * 512), BF16)
            wdt = [sb6(f"p5_wd{i}", (128, NKB_D * 128), BF16) for i in range(2)]
            hg = sb6("p5_hg", (86, 512), BF16)
            xf5 = sb6("p5_xf5", (128, 512), BF16)
            r2t = sb6("p5_r2", (128, 512), F32)
            r2bf = sb6("p5_r2bf", (128, 512), BF16)
            sq5 = sb6("p5_sq5", (128, 512), BF16)
            rowm5 = sb6("p5_rm", (1, TH), F32)
            rowv5 = sb6("p5_rv", (1, TH), F32)
            rowt5 = sb6("p5_rt", (1, TH), F32)
            scb5 = sb6("p5_scb", (128, TH), F32)
            mrb5 = sb6("p5_mrb", (128, TH), F32)
            outt = sb6("p5_out", (128, TH), F32)
            s_oneh86 = sb6("p5_oh86", (128, 86 * 86), BF16)
            s_zsd = [sb6(f"p5_zsd{i}", (86, 128), BF16) for i in range(2)]
            psD = [psum("p5_pD0", (128, 512)), psum("p5_pD1", (128, 512))]
            ps_hg = psum("p5_phg", (86, 512))
            ps_m5 = psum("p5_pm5", (1, 512))
            ps_v5 = psum("p5_pv5", (1, 512))

            @on(block.sync)
            def _(q):
                qS.inc(q.dma_start(out=s_oneh86[:], in_=oneh86), 16)
            p5b_consts = qS.n

            dn = {}
            for tt in range(4):
                c0 = tt * 512

                @on(block.sync)
                def _(q, tt=tt, c0=c0, h_done=h_done):
                    if tt == 0:
                        q.wait_ge(qS.sem, h_done)
                    else:
                        q.wait_ge(cT.sem, dn[(tt - 1, 31, "mm")])
                    qS.inc(q.dma_start(
                        out=hslab[:],
                        in_=h_s[:, c0:c0 + 512].rearrange("(kb p) c -> p kb c", p=128)), 16)
                hs_dma = qS.n

                @on(block.tensor)
                def _(t, tt=tt, hs_dma=hs_dma):
                    t.wait_ge(qS.sem, hs_dma)
                    if tt == 0:
                        t.wait_ge(qS.sem, p5b_consts)
                    if (tt - 1, "hg_ev") in dn:
                        t.wait_ge(cV.sem, dn[(tt - 1, "hg_ev")])
                    last = None
                    for kb in range(NKB_D):
                        last = t.matmul(ps_hg[:], s_oneh86[:, kb * 86:(kb + 1) * 86],
                                        hslab[:, kb * 512:(kb + 1) * 512],
                                        start=(kb == 0), stop=(kb == NKB_D - 1))
                    cT.inc(last)
                    dn[(tt, "hg_mm")] = cT.n

                @on(block.vector)
                def _(v, tt=tt):
                    v.wait_ge(cT.sem, dn[(tt, "hg_mm")])
                    cV.inc(v.tensor_copy(hg[:], ps_hg[:]))
                    dn[(tt, "hg_ev")] = cV.n

                for fb in range(32):
                    buf = fb % 2

                    @on(block.gpsimd)
                    def _(g, tt=tt, fb=fb, buf=buf):
                        if tt == 0 and fb < 2:
                            g.wait_ge(cT.sem, gu[(NKB_D - 1, "mmU")])
                        if (tt, fb - 2, "mm") in dn:
                            g.wait_ge(cT.sem, dn[(tt, fb - 2, "mm")])
                        elif tt > 0 and fb < 2:
                            g.wait_ge(cT.sem, dn[(tt - 1, 30 + fb, "mm")])
                        qG.inc(g.dma_start(
                            out=wdt[buf][:],
                            in_=wd_s[:, fb * 128:(fb + 1) * 128]
                            .rearrange("(kb p) c -> p kb c", p=128)), 16)
                        qG.inc(g.dma_start(out=s_zsd[buf][:],
                                           in_=zsd_s[:, fb * 128:(fb + 1) * 128]), 16)
                    wd_dma = qG.n

                    @on(block.sync)
                    def _(q, tt=tt, fb=fb, c0=c0):
                        if (tt, fb - 1, "r2") in dn:
                            q.wait_ge(cV.sem, dn[(tt, fb - 1, "r2")])
                        elif tt > 0 and fb == 0:
                            q.wait_ge(cV.sem, dn[(tt - 1, 31, "r2")])
                        qS.inc(q.dma_start(out=xf5[:],
                                           in_=xf_s[fb * 128:(fb + 1) * 128, c0:c0 + 512]), 16)
                        dn[(tt, fb, "xf")] = qS.n

                    @on(block.tensor)
                    def _(t, tt=tt, fb=fb, buf=buf, wd_dma=wd_dma):
                        t.wait_ge(qG.sem, wd_dma)
                        if fb == 0:
                            t.wait_ge(cV.sem, dn[(tt, "hg_ev")])
                        if (tt, fb - 2, "r2") in dn:
                            t.wait_ge(cV.sem, dn[(tt, fb - 2, "r2")])
                        elif tt > 0 and fb < 2:
                            t.wait_ge(cV.sem, dn[(tt - 1, 30 + fb, "r2")])
                        last = None
                        for kb in range(NKB_D):
                            last = t.matmul(psD[buf][:],
                                            wdt[buf][:, kb * 128:(kb + 1) * 128],
                                            hslab[:, kb * 512:(kb + 1) * 512],
                                            start=(kb == 0), stop=False)
                        last = t.matmul(psD[buf][:], s_zsd[buf][:],
                                        hg[:], start=False, stop=True)
                        cT.inc(last)
                        dn[(tt, fb, "mm")] = cT.n

                    @on(block.vector)
                    def _(v, tt=tt, fb=fb, buf=buf):
                        v.wait_ge(cT.sem, dn[(tt, fb, "mm")])
                        v.wait_ge(qS.sem, dn[(tt, fb, "xf")])
                        if (tt, fb - 1, "sq") in dn:
                            v.wait_ge(cA.sem, dn[(tt, fb - 1, "sq")])
                            v.wait_ge(qS.sem, dn[(tt, fb - 1, "r2st")])
                        elif tt > 0 and fb == 0:
                            v.wait_ge(cA.sem, dn[(tt - 1, 31, "sq")])
                            v.wait_ge(qS.sem, dn[(tt - 1, 31, "r2st")])
                        cV.inc(v.tensor_tensor(r2t[:], psD[buf][:], xf5[:], OP.add))
                        dn[(tt, fb, "r2")] = cV.n

                    @on(block.sync)
                    def _(q, tt=tt, fb=fb, c0=c0):
                        q.wait_ge(cV.sem, dn[(tt, fb, "r2")])
                        qS.inc(q.dma_start(out=r2_s[fb * 128:(fb + 1) * 128, c0:c0 + 512],
                                           in_=r2t[:]), 16)
                        dn[(tt, fb, "r2st")] = qS.n

                    @on(block.scalar)
                    def _(a, tt=tt, fb=fb):
                        a.wait_ge(cV.sem, dn[(tt, fb, "r2")])
                        if (tt, fb - 1, "stats") in dn:
                            a.wait_ge(cT.sem, dn[(tt, fb - 1, "stats")])
                        elif tt > 0 and fb == 0:
                            a.wait_ge(cT.sem, dn[(tt - 1, 31, "stats")])
                        a.copy(r2bf[:], r2t[:])
                        cA.inc(a.activation(sq5[:], r2t[:], AF.Square))
                        dn[(tt, fb, "sq")] = cA.n

                    @on(block.tensor)
                    def _(t, tt=tt, fb=fb, c0=c0):
                        t.wait_ge(cA.sem, dn[(tt, fb, "sq")])
                        if fb == 0 and (tt - 1, "strow") in dn:
                            t.wait_ge(cV.sem, dn[(tt - 1, "strow")])
                        t.matmul(ps_m5[:], s_ones[:], r2bf[:],
                                 start=(fb == 0), stop=(fb == 31))
                        last = t.matmul(ps_v5[:], s_ones[:], sq5[:],
                                        start=(fb == 0), stop=(fb == 31))
                        cT.inc(last)
                        dn[(tt, fb, "stats")] = cT.n

                @on(block.vector)
                def _(v, tt=tt, c0=c0):
                    v.wait_ge(cT.sem, dn[(tt, 31, "stats")])
                    v.tensor_copy(rowm5[0:1, c0:c0 + 512], ps_m5[:])
                    cV.inc(v.tensor_copy(rowv5[0:1, c0:c0 + 512], ps_v5[:]))
                    dn[(tt, "strow")] = cV.n

            @on(block.vector)
            def _(v):
                v.wait_ge(cV.sem, dn[(3, "strow")])
                v.tensor_scalar(rowm5[:], rowm5[:], 1.0 / H, None, OP.mult)
                v.tensor_scalar(rowv5[:], rowv5[:], 1.0 / H, None, OP.mult)
                v.tensor_tensor(rowt5[:], rowm5[:], rowm5[:], OP.mult)
                cV.inc(v.tensor_tensor(rowv5[:], rowv5[:], rowt5[:], OP.subtract))
            ln2_rows0 = cV.n

            @on(block.scalar)
            def _(a, ln2_rows0=ln2_rows0):
                a.wait_ge(cV.sem, ln2_rows0)
                a.activation(rowv5[:], rowv5[:], AF.Ln, bias=s_cst[0:1, 1:2])
                cA.inc(a.activation(rowv5[:], rowv5[:], AF.Exp, scale=s_cst[0:1, 4:5]))
            ln2_rstd = cA.n

            @on(block.vector)
            def _(v, ln2_rstd=ln2_rstd):
                v.wait_ge(cA.sem, ln2_rstd)
                v.tensor_tensor(rowm5[:], rowm5[:], rowv5[:], OP.mult)
                cV.inc(v.tensor_scalar(rowm5[:], rowm5[:], -1.0, None, OP.mult))
            ln2_rows = cV.n

            @on(block.sync)
            def _(q, ln2_rows=ln2_rows):
                q.wait_ge(cV.sem, ln2_rows)
                qS.inc(q.dma_start(out=row_bounce[0:1, 0:TCH], in_=rowv5[0:1, 0:TCH]), 16)
                qS.inc(q.dma_start(out=row_bounce[1:2, 0:TCH], in_=rowv5[0:1, TCH:2 * TCH]), 16)
                qS.inc(q.dma_start(out=row_bounce[2:3, 0:TCH], in_=rowm5[0:1, 0:TCH]), 16)
                qS.inc(q.dma_start(out=row_bounce[3:4, 0:TCH], in_=rowm5[0:1, TCH:2 * TCH]), 16)
            ln2_rows_st = qS.n

            @on(block.gpsimd)
            def _(g, ln2_rows_st=ln2_rows_st):
                g.wait_ge(qS.sem, ln2_rows_st)
                qG.inc(g.dma_start(out=scb5[:, 0:TCH],
                                   in_=row_bounce[0:1, :].partition_broadcast(128).squeeze(1)), 16)
                qG.inc(g.dma_start(out=scb5[:, TCH:2 * TCH],
                                   in_=row_bounce[1:2, :].partition_broadcast(128).squeeze(1)), 16)
                qG.inc(g.dma_start(out=mrb5[:, 0:TCH],
                                   in_=row_bounce[2:3, :].partition_broadcast(128).squeeze(1)), 16)
                qG.inc(g.dma_start(out=mrb5[:, TCH:2 * TCH],
                                   in_=row_bounce[3:4, :].partition_broadcast(128).squeeze(1)), 16)
            ln2_bc = qG.n

            fin = {}
            for fb in range(32):
                @on(block.sync)
                def _(q, fb=fb):
                    if fb >= 1:
                        q.wait_ge(cV.sem, fin[(fb - 1, "o")])
                    qS.inc(q.dma_start(out=outt[:],
                                       in_=r2_s[fb * 128:(fb + 1) * 128, :]), 16)
                    fin[(fb, "ld")] = qS.n

                @on(block.vector)
                def _(v, fb=fb, ln2_bc=ln2_bc):
                    if fb == 0:
                        v.wait_ge(qG.sem, ln2_bc)
                    v.wait_ge(qS.sem, fin[(fb, "ld")])
                    v.tensor_tensor(outt[:], outt[:], scb5[:], OP.mult)
                    cV.inc(v.tensor_tensor(outt[:], outt[:], mrb5[:], OP.add))
                    fin[(fb, "o")] = cV.n

                @on(block.sync)
                def _(q, fb=fb):
                    q.wait_ge(cV.sem, fin[(fb, "o")])
                    qS.inc(q.dma_start(out=out[fb * 128:(fb + 1) * 128, :], in_=outt[:]), 16)

            @on(block.sync)
            def _(q):
                q.wait_ge(qS.sem, qS.n)
                q.wait_ge(qG.sem, qG.n)

    return nc


def host_prep(inputs):
    x = inputs["L0_attn_hidden_states"]
    W_in = inputs["L0_attn_W_in"]
    conv_w = inputs["L0_attn_conv_w"]
    conv_b = inputs["L0_attn_conv_b"]
    W_x = inputs["L0_attn_W_x"]
    W_dt = inputs["L0_attn_W_dt"]
    b_dt = inputs["L0_attn_b_dt"]
    A_log = inputs["L0_attn_A_log"]
    Dp = inputs["L0_attn_D"]
    W_out = inputs["L0_attn_W_out"]

    def C(a):
        return np.ascontiguousarray(a)

    ones_bf = np.ones((128, 1), dtype=ml_dtypes.bfloat16)
    oneh32 = np.zeros((128, 32 * 32), dtype=ml_dtypes.bfloat16)
    for kb in range(32):
        oneh32[:, kb * 32 + kb] = 1
    oneh86 = np.zeros((128, 86 * 86), dtype=ml_dtypes.bfloat16)
    for kb in range(86):
        oneh86[:, kb * 86 + kb] = 1
    shifts = C(np.tile((np.arange(8, dtype=np.int32) * 4)[None, :], (128, 1)))
    mask15 = np.full((128, 1), 15, dtype=np.int32)

    maps = []
    for core in range(8):
        b, h = core // 2, core % 2
        dlo, dhi = h * DH, (h + 1) * DH
        xt = C(x[b].T.astype(np.float32))
        m = {
            "xT": xt,
            "xT_my": C(xt[:, h * TH:(h + 1) * TH]),
            "winT": C(np.concatenate([W_in[dlo:dhi], W_in[DIN + dlo:DIN + dhi]], 0).T),
            "convw": C(conv_w[dlo:dhi].reshape(NDB, 128, 4).transpose(1, 0, 2).reshape(128, NDB * 4)),
            "convb": C(conv_b[dlo:dhi].reshape(NDB, 128).T),
            "bdt": C(b_dt[dlo:dhi].reshape(NDB, 128).T),
            "Dv": C(Dp[dlo:dhi].reshape(NDB, 128).T),
            "alog": C(A_log[dlo:dhi].reshape(NDB, 128, NST).transpose(1, 0, 2).reshape(128, NDB * NST)),
            "wxT": C(W_x[:, dlo:dhi].T),
            "wdtT": C(W_dt[dlo:dhi].T),
            "woutT": C(W_out[:, dlo:dhi].T),
            "gq": np.asarray(inputs["L0_ffn_gate_qweight"]),
            "gs": np.asarray(inputs["L0_ffn_gate_scales"]),
            "gz": np.asarray(inputs["L0_ffn_gate_qzeros"]),
            "uq": np.asarray(inputs["L0_ffn_up_qweight"]),
            "us": np.asarray(inputs["L0_ffn_up_scales"]),
            "uz": np.asarray(inputs["L0_ffn_up_qzeros"]),
            "dq": np.asarray(inputs["L0_ffn_down_qweight"]),
            "ds": np.asarray(inputs["L0_ffn_down_scales"]),
            "dz": np.asarray(inputs["L0_ffn_down_qzeros"]),
            "ones_bf": ones_bf,
            "oneh32": oneh32,
            "oneh86": oneh86,
            "shifts": shifts,
            "mask15": mask15,
            "shifts32": C(shifts[:32]),
            "constv": C(np.tile(np.array([[EPS_RMS, EPS_LN, 1.0 / R, 1.0 / NST, -0.5, -1.0, 1.0, 0.0]], dtype=np.float32), (128, 1))),
            "mask15_32": C(mask15[:32]),
            "shifts86": C(shifts[:86]),
            "mask15_86": C(mask15[:86]),
        }
        maps.append(m)
    return maps


_RES = {}


def kernel(**inputs):
    maps = host_prep(inputs)
    nc = build_nc()
    res = run_bass_kernel_spmd(nc, maps, core_ids=list(range(8)))
    _RES["res"] = res
    out = np.empty((4, L, H), dtype=np.float32)
    for core in range(8):
        b, h = core // 2, core % 2
        out[b, h * TH:(h + 1) * TH, :] = np.asarray(res.results[core]["out"]).T
    return out


if __name__ == "__main__":
    print("building program ...")
    nc = build_nc()
    print("built ok")


# version 02
# speedup vs baseline: 1.0231x; 1.0231x over previous
import os
import numpy as np
import ml_dtypes
import concourse.bass as bass
import concourse.mybir as mybir
from concourse.alu_op_type import AluOpType
from concourse.bass_utils import run_bass_kernel_spmd
from contextlib import ExitStack

F32 = mybir.dt.float32
BF16 = mybir.dt.bfloat16
I32 = mybir.dt.int32
AF = mybir.ActivationFunctionType
OP = AluOpType

H = 4096
DIN = 8192
DH = 4096
NDB = 32
NST = 16
R = 256
L = 4096
TCH = 1024
NCH = L // TCH
THF = 512
TH = 2048
II = 11008
NKB_D = 86
EPS_RMS = 1e-6
EPS_LN = 1e-5

PAIRS = [[0, 1], [2, 3], [4, 5], [6, 7]]
DEBUG = os.environ.get("MK_DEBUG", "") == "1"


class Ctr:
    def __init__(self, stack, nc, name):
        self.sem = stack.enter_context(nc.semaphore(name))
        self.n = 0

    def inc(self, instr, k=1):
        instr.then_inc(self.sem, k)
        self.n += k
        return self.n


def build_nc():
    nc = bass.Bass("TRN2", target_bir_lowering=False)

    def din(name, shape, dt):
        return nc.dram_tensor(name, list(shape), dt, kind="ExternalInput").ap()

    xT = din("xT", (H, L), F32)
    xT_my = din("xT_my", (H, TH), F32)
    winT = din("winT", (H, DIN), F32)
    convw = din("convw", (128, NDB * 4), F32)
    convb = din("convb", (128, NDB), F32)
    bdt = din("bdt", (128, NDB), F32)
    Dv = din("Dv", (128, NDB), F32)
    alog = din("alog", (128, NDB * NST), F32)
    wxT = din("wxT", (DH, 288), F32)
    wdtT = din("wdtT", (R, DH), F32)
    woutT = din("woutT", (DH, H), F32)
    gq = din("gq", (H, II // 8), I32)
    gs = din("gs", (32, II), F32)
    gz = din("gz", (32, II // 8), I32)
    uq = din("uq", (H, II // 8), I32)
    us = din("us", (32, II), F32)
    uz = din("uz", (32, II // 8), I32)
    dq = din("dq", (II, H // 8), I32)
    ds = din("ds", (NKB_D, H), F32)
    dz = din("dz", (NKB_D, H // 8), I32)
    ones_bf = din("ones_bf", (128, 1), BF16)
    oneh32 = din("oneh32", (128, 32 * 32), BF16)
    oneh86 = din("oneh86", (128, 86 * 86), BF16)
    shifts = din("shifts", (128, 8), I32)
    mask15 = din("mask15", (128, 1), I32)
    shifts32 = din("shifts32", (32, 8), I32)
    mask15_32 = din("mask15_32", (32, 1), I32)
    shifts86 = din("shifts86", (86, 8), I32)
    mask15_86 = din("mask15_86", (86, 1), I32)
    constv = din("constv", (128, 8), F32)

    out = nc.dram_tensor("out", [H, TH], F32, kind="ExternalOutput").ap()

    def scratch(name, shape, dt):
        kind = "ExternalOutput" if DEBUG else "Internal"
        return nc.dram_tensor(name, list(shape), dt, kind=kind).ap()

    proj_s = scratch("proj_s", (DIN, L), BF16)
    u_s = scratch("u_s", (DH, L), BF16)
    sg_s = scratch("sg_s", (DH, L), BF16)
    y_s = scratch("y_s", (DH, L), BF16)
    wg_s = scratch("wg_s", (H, II), BF16)
    wu_s = scratch("wu_s", (H, II), BF16)
    wd_s = scratch("wd_s", (II, H), BF16)
    zsg_s = scratch("zsg_s", (32, II), BF16)
    zsu_s = scratch("zsu_s", (32, II), BF16)
    zsd_s = scratch("zsd_s", (86, H), BF16)
    xf_s = scratch("xf_s", (H, TH), BF16)
    r1_s = scratch("r1_s", (H, TH), F32)
    r2_s = scratch("r2_s", (H, TH), F32)
    h_s = scratch("h_s", (II, TH), BF16)
    ssm_dbg = scratch("ssm_dbg", (NCH * 288, TCH), F32) if DEBUG else None
    dt_dbg = scratch("dt_dbg", (DH, TCH), F32) if DEBUG else None
    tsn_dbg = scratch("tsn_dbg", (256, TCH), BF16) if DEBUG else None
    pre_dbg = scratch("pre_dbg", (128, TCH), F32) if DEBUG else None
    wdt0_dbg = scratch("wdt0_dbg", (128, 256), BF16) if DEBUG else None

    ar_in = nc.dram_tensor("ar_in", [288, TCH], F32, kind="Internal").ap()
    ar_out = nc.dram_tensor("ar_out", [288, TCH], F32, kind="Internal").ap()
    rs_in = nc.dram_tensor("rs_in", [4 * H, TCH], BF16, kind="Internal").ap()
    rs_out = nc.dram_tensor("rs_out", [2 * H, TCH], BF16, kind="Internal").ap()
    row_bounce = nc.dram_tensor("row_bounce", [64, TCH], F32, kind="Internal").ap()

    es = ExitStack()
    with es:
        def sb(name, shape, dt):
            return es.enter_context(nc.sbuf_tensor(name, list(shape), dt))

        _psum_stack = [es]

        def psum(name, shape):
            return _psum_stack[-1].enter_context(nc.psum_tensor(name, list(shape), F32))

        qS = Ctr(es, nc, "qS")
        qG = Ctr(es, nc, "qG")
        cT = Ctr(es, nc, "cT")
        cV = Ctr(es, nc, "cV")
        cA = Ctr(es, nc, "cA")
        cC = Ctr(es, nc, "cC")

        block = es.enter_context(nc.Block())

        def on(engine_section):
            def run(fn):
                engine_section(fn)
            return run

        s_convw = sb("s_convw", (128, NDB * 4), F32)
        s_convb = sb("s_convb", (128, NDB), F32)
        s_bdt = sb("s_bdt", (128, NDB), F32)
        s_D = sb("s_D", (128, NDB), F32)
        s_A = sb("s_A", (128, NDB * NST), F32)
        s_ones = sb("s_ones", (128, 1), BF16)
        s_shifts = sb("s_shifts", (128, 8), I32)
        s_mask = sb("s_mask", (128, 1), I32)
        s_carry = sb("s_carry", (128, NDB * NST), F32)
        s_cst = sb("s_cst", (128, 8), F32)

        p0s = ExitStack()
        with p0s:
            def sb0(n, s, d):
                return p0s.enter_context(nc.sbuf_tensor(n, list(s), d))
            s_alog = sb0("s_alog", (128, NDB * NST), F32)
            s_shifts32 = sb0("s_shifts32", (32, 8), I32)
            s_mask32 = sb0("s_mask32", (32, 1), I32)
            s_shifts86 = sb0("s_shifts86", (86, 8), I32)
            s_mask86 = sb0("s_mask86", (86, 1), I32)
            s_sflt = sb0("s_sflt", (86, II), F32)
            s_zq = sb0("s_zq", (86, II // 8), I32)
            s_zi = sb0("s_zi", (86, II), I32)
            s_zs = sb0("s_zs", (86, II), BF16)

            @on(block.sync)
            def _(q):
                for dst, src in [(s_convw, convw), (s_convb, convb), (s_bdt, bdt),
                                 (s_D, Dv), (s_ones, ones_bf),
                                 (s_shifts, shifts), (s_mask, mask15),
                                 (s_shifts32, shifts32), (s_mask32, mask15_32),
                                 (s_shifts86, shifts86), (s_mask86, mask15_86),
                                 (s_alog, alog), (s_cst, constv)]:
                    qS.inc(q.dma_start(out=dst[:], in_=src), 16)
            p0_loads = qS.n

            @on(block.scalar)
            def _(a):
                a.wait_ge(qS.sem, p0_loads)
                cA.inc(a.activation(s_A[:], s_alog[:], AF.Exp))
            p0_exp = cA.n

            @on(block.vector)
            def _(v):
                v.wait_ge(cA.sem, p0_exp)
                v.tensor_scalar(s_A[:], s_A[:], -1.0, None, OP.mult)
                cV.inc(v.memset(s_carry[:], 0.0))
            p0_A = cV.n

            def emit_zs(s_in, z_in, zdst, P, N, shp, msk, prev_st):
                @on(block.sync)
                def _(q):
                    if prev_st:
                        q.wait_ge(qS.sem, prev_st)
                    qS.inc(q.dma_start(out=s_sflt[0:P, 0:N], in_=s_in), 16)
                    qS.inc(q.dma_start(out=s_zq[0:P, 0:N // 8], in_=z_in), 16)
                marker = qS.n

                @on(block.vector)
                def _(v):
                    v.wait_ge(qS.sem, marker)
                    v.wait_ge(qS.sem, p0_loads)
                    for j in range(8):
                        v.tensor_scalar(s_zi[0:P, j:N:8], s_zq[0:P, 0:N // 8],
                                        shp[:, j:j + 1], msk[:],
                                        OP.logical_shift_right, OP.bitwise_and)
                    v.tensor_tensor(s_zs[0:P, 0:N], s_zi[0:P, 0:N],
                                    s_sflt[0:P, 0:N], OP.mult)
                    cV.inc(v.tensor_scalar(s_zs[0:P, 0:N], s_zs[0:P, 0:N],
                                           -1.0, None, OP.mult))
                zz = cV.n

                @on(block.sync)
                def _(q):
                    q.wait_ge(cV.sem, zz)
                    qS.inc(q.dma_start(out=zdst, in_=s_zs[0:P, 0:N]), 16)
                return qS.n

            st1 = emit_zs(gs, gz, zsg_s, 32, II, s_shifts32, s_mask32, None)
            st2 = emit_zs(us, uz, zsu_s, 32, II, s_shifts32, s_mask32, st1)
            zs_store = emit_zs(ds, dz, zsd_s, 86, H, s_shifts86, s_mask86, st2)
        p0_vec_done = cV.n

        p1 = ExitStack()
        with p1:
            _psum_stack.append(p1); p1.callback(_psum_stack.pop)
            def sb1(n, s, d):
                return p1.enter_context(nc.sbuf_tensor(n, list(s), d))
            xbf = sb1("p1_xbf", (128, 32 * TH), BF16)
            swin = [sb1(f"p1_swin{i}", (128, 32 * 128), BF16) for i in range(2)]
            pbf = [sb1(f"p1_pbf{i}", (128, TH), BF16) for i in range(2)]
            pp = [psum("p1_ps0", (128, TH)), psum("p1_ps1", (128, TH))]

            mm_done = {}
            evac_done = {}
            store_done = {}

            for th in range(2):
                t0 = th * TH

                @on(block.gpsimd)
                def _(g, th=th, t0=t0):
                    if th == 0:
                        g.wait_ge(qS.sem, zs_store)
                        g.wait_ge(cV.sem, p0_vec_done)
                        g.wait_ge(cA.sem, p0_exp)
                    else:
                        g.wait_ge(cT.sem, mm_done[(th - 1, 63)])
                    for kb in range(32):
                        qG.inc(g.dma_start(
                            out=xbf[:, kb * TH:(kb + 1) * TH],
                            in_=xT[kb * 128:(kb + 1) * 128, t0:t0 + TH]), 16)

                for fb in range(64):
                    buf = fb % 2

                    @on(block.gpsimd)
                    def _(g, th=th, fb=fb, buf=buf):
                        if fb >= 2:
                            g.wait_ge(cT.sem, mm_done[(th, fb - 2)])
                        qG.inc(g.dma_start(
                            out=swin[buf][:],
                            in_=winT[:, fb * 128:(fb + 1) * 128]
                            .rearrange("(kb p) c -> p kb c", p=128)), 16)
                    win_dma = qG.n

                    @on(block.tensor)
                    def _(t, th=th, fb=fb, buf=buf, win_dma=win_dma):
                        t.wait_ge(qG.sem, win_dma)
                        if fb >= 2:
                            t.wait_ge(cA.sem, evac_done[(th, fb - 2)])
                        last = None
                        for kb in range(32):
                            for nn in range(4):
                                last = t.matmul(
                                    pp[buf][:, nn * 512:(nn + 1) * 512],
                                    swin[buf][:, kb * 128:(kb + 1) * 128],
                                    xbf[:, kb * TH + nn * 512:kb * TH + (nn + 1) * 512],
                                    start=(kb == 0), stop=(kb == 31))
                        cT.inc(last)
                        mm_done[(th, fb)] = cT.n

                    @on(block.scalar)
                    def _(a, th=th, fb=fb, buf=buf):
                        a.wait_ge(cT.sem, mm_done[(th, fb)])
                        if fb >= 2:
                            a.wait_ge(qS.sem, store_done[(th, fb - 2)])
                        cA.inc(a.copy(pbf[buf][:], pp[buf][:]))
                        evac_done[(th, fb)] = cA.n

                    @on(block.sync)
                    def _(q, th=th, fb=fb, buf=buf, t0=t0):
                        q.wait_ge(cA.sem, evac_done[(th, fb)])
                        qS.inc(q.dma_start(
                            out=proj_s[fb * 128:(fb + 1) * 128, t0:t0 + TH],
                            in_=pbf[buf][:]), 16)
                        store_done[(th, fb)] = qS.n
            p1_done = qS.n

        p1b = ExitStack()
        with p1b:
            def sb2(n, s, d):
                return p1b.enter_context(nc.sbuf_tensor(n, list(s), d))
            qtile = [sb2(f"dq_q{i}", (128, II // 8), I32) for i in range(2)]
            sbc = [sb2(f"dq_s{i}", (128, II), BF16) for i in range(2)]
            wint = sb2("dq_wi", (128, II), I32)
            wbf = [sb2(f"dq_w{i}", (128, II), BF16) for i in range(2)]

            steps = ([("g", kb) for kb in range(32)] +
                     [("u", kb) for kb in range(32)] +
                     [("d", kb) for kb in range(NKB_D)])
            dq_v = {}
            dq_st = {}
            for i, (w, kb) in enumerate(steps):
                buf = i % 2
                src = {"g": gq, "u": uq, "d": dq}[w]
                srow = {"g": gs, "u": us, "d": ds}[w]
                dstt = {"g": wg_s, "u": wu_s, "d": wd_s}[w]
                ncols = II if w != "d" else H
                nq = ncols // 8

                @on(block.sync)
                def _(q, i=i, buf=buf, src=src, kb=kb, nq=nq):
                    if i == 0:
                        q.wait_ge(cT.sem, mm_done[(1, 63)])
                    if i >= 2:
                        q.wait_ge(cV.sem, dq_v[i - 2])
                    qS.inc(q.dma_start(out=qtile[buf][:, 0:nq],
                                       in_=src[kb * 128:(kb + 1) * 128, :]), 16)
                dq_q = qS.n

                @on(block.gpsimd)
                def _(g, i=i, buf=buf, srow=srow, kb=kb, ncols=ncols):
                    if i == 0:
                        g.wait_ge(cT.sem, mm_done[(1, 63)])
                    if i >= 2:
                        g.wait_ge(cV.sem, dq_v[i - 2])
                    qG.inc(g.dma_start(
                        out=sbc[buf][:, 0:ncols],
                        in_=srow[kb:kb + 1, :].partition_broadcast(128).squeeze(1)), 16)
                dq_s = qG.n

                @on(block.vector)
                def _(v, i=i, buf=buf, ncols=ncols, nq=nq, dq_q=dq_q, dq_s=dq_s):
                    if i == 0:
                        v.wait_ge(qS.sem, p0_loads)
                    v.wait_ge(qS.sem, dq_q)
                    v.wait_ge(qG.sem, dq_s)
                    if i >= 2:
                        v.wait_ge(qS.sem, dq_st[i - 2])
                    for j in range(8):
                        v.tensor_scalar(wint[:, j:ncols:8], qtile[buf][:, 0:nq],
                                        s_shifts[:, j:j + 1], s_mask[:],
                                        OP.logical_shift_right, OP.bitwise_and)
                    cV.inc(v.tensor_tensor(wbf[buf][:, 0:ncols], wint[:, 0:ncols],
                                           sbc[buf][:, 0:ncols], OP.mult))
                    dq_v[i] = cV.n

                @on(block.sync)
                def _(q, i=i, buf=buf, dstt=dstt, kb=kb, ncols=ncols):
                    q.wait_ge(cV.sem, dq_v[i])
                    qS.inc(q.dma_start(out=dstt[kb * 128:(kb + 1) * 128, :],
                                       in_=wbf[buf][:, 0:ncols]), 16)
                    dq_st[i] = qS.n
            p1b_vec_done = cV.n
            p1b_done = qS.n

        p2 = ExitStack()
        with p2:
            _psum_stack.append(p2); p2.callback(_psum_stack.pop)
            def sb3(n, s, d):
                return p2.enter_context(nc.sbuf_tensor(n, list(s), d))
            u_slab = sb3("p2_u", (128, NDB * THF), BF16)
            xp = sb3("p2_xp", (128, THF + 3), BF16)
            xc = sb3("p2_xc", (128, THF), F32)
            gt2 = sb3("p2_gt", (128, THF), BF16)
            sgt2 = sb3("p2_sgt", (128, THF), BF16)
            s_wx = [sb3(f"p2_wx{i}", (128, 288), BF16) for i in range(2)]
            s_wdtT = sb3("p2_wdtT", (128, 2 * DH), BF16)
            ts_sb = [sb3(f"p2_ts{i}", (128, TCH), F32) for i in range(2)]
            bc_sb = [sb3(f"p2_bc{i}", (16, TCH), F32) for i in range(2)]
            sq_bf = sb3("p2_sq", (128, TCH), BF16)
            row_a = sb3("p2_rowa", (1, TCH), F32)
            row_b = sb3("p2_rowb", (1, TCH), F32)
            rstd_b = sb3("p2_rstdb", (128, TCH), F32)
            tsn = [sb3(f"p2_tsn{i}", (128, TCH), BF16) for i in range(2)]
            bcn = [sb3(f"p2_bcn{i}", (16, TCH), F32) for i in range(2)]
            Bb = sb3("p2_Bb", (128, NST * TCH), BF16)
            Cb = sb3("p2_Cb", (128, NST * TCH), BF16)
            dt_f = sb3("p2_dt", (128, TCH), F32)
            sp_t = sb3("p2_spt", (128, TCH), F32)
            dtu = sp_t
            u_db = sb3("p2_udb", (128, TCH), BF16)
            sg_db = sb3("p2_sgdb", (128, TCH), BF16)
            dA = sb3("p2_dA", (128, 2 * TCH), F32)
            Xb = sb3("p2_X", (128, 2 * TCH), F32)
            st_b = sb3("p2_st", (128, 2 * TCH), F32)
            fix = sb3("p2_fix", (128, 2), F32)
            yacc = sb3("p2_ys", (128, TCH), F32)
            ps_ts0 = psum("p2_pts0", (128, THF))
            ps_ts1 = psum("p2_pts1", (128, THF))
            ps_B = psum("p2_pB", (16, THF))
            ps_C = psum("p2_pC", (16, THF))
            ps_row = psum("p2_prow", (1, TCH))
            ps_dt = psum("p2_pdt", (128, TCH))

            conv_v = {}
            silu_a = {}
            wx_mm = {}
            ssm_evac = {}
            scan_fin = {}
            scan_sp = {}
            udma = {}
            prev_fin = {}

            NT2 = TCH // THF

            @on(block.gpsimd)
            def _(g):
                g.wait_ge(cV.sem, p1b_vec_done)
                qG.inc(g.dma_start(out=s_wdtT[:, 0:DH], in_=wdtT[0:128, :]), 16)
                qG.inc(g.dma_start(out=s_wdtT[:, DH:2 * DH], in_=wdtT[128:256, :]), 16)
            wdtT_res = qG.n

            for ch in range(NCH):
                t0 = ch * TCH
                for th2 in range(NT2):
                    tt0 = t0 + th2 * THF
                    for db in range(NDB):
                        def prev_key(d, ch=ch, th2=th2, db=db):
                            for k in [(ch, th2, db - 1), (ch, th2 - 1, NDB - 1),
                                      (ch - 1, NT2 - 1, NDB - 1)]:
                                if k in d:
                                    return k
                            return None

                        @on(block.sync)
                        def _(q, ch=ch, th2=th2, db=db, tt0=tt0):
                            if ch == 0 and th2 == 0 and db == 0:
                                q.wait_ge(qS.sem, p1_done)
                            pk = prev_key(conv_v)
                            if pk:
                                q.wait_ge(cV.sem, conv_v[pk])
                            lo = tt0 - 3
                            if lo < 0:
                                qS.inc(q.dma_start(
                                    out=xp[:, 3:THF + 3],
                                    in_=proj_s[db * 128:(db + 1) * 128, 0:THF]), 16)
                            else:
                                qS.inc(q.dma_start(
                                    out=xp[:, 0:THF + 3],
                                    in_=proj_s[db * 128:(db + 1) * 128, lo:lo + THF + 3]), 16)
                            pk = prev_key(silu_a)
                            if pk:
                                q.wait_ge(cA.sem, silu_a[pk])
                            qS.inc(q.dma_start(
                                out=gt2[:],
                                in_=proj_s[DH + db * 128:DH + (db + 1) * 128,
                                           tt0:tt0 + THF]), 16)
                        conv_dma = qS.n

                        @on(block.vector)
                        def _(v, ch=ch, th2=th2, db=db, tt0=tt0, conv_dma=conv_dma):
                            v.wait_ge(qS.sem, conv_dma)
                            if ch == 0 and th2 == 0 and db == 0:
                                v.wait_ge(qS.sem, p0_loads)
                            pk = prev_key(silu_a)
                            if pk:
                                v.wait_ge(cA.sem, silu_a[pk])
                            if tt0 - 3 < 0:
                                v.memset(xp[:, 0:3], 0.0)
                            v.tensor_scalar(xc[:], xp[:, 0:THF],
                                            s_convw[:, db * 4:db * 4 + 1],
                                            s_convb[:, db:db + 1], OP.mult, OP.add)
                            for j in range(1, 3):
                                v.scalar_tensor_tensor(
                                    xc[:], xp[:, j:j + THF],
                                    s_convw[:, db * 4 + j:db * 4 + j + 1],
                                    xc[:], OP.mult, OP.add)
                            cV.inc(v.scalar_tensor_tensor(
                                xc[:], xp[:, 3:3 + THF],
                                s_convw[:, db * 4 + 3:db * 4 + 4],
                                xc[:], OP.mult, OP.add))
                            conv_v[(ch, th2, db)] = cV.n

                        @on(block.scalar)
                        def _(a, ch=ch, th2=th2, db=db):
                            a.wait_ge(cV.sem, conv_v[(ch, th2, db)])
                            for k in [(ch, th2 - 1, db), (ch - 1, NT2 - 1, db)]:
                                if k in wx_mm:
                                    a.wait_ge(cT.sem, wx_mm[k])
                                    break
                            a.activation(u_slab[:, db * THF:(db + 1) * THF],
                                         xc[:], AF.Silu)
                            cA.inc(a.activation(sgt2[:], gt2[:], AF.Silu))
                            silu_a[(ch, th2, db)] = cA.n

                        @on(block.sync)
                        def _(q, ch=ch, th2=th2, db=db, tt0=tt0):
                            q.wait_ge(cA.sem, silu_a[(ch, th2, db)])
                            qS.inc(q.dma_start(
                                out=u_s[db * 128:(db + 1) * 128, tt0:tt0 + THF],
                                in_=u_slab[:, db * THF:(db + 1) * THF]), 16)
                            qS.inc(q.dma_start(
                                out=sg_s[db * 128:(db + 1) * 128, tt0:tt0 + THF],
                                in_=sgt2[:]), 16)
                            udma[(ch, th2, db)] = qS.n

                        wxbuf = db % 2

                        @on(block.gpsimd)
                        def _(g, ch=ch, th2=th2, db=db, wxbuf=wxbuf):
                            if ch == 0 and th2 == 0 and db == 0:
                                g.wait_ge(cV.sem, p1b_vec_done)
                            for k in [(ch, th2, db - 2), (ch, th2 - 1, NDB - 2 + db),
                                      (ch - 1, NT2 - 1, NDB - 2 + db)]:
                                if k in wx_mm:
                                    g.wait_ge(cT.sem, wx_mm[k])
                                    break
                            qG.inc(g.dma_start(
                                out=s_wx[wxbuf][:],
                                in_=wxT[db * 128:(db + 1) * 128, :]), 16)
                        wx_dma = qG.n

                        @on(block.tensor)
                        def _(t, ch=ch, th2=th2, db=db, wxbuf=wxbuf, wx_dma=wx_dma):
                            t.wait_ge(cA.sem, silu_a[(ch, th2, db)])
                            t.wait_ge(qG.sem, wx_dma)
                            if db == 0:
                                for k in [(ch, th2 - 1), (ch - 1, NT2 - 1)]:
                                    if k in ssm_evac:
                                        t.wait_ge(cV.sem, ssm_evac[k])
                                        break
                            st = (db == 0)
                            sp = (db == NDB - 1)
                            mv = u_slab[:, db * THF:(db + 1) * THF]
                            t.matmul(ps_ts0[:], s_wx[wxbuf][:, 0:128], mv, start=st, stop=sp)
                            t.matmul(ps_ts1[:], s_wx[wxbuf][:, 128:256], mv, start=st, stop=sp)
                            t.matmul(ps_B[:], s_wx[wxbuf][:, 256:272], mv, start=st, stop=sp)
                            last = t.matmul(ps_C[:], s_wx[wxbuf][:, 272:288], mv,
                                            start=st, stop=sp)
                            cT.inc(last)
                            wx_mm[(ch, th2, db)] = cT.n

                    @on(block.vector)
                    def _(v, ch=ch, th2=th2):
                        v.wait_ge(cT.sem, wx_mm[(ch, th2, NDB - 1)])
                        c0 = th2 * THF
                        v.tensor_copy(ts_sb[0][:, c0:c0 + THF], ps_ts0[:])
                        v.tensor_copy(ts_sb[1][:, c0:c0 + THF], ps_ts1[:])
                        v.tensor_copy(bc_sb[0][:, c0:c0 + THF], ps_B[:])
                        cV.inc(v.tensor_copy(bc_sb[1][:, c0:c0 + THF], ps_C[:]))
                        ssm_evac[(ch, th2)] = cV.n

                @on(block.sync)
                def _(q, ch=ch):
                    q.wait_ge(cV.sem, ssm_evac[(ch, NT2 - 1)])
                    qS.inc(q.dma_start(out=ar_in[0:128, :], in_=ts_sb[0][:]), 16)
                    qS.inc(q.dma_start(out=ar_in[128:256, :], in_=ts_sb[1][:]), 16)
                    qS.inc(q.dma_start(out=ar_in[256:272, :], in_=bc_sb[0][:]), 16)
                    qS.inc(q.dma_start(out=ar_in[272:288, :], in_=bc_sb[1][:]), 16)
                ar_staged = qS.n

                @on(block.gpsimd)
                def _(g, ch=ch, ar_staged=ar_staged):
                    g.wait_ge(qS.sem, ar_staged)
                    cC.inc(g.collective_compute(
                        "AllReduce", OP.add, replica_groups=PAIRS,
                        ins=[ar_in.opt()], outs=[ar_out.opt()]))
                ar_done = cC.n

                @on(block.sync)
                def _(q, ch=ch, ar_done=ar_done):
                    q.wait_ge(cC.sem, ar_done)
                    qS.inc(q.dma_start(out=ts_sb[0][:], in_=ar_out[0:128, :]), 16)
                    qS.inc(q.dma_start(out=ts_sb[1][:], in_=ar_out[128:256, :]), 16)
                    qS.inc(q.dma_start(out=bc_sb[0][:], in_=ar_out[256:272, :]), 16)
                    qS.inc(q.dma_start(out=bc_sb[1][:], in_=ar_out[272:288, :]), 16)
                    if DEBUG:
                        qS.inc(q.dma_start(out=ssm_dbg[ch * 288:(ch + 1) * 288, :],
                                           in_=ar_out), 16)
                ssm_full = qS.n

                @on(block.scalar)
                def _(a, ch=ch, ssm_full=ssm_full):
                    a.wait_ge(qS.sem, ssm_full)
                    cA.inc(a.activation(sq_bf[:], ts_sb[0][:], AF.Square))
                sq0 = cA.n

                @on(block.tensor)
                def _(t, ch=ch, sq0=sq0):
                    t.wait_ge(cA.sem, sq0)
                    last = None
                    for nn in range(2):
                        last = t.matmul(ps_row[:, nn * 512:(nn + 1) * 512], s_ones[:],
                                        sq_bf[:, nn * 512:(nn + 1) * 512],
                                        start=True, stop=False)
                    cT.inc(last)
                mm0 = cT.n

                @on(block.scalar)
                def _(a, ch=ch, mm0=mm0):
                    a.wait_ge(cT.sem, mm0)
                    cA.inc(a.activation(sq_bf[:], ts_sb[1][:], AF.Square))
                sq1 = cA.n

                @on(block.tensor)
                def _(t, ch=ch, sq1=sq1):
                    t.wait_ge(cA.sem, sq1)
                    last = None
                    for nn in range(2):
                        last = t.matmul(ps_row[:, nn * 512:(nn + 1) * 512], s_ones[:],
                                        sq_bf[:, nn * 512:(nn + 1) * 512],
                                        start=False, stop=True)
                    cT.inc(last)
                mm1 = cT.n

                @on(block.scalar)
                def _(a, ch=ch, mm1=mm1):
                    a.wait_ge(cT.sem, mm1)
                    a.activation(row_a[:], ps_row[:], AF.Ln, scale=s_cst[0:1, 2:3], bias=s_cst[0:1, 0:1])
                    cA.inc(a.activation(row_a[:], row_a[:], AF.Exp, scale=s_cst[0:1, 4:5]))
                rstd_ts = cA.n

                @on(block.sync)
                def _(q, rstd_ts=rstd_ts):
                    q.wait_ge(cA.sem, rstd_ts)
                    qS.inc(q.dma_start(out=row_bounce[0:1, :], in_=row_a[:]), 16)
                rb1 = qS.n

                @on(block.gpsimd)
                def _(g, rb1=rb1, ch=ch):
                    g.wait_ge(qS.sem, rb1)
                    qG.inc(g.dma_start(
                        out=rstd_b[:],
                        in_=row_bounce[0:1, :].partition_broadcast(128).squeeze(1)), 16)
                rstd_bc = qG.n

                @on(block.vector)
                def _(v, ch=ch, rstd_bc=rstd_bc):
                    v.wait_ge(qG.sem, rstd_bc)
                    if (ch - 1, NDB - 1) in scan_sp:
                        v.wait_ge(cA.sem, scan_sp[(ch - 1, NDB - 1)])
                    v.tensor_tensor(tsn[0][:], ts_sb[0][:], rstd_b[:], OP.mult)
                    cV.inc(v.tensor_tensor(tsn[1][:], ts_sb[1][:], rstd_b[:], OP.mult))
                tsn_done = cV.n

                bcn_rows = {}
                for bi in range(2):
                    @on(block.scalar)
                    def _(a, ch=ch, bi=bi, tsn_done=tsn_done):
                        if bi == 0:
                            a.wait_ge(cV.sem, tsn_done)
                        cA.inc(a.activation(sq_bf[0:16, :], bc_sb[bi][:], AF.Square))
                    sqb = cA.n

                    @on(block.tensor)
                    def _(t, ch=ch, bi=bi, sqb=sqb):
                        t.wait_ge(cA.sem, sqb)
                        last = None
                        for nn in range(2):
                            last = t.matmul(ps_row[:, nn * 512:(nn + 1) * 512],
                                            s_ones[0:16, :],
                                            sq_bf[0:16, nn * 512:(nn + 1) * 512],
                                            start=True, stop=True)
                        cT.inc(last)
                    mb = cT.n

                    @on(block.scalar)
                    def _(a, ch=ch, bi=bi, mb=mb):
                        a.wait_ge(cT.sem, mb)
                        a.activation(row_b[:], ps_row[:], AF.Ln, scale=s_cst[0:1, 3:4], bias=s_cst[0:1, 0:1])
                        cA.inc(a.activation(row_b[:], row_b[:], AF.Exp, scale=s_cst[0:1, 4:5]))
                    rstdb = cA.n

                    @on(block.sync)
                    def _(q, rstdb=rstdb):
                        q.wait_ge(cA.sem, rstdb)
                        qS.inc(q.dma_start(out=row_bounce[1:2, :], in_=row_b[:]), 16)
                    rbx = qS.n

                    @on(block.gpsimd)
                    def _(g, rbx=rbx, tsn_done=tsn_done):
                        g.wait_ge(qS.sem, rbx)
                        g.wait_ge(cV.sem, tsn_done)
                        qG.inc(g.dma_start(
                            out=rstd_b[:],
                            in_=row_bounce[1:2, :].partition_broadcast(128).squeeze(1)), 16)
                    rbv = qG.n

                    @on(block.vector)
                    def _(v, ch=ch, bi=bi, rbv=rbv):
                        v.wait_ge(qG.sem, rbv)
                        cV.inc(v.tensor_tensor(bcn[bi][:], bc_sb[bi][:],
                                               rstd_b[0:16, :], OP.mult))
                    bcnv = cV.n

                    @on(block.sync)
                    def _(q, bi=bi, bcnv=bcnv):
                        q.wait_ge(cV.sem, bcnv)
                        qS.inc(q.dma_start(
                            out=row_bounce[16 * (1 + bi):16 * (1 + bi) + 16, :],
                            in_=bcn[bi][:]), 16)
                        bcn_rows[bi] = qS.n

                @on(block.gpsimd)
                def _(g, ch=ch):
                    g.wait_ge(qS.sem, bcn_rows[0])
                    g.wait_ge(qS.sem, bcn_rows[1])
                    if (ch - 1, NDB - 1) in prev_fin:
                        g.wait_ge(cV.sem, prev_fin[(ch - 1, NDB - 1)])
                    for n in range(NST):
                        qG.inc(g.dma_start(
                            out=Bb[:, n * TCH:(n + 1) * TCH],
                            in_=row_bounce[16 + n:17 + n, :]
                            .partition_broadcast(128).squeeze(1)), 16)
                        qG.inc(g.dma_start(
                            out=Cb[:, n * TCH:(n + 1) * TCH],
                            in_=row_bounce[32 + n:33 + n, :]
                            .partition_broadcast(128).squeeze(1)), 16)
                bc_slabs = qG.n

                for db in range(NDB):
                    @on(block.sync)
                    def _(q, ch=ch, db=db, t0=t0):
                        for k in [(ch, db - 1), (ch - 1, NDB - 1)]:
                            if k in prev_fin:
                                q.wait_ge(cV.sem, prev_fin[k])
                                break
                        q.wait_ge(qS.sem, udma[(ch, NT2 - 1, db)])
                        qS.inc(q.dma_start(out=u_db[:],
                                           in_=u_s[db * 128:(db + 1) * 128, t0:t0 + TCH]), 16)
                        qS.inc(q.dma_start(out=sg_db[:],
                                           in_=sg_s[db * 128:(db + 1) * 128, t0:t0 + TCH]), 16)
                    db_dma = qS.n

                    @on(block.tensor)
                    def _(t, ch=ch, db=db):
                        t.wait_ge(cV.sem, tsn_done)
                        if ch == 0 and db == 0:
                            t.wait_ge(qG.sem, wdtT_res)
                        for k in [(ch, db - 1), (ch - 1, NDB - 1)]:
                            if k in scan_sp:
                                t.wait_ge(cA.sem, scan_sp[k])
                                break
                        last = None
                        for kt in range(2):
                            for nn in range(2):
                                last = t.matmul(
                                    ps_dt[:, nn * 512:(nn + 1) * 512],
                                    s_wdtT[:, kt * DH + db * 128:kt * DH + (db + 1) * 128],
                                    tsn[kt][:, nn * 512:(nn + 1) * 512],
                                    start=(kt == 0), stop=(kt == 1))
                        cT.inc(last)
                    dt_mm = cT.n

                    @on(block.scalar)
                    def _(a, ch=ch, db=db, dt_mm=dt_mm):
                        a.wait_ge(cT.sem, dt_mm)
                        if DEBUG and ch == 0 and db == 0:
                            cA.inc(a.copy(sp_t[:], ps_dt[:]))
                            _m = cA.n

                            @on(block.sync)
                            def _(q, _m=_m):
                                q.wait_ge(cA.sem, _m)
                                qS.inc(q.dma_start(out=pre_dbg, in_=sp_t[:]), 16)
                                qS.inc(q.dma_start(out=wdt0_dbg, in_=s_wdtT[:, 0:256]), 16)
                            a.wait_ge(qS.sem, qS.n)
                        if ch == 0 and db == 0:
                            a.wait_ge(cV.sem, p0_A)
                        for k in [(ch, db - 1, NST // 2 - 1), (ch - 1, NDB - 1, NST // 2 - 1)]:
                            if k in scan_fin:
                                a.wait_ge(cV.sem, scan_fin[k])
                                break
                        a.activation(dt_f[:], ps_dt[:], AF.Relu, bias=s_bdt[:, db:db + 1])
                        a.activation(sp_t[:], ps_dt[:], AF.Abs, bias=s_bdt[:, db:db + 1])
                        a.activation(sp_t[:], sp_t[:], AF.Exp, scale=s_cst[:, 5:6])
                        cA.inc(a.activation(sp_t[:], sp_t[:], AF.Ln, bias=s_cst[:, 6:7]))
                        scan_sp[(ch, db)] = cA.n

                    @on(block.vector)
                    def _(v, ch=ch, db=db, db_dma=db_dma):
                        v.wait_ge(cA.sem, scan_sp[(ch, db)])
                        v.tensor_tensor(dt_f[:], dt_f[:], sp_t[:], OP.add)
                        v.wait_ge(qS.sem, db_dma)
                        cV.inc(v.tensor_tensor(dtu[:], dt_f[:], u_db[:], OP.mult))
                        scan_fin[(ch, db, "dtu")] = cV.n

                    if DEBUG and ch == 0:
                        @on(block.sync)
                        def _(q, db=db):
                            q.wait_ge(cV.sem, scan_fin[(0, db, "dtu")])
                            qS.inc(q.dma_start(out=dt_dbg[db * 128:(db + 1) * 128, :],
                                               in_=dt_f[:]), 16)

                    for np_i in range(NST // 2):
                        @on(block.scalar)
                        def _(a, ch=ch, db=db, np_i=np_i):
                            a.wait_ge(cV.sem, scan_fin[(ch, db, "dtu")])
                            for k in [(ch, db, np_i - 1), (ch, db - 1, NST // 2 - 1),
                                      (ch - 1, NDB - 1, NST // 2 - 1)]:
                                if k in scan_fin:
                                    a.wait_ge(cV.sem, scan_fin[k])
                                    break
                            n0 = db * NST + 2 * np_i
                            a.activation(dA[:, 0:TCH], dt_f[:], AF.Exp,
                                         scale=s_A[:, n0:n0 + 1])
                            cA.inc(a.activation(dA[:, TCH:2 * TCH], dt_f[:], AF.Exp,
                                                scale=s_A[:, n0 + 1:n0 + 2]))
                        dA_a = cA.n

                        @on(block.vector)
                        def _(v, ch=ch, db=db, np_i=np_i, dA_a=dA_a):
                            n0 = 2 * np_i
                            v.wait_ge(cA.sem, dA_a)
                            if np_i == 0 and db == 0:
                                v.wait_ge(qG.sem, bc_slabs)
                            v.tensor_tensor(
                                Xb[:].rearrange("p (r c) -> p r c", r=2),
                                dtu[:].unsqueeze(1).broadcast_to((128, 2, TCH)),
                                Bb[:, n0 * TCH:(n0 + 2) * TCH]
                                .rearrange("p (r c) -> p r c", r=2), OP.mult)
                            cs = s_carry[:, db * NST + n0:db * NST + n0 + 2]
                            dA3 = dA[:].rearrange("p (r c) -> p r c", r=2)
                            Xb3 = Xb[:].rearrange("p (r c) -> p r c", r=2)
                            v.tensor_tensor(fix[:].unsqueeze(2), cs.unsqueeze(2),
                                            dA3[:, :, 0:1], OP.mult)
                            v.tensor_tensor(Xb3[:, :, 0:1], Xb3[:, :, 0:1],
                                            fix[:].unsqueeze(2), OP.add)
                            v.memset(dA3[:, :, 0:1], 0.0)
                            v.tensor_tensor_scan(st_b[:], dA[:], Xb[:], 0.0,
                                                 OP.mult, OP.add)
                            st3 = st_b[:].rearrange("p (r c) -> p r c", r=2)
                            v.tensor_copy(cs.unsqueeze(2), st3[:, :, TCH - 1:TCH])
                            v.tensor_tensor(
                                Xb3, st3,
                                Cb[:, n0 * TCH:(n0 + 2) * TCH]
                                .rearrange("p (r c) -> p r c", r=2), OP.mult)
                            v.tensor_tensor(Xb[:, 0:TCH], Xb[:, 0:TCH],
                                            Xb[:, TCH:2 * TCH], OP.add)
                            if np_i == 0:
                                cV.inc(v.tensor_copy(yacc[:], Xb[:, 0:TCH]))
                            else:
                                cV.inc(v.tensor_tensor(yacc[:], yacc[:],
                                                       Xb[:, 0:TCH], OP.add))
                            scan_fin[(ch, db, np_i)] = cV.n

                    @on(block.vector)
                    def _(v, ch=ch, db=db):
                        y2v = Xb[:, 0:TCH]
                        y3v = Xb[:].bitcast(BF16)[:, 2 * TCH:3 * TCH]
                        v.scalar_tensor_tensor(y2v, u_db[:], s_D[:, db:db + 1],
                                               yacc[:], OP.mult, OP.add)
                        cV.inc(v.tensor_tensor(y3v, y2v, sg_db[:], OP.mult))
                        prev_fin[(ch, db)] = cV.n
                        scan_fin[(ch, db, NST // 2 - 1)] = cV.n

                    @on(block.sync)
                    def _(q, ch=ch, db=db, t0=t0):
                        q.wait_ge(cV.sem, prev_fin[(ch, db)])
                        qS.inc(q.dma_start(out=y_s[db * 128:(db + 1) * 128, t0:t0 + TCH],
                                           in_=Xb[:].bitcast(BF16)[:, 2 * TCH:3 * TCH]), 16)
                        if DEBUG and ch == 0 and db == NDB - 1:
                            qS.inc(q.dma_start(out=tsn_dbg[0:128, :], in_=tsn[0][:]), 16)
                            qS.inc(q.dma_start(out=tsn_dbg[128:256, :], in_=tsn[1][:]), 16)
            p2_vec_done = cV.n
            p2_y_done = qS.n

        p4 = ExitStack()
        with p4:
            _psum_stack.append(p4); p4.callback(_psum_stack.pop)
            def sb4(n, s, d):
                return p4.enter_context(nc.sbuf_tensor(n, list(s), d))
            p4a = ExitStack()
            p4a.callback(lambda: None)
            _psum_stack.append(p4a)
            yslab = p4a.enter_context(nc.sbuf_tensor("p4_y", [128, NDB * TH], BF16))
            swo = [p4a.enter_context(nc.sbuf_tensor(f"p4_w{i}", [128, 32 * 128], BF16)) for i in range(2)]
            pbf4 = [p4a.enter_context(nc.sbuf_tensor(f"p4_e{i}", [128, TH], BF16)) for i in range(2)]
            pp4 = [psum("p4_ps0", (128, TH)), psum("p4_ps1", (128, TH))]

            wo_mm = {}
            wo_ev = {}
            wo_st = {}
            for cp in range(2):
                t0 = cp * TH

                @on(block.sync)
                def _(q, cp=cp, t0=t0):
                    if cp == 0:
                        q.wait_ge(qS.sem, p2_y_done)
                    else:
                        q.wait_ge(cT.sem, wo_mm[(cp - 1, 31)])
                    for db in range(NDB):
                        qS.inc(q.dma_start(out=yslab[:, db * TH:(db + 1) * TH],
                                           in_=y_s[db * 128:(db + 1) * 128, t0:t0 + TH]), 16)
                yslab_dma = qS.n

                for fb in range(32):
                    buf = fb % 2

                    @on(block.gpsimd)
                    def _(g, cp=cp, fb=fb, buf=buf):
                        if cp == 0 and fb < 2:
                            g.wait_ge(cV.sem, p2_vec_done)
                        if (cp, fb - 2) in wo_mm:
                            g.wait_ge(cT.sem, wo_mm[(cp, fb - 2)])
                        elif cp > 0 and fb < 2:
                            g.wait_ge(cT.sem, wo_mm[(cp - 1, 30 + fb)])
                        qG.inc(g.dma_start(
                            out=swo[buf][:],
                            in_=woutT[:, fb * 128:(fb + 1) * 128]
                            .rearrange("(kb p) c -> p kb c", p=128)), 16)
                    wo_dma = qG.n

                    @on(block.tensor)
                    def _(t, cp=cp, fb=fb, buf=buf, wo_dma=wo_dma, yslab_dma=yslab_dma):
                        if fb == 0:
                            t.wait_ge(qS.sem, yslab_dma)
                        t.wait_ge(qG.sem, wo_dma)
                        if (cp, fb - 2) in wo_ev:
                            t.wait_ge(cA.sem, wo_ev[(cp, fb - 2)])
                        elif cp > 0 and fb < 2:
                            t.wait_ge(cA.sem, wo_ev[(cp - 1, 30 + fb)])
                        last = None
                        for kb in range(NDB):
                            for nn in range(4):
                                last = t.matmul(
                                    pp4[buf][:, nn * 512:(nn + 1) * 512],
                                    swo[buf][:, kb * 128:(kb + 1) * 128],
                                    yslab[:, kb * TH + nn * 512:kb * TH + (nn + 1) * 512],
                                    start=(kb == 0), stop=(kb == NDB - 1))
                        cT.inc(last)
                        wo_mm[(cp, fb)] = cT.n

                    @on(block.scalar)
                    def _(a, cp=cp, fb=fb, buf=buf):
                        a.wait_ge(cT.sem, wo_mm[(cp, fb)])
                        if (cp, fb - 2) in wo_st:
                            a.wait_ge(qS.sem, wo_st[(cp, fb - 2)])
                        elif cp > 0 and fb < 2:
                            a.wait_ge(qS.sem, wo_st[(cp - 1, 30 + fb)])
                        cA.inc(a.copy(pbf4[buf][:], pp4[buf][:]))
                        wo_ev[(cp, fb)] = cA.n

                    @on(block.sync)
                    def _(q, cp=cp, fb=fb, buf=buf):
                        q.wait_ge(cA.sem, wo_ev[(cp, fb)])
                        qS.inc(q.dma_start(
                            out=rs_in[cp * 2 * H + fb * 128:cp * 2 * H + (fb + 1) * 128, :],
                            in_=pbf4[buf][:, 0:TCH]), 16)
                        qS.inc(q.dma_start(
                            out=rs_in[cp * 2 * H + H + fb * 128:cp * 2 * H + H + (fb + 1) * 128, :],
                            in_=pbf4[buf][:, TCH:2 * TCH]), 16)
                        wo_st[(cp, fb)] = qS.n
            rs_staged = qS.n

            @on(block.gpsimd)
            def _(g):
                g.wait_ge(qS.sem, rs_staged)
                cC.inc(g.collective_compute(
                    "ReduceScatter", OP.add, replica_groups=PAIRS,
                    ins=[rs_in.opt()], outs=[rs_out.opt()]))
            rs_done = cC.n
            _psum_stack.pop()
            p4a.close()

            attn_bf = sb4("p4_attn", (128, TH), BF16)
            xres = sb4("p4_xres", (128, TH), F32)
            r1t = sb4("p4_r1", (128, TH), F32)
            r1bf = sb4("p4_r1bf", (128, TH), BF16)
            sq4 = sb4("p4_sq", (128, TH), BF16)
            rowm = sb4("p4_rowm", (1, TH), F32)
            rowv = sb4("p4_rowv", (1, TH), F32)
            rowt = sb4("p4_rowt", (1, TH), F32)
            sc_b = sb4("p4_scb", (128, TH), F32)
            mr_b = sb4("p4_mrb", (128, TH), F32)
            ps_m = psum("p4_pm", (1, TH))
            ps_v = psum("p4_pv", (1, TH))

            ln1 = {}
            for fb in range(32):
                @on(block.sync)
                def _(q, fb=fb, rs_done=rs_done):
                    if fb == 0:
                        q.wait_ge(cC.sem, rs_done)
                    if (fb - 1, "r1") in ln1:
                        q.wait_ge(cV.sem, ln1[(fb - 1, "r1")])
                    qS.inc(q.dma_start(out=attn_bf[:, 0:TCH],
                                       in_=rs_out[fb * 128:(fb + 1) * 128, :]), 16)
                    qS.inc(q.dma_start(out=attn_bf[:, TCH:2 * TCH],
                                       in_=rs_out[H + fb * 128:H + (fb + 1) * 128, :]), 16)
                    qS.inc(q.dma_start(out=xres[:],
                                       in_=xT_my[fb * 128:(fb + 1) * 128, :]), 16)
                    ln1[(fb, "dma")] = qS.n

                @on(block.vector)
                def _(v, fb=fb):
                    v.wait_ge(qS.sem, ln1[(fb, "dma")])
                    if (fb - 1, "cv") in ln1:
                        v.wait_ge(cA.sem, ln1[(fb - 1, "cv")])
                    cV.inc(v.tensor_tensor(r1t[:], attn_bf[:], xres[:], OP.add))
                    ln1[(fb, "r1")] = cV.n

                @on(block.sync)
                def _(q, fb=fb):
                    q.wait_ge(cV.sem, ln1[(fb, "r1")])
                    qS.inc(q.dma_start(out=r1_s[fb * 128:(fb + 1) * 128, :], in_=r1t[:]), 16)
                    ln1[(fb, "r1st")] = qS.n

                @on(block.scalar)
                def _(a, fb=fb):
                    a.wait_ge(cV.sem, ln1[(fb, "r1")])
                    if (fb - 1, "mm") in ln1:
                        a.wait_ge(cT.sem, ln1[(fb - 1, "mm")])
                    a.copy(r1bf[:], r1t[:])
                    cA.inc(a.activation(sq4[:], r1t[:], AF.Square))
                    ln1[(fb, "cv")] = cA.n

                @on(block.tensor)
                def _(t, fb=fb):
                    t.wait_ge(cA.sem, ln1[(fb, "cv")])
                    last = None
                    for nn in range(4):
                        t.matmul(ps_m[:, nn * 512:(nn + 1) * 512], s_ones[:],
                                 r1bf[:, nn * 512:(nn + 1) * 512],
                                 start=(fb == 0), stop=(fb == 31))
                        last = t.matmul(ps_v[:, nn * 512:(nn + 1) * 512], s_ones[:],
                                        sq4[:, nn * 512:(nn + 1) * 512],
                                        start=(fb == 0), stop=(fb == 31))
                    cT.inc(last)
                    ln1[(fb, "mm")] = cT.n

            @on(block.vector)
            def _(v):
                v.wait_ge(cT.sem, ln1[(31, "mm")])
                v.tensor_scalar(rowm[:], ps_m[:], 1.0 / H, None, OP.mult)
                v.tensor_scalar(rowv[:], ps_v[:], 1.0 / H, None, OP.mult)
                v.tensor_tensor(rowt[:], rowm[:], rowm[:], OP.mult)
                cV.inc(v.tensor_tensor(rowv[:], rowv[:], rowt[:], OP.subtract))
            ln1_rows0 = cV.n

            @on(block.scalar)
            def _(a, ln1_rows0=ln1_rows0):
                a.wait_ge(cV.sem, ln1_rows0)
                a.activation(rowv[:], rowv[:], AF.Ln, bias=s_cst[0:1, 1:2])
                cA.inc(a.activation(rowv[:], rowv[:], AF.Exp, scale=s_cst[0:1, 4:5]))
            ln1_rstd = cA.n

            @on(block.vector)
            def _(v, ln1_rstd=ln1_rstd):
                v.wait_ge(cA.sem, ln1_rstd)
                v.tensor_tensor(rowm[:], rowm[:], rowv[:], OP.mult)
                cV.inc(v.tensor_scalar(rowm[:], rowm[:], -1.0, None, OP.mult))
            ln1_rows = cV.n

            @on(block.sync)
            def _(q, ln1_rows=ln1_rows):
                q.wait_ge(cV.sem, ln1_rows)
                qS.inc(q.dma_start(out=row_bounce[0:1, 0:TCH], in_=rowv[0:1, 0:TCH]), 16)
                qS.inc(q.dma_start(out=row_bounce[1:2, 0:TCH], in_=rowv[0:1, TCH:2 * TCH]), 16)
                qS.inc(q.dma_start(out=row_bounce[2:3, 0:TCH], in_=rowm[0:1, 0:TCH]), 16)
                qS.inc(q.dma_start(out=row_bounce[3:4, 0:TCH], in_=rowm[0:1, TCH:2 * TCH]), 16)
            ln1_rows_st = qS.n

            @on(block.gpsimd)
            def _(g, ln1_rows_st=ln1_rows_st):
                g.wait_ge(qS.sem, ln1_rows_st)
                qG.inc(g.dma_start(out=sc_b[:, 0:TCH],
                                   in_=row_bounce[0:1, :].partition_broadcast(128).squeeze(1)), 16)
                qG.inc(g.dma_start(out=sc_b[:, TCH:2 * TCH],
                                   in_=row_bounce[1:2, :].partition_broadcast(128).squeeze(1)), 16)
                qG.inc(g.dma_start(out=mr_b[:, 0:TCH],
                                   in_=row_bounce[2:3, :].partition_broadcast(128).squeeze(1)), 16)
                qG.inc(g.dma_start(out=mr_b[:, TCH:2 * TCH],
                                   in_=row_bounce[3:4, :].partition_broadcast(128).squeeze(1)), 16)
            ln1_bc = qG.n

            for fb in range(32):
                @on(block.sync)
                def _(q, fb=fb):
                    if fb == 0:
                        q.wait_ge(cT.sem, ln1[(31, "mm")])
                    else:
                        q.wait_ge(cV.sem, ln1[(fb - 1, "xf")])
                    q.wait_ge(qS.sem, ln1[(fb, "r1st")])
                    qS.inc(q.dma_start(out=r1t[:], in_=r1_s[fb * 128:(fb + 1) * 128, :]), 16)
                    ln1[(fb, "r1ld")] = qS.n

                @on(block.vector)
                def _(v, fb=fb, ln1_bc=ln1_bc):
                    if fb == 0:
                        v.wait_ge(qG.sem, ln1_bc)
                    v.wait_ge(qS.sem, ln1[(fb, "r1ld")])
                    v.tensor_tensor(r1t[:], r1t[:], sc_b[:], OP.mult)
                    cV.inc(v.tensor_tensor(r1bf[:], r1t[:], mr_b[:], OP.add))
                    ln1[(fb, "xf")] = cV.n

                @on(block.sync)
                def _(q, fb=fb):
                    q.wait_ge(cV.sem, ln1[(fb, "xf")])
                    qS.inc(q.dma_start(out=xf_s[fb * 128:(fb + 1) * 128, :], in_=r1bf[:]), 16)
            p4_vec_done = cV.n
            xf_done = qS.n

        p5 = ExitStack()
        with p5:
            _psum_stack.append(p5); p5.callback(_psum_stack.pop)
            def sb5(n, s, d):
                return p5.enter_context(nc.sbuf_tensor(n, list(s), d))
            xf = sb5("p5_xf", (128, 32 * TH), BF16)
            wgt = [sb5(f"p5_wg{i}", (128, 32 * 128), BF16) for i in range(2)]
            wut = [sb5(f"p5_wu{i}", (128, 32 * 128), BF16) for i in range(2)]
            xfg = sb5("p5_xfg", (32, TH), BF16)
            sg5 = sb5("p5_sg", (128, TH), BF16)
            hbf = sb5("p5_h", (128, TH), BF16)
            s_oneh32 = sb5("p5_oh32", (128, 32 * 32), BF16)
            s_zsg = [sb5(f"p5_zsg{i}", (32, 128), BF16) for i in range(2)]
            s_zsu = [sb5(f"p5_zsu{i}", (32, 128), BF16) for i in range(2)]
            p5x = ExitStack()
            _psum_stack.append(p5x)
            ps_x = psum("p5_px", (32, TH))

            @on(block.sync)
            def _(q, xf_done=xf_done):
                q.wait_ge(qS.sem, xf_done)
                q.wait_ge(qS.sem, zs_store)
                qS.inc(q.dma_start(out=s_oneh32[:], in_=oneh32), 16)
                for kb in range(32):
                    qS.inc(q.dma_start(out=xf[:, kb * TH:(kb + 1) * TH],
                                       in_=xf_s[kb * 128:(kb + 1) * 128, :]), 16)
            xf_slab = qS.n

            @on(block.tensor)
            def _(t, xf_slab=xf_slab):
                t.wait_ge(qS.sem, xf_slab)
                last = None
                for kb in range(32):
                    for nn in range(4):
                        last = t.matmul(ps_x[:, nn * 512:(nn + 1) * 512],
                                        s_oneh32[:, kb * 32:(kb + 1) * 32],
                                        xf[:, kb * TH + nn * 512:kb * TH + (nn + 1) * 512],
                                        start=(kb == 0), stop=(kb == 31))
                cT.inc(last)
            xfg_mm = cT.n

            @on(block.vector)
            def _(v, xfg_mm=xfg_mm):
                v.wait_ge(cT.sem, xfg_mm)
                cV.inc(v.tensor_copy(xfg[:], ps_x[:]))
            xfg_ev = cV.n
            _psum_stack.pop()
            p5x.close()
            psG = psum("p5_pG", (128, TH))
            psU = psum("p5_pU", (128, TH))

            gu = {}
            for ib in range(NKB_D):
                buf = ib % 2

                @on(block.gpsimd)
                def _(g, ib=ib, buf=buf):
                    if (ib - 2, "mmU") in gu:
                        g.wait_ge(cT.sem, gu[(ib - 2, "mmU")])
                    if ib == 0:
                        g.wait_ge(qS.sem, p1b_done)
                        g.wait_ge(cV.sem, p4_vec_done)
                    qG.inc(g.dma_start(
                        out=wgt[buf][:],
                        in_=wg_s[:, ib * 128:(ib + 1) * 128]
                        .rearrange("(kb p) c -> p kb c", p=128)), 16)
                    qG.inc(g.dma_start(
                        out=wut[buf][:],
                        in_=wu_s[:, ib * 128:(ib + 1) * 128]
                        .rearrange("(kb p) c -> p kb c", p=128)), 16)
                    qG.inc(g.dma_start(out=s_zsg[buf][:],
                                       in_=zsg_s[:, ib * 128:(ib + 1) * 128]), 16)
                    qG.inc(g.dma_start(out=s_zsu[buf][:],
                                       in_=zsu_s[:, ib * 128:(ib + 1) * 128]), 16)
                wgu_dma = qG.n

                @on(block.tensor)
                def _(t, ib=ib, buf=buf, wgu_dma=wgu_dma):
                    t.wait_ge(qG.sem, wgu_dma)
                    if ib == 0:
                        t.wait_ge(cV.sem, xfg_ev)
                    if (ib - 1, "sg") in gu:
                        t.wait_ge(cA.sem, gu[(ib - 1, "sg")])
                    last = None
                    for kb in range(32):
                        for nn in range(4):
                            last = t.matmul(psG[:, nn * 512:(nn + 1) * 512],
                                            wgt[buf][:, kb * 128:(kb + 1) * 128],
                                            xf[:, kb * TH + nn * 512:kb * TH + (nn + 1) * 512],
                                            start=(kb == 0), stop=False)
                    for nn in range(4):
                        last = t.matmul(psG[:, nn * 512:(nn + 1) * 512],
                                        s_zsg[buf][:],
                                        xfg[:, nn * 512:(nn + 1) * 512],
                                        start=False, stop=True)
                    cT.inc(last)
                    gu[(ib, "mmG")] = cT.n
                    if (ib - 1, "h") in gu:
                        t.wait_ge(cV.sem, gu[(ib - 1, "h")])
                    for kb in range(32):
                        for nn in range(4):
                            last = t.matmul(psU[:, nn * 512:(nn + 1) * 512],
                                            wut[buf][:, kb * 128:(kb + 1) * 128],
                                            xf[:, kb * TH + nn * 512:kb * TH + (nn + 1) * 512],
                                            start=(kb == 0), stop=False)
                    for nn in range(4):
                        last = t.matmul(psU[:, nn * 512:(nn + 1) * 512],
                                        s_zsu[buf][:],
                                        xfg[:, nn * 512:(nn + 1) * 512],
                                        start=False, stop=True)
                    cT.inc(last)
                    gu[(ib, "mmU")] = cT.n

                @on(block.scalar)
                def _(a, ib=ib):
                    a.wait_ge(cT.sem, gu[(ib, "mmG")])
                    if (ib - 1, "h") in gu:
                        a.wait_ge(cV.sem, gu[(ib - 1, "h")])
                    cA.inc(a.activation(sg5[:], psG[:], AF.Silu))
                    gu[(ib, "sg")] = cA.n

                @on(block.vector)
                def _(v, ib=ib):
                    v.wait_ge(cA.sem, gu[(ib, "sg")])
                    v.wait_ge(cT.sem, gu[(ib, "mmU")])
                    if (ib - 1, "hst") in gu:
                        v.wait_ge(qS.sem, gu[(ib - 1, "hst")])
                    cV.inc(v.tensor_tensor(hbf[:], sg5[:], psU[:], OP.mult))
                    gu[(ib, "h")] = cV.n

                @on(block.sync)
                def _(q, ib=ib):
                    q.wait_ge(cV.sem, gu[(ib, "h")])
                    qS.inc(q.dma_start(out=h_s[ib * 128:(ib + 1) * 128, :], in_=hbf[:]), 16)
                    gu[(ib, "hst")] = qS.n
            h_done = qS.n

        p5b = ExitStack()
        with p5b:
            _psum_stack.append(p5b); p5b.callback(_psum_stack.pop)
            def sb6(n, s, d):
                return p5b.enter_context(nc.sbuf_tensor(n, list(s), d))
            hslab = sb6("p5_hs", (128, NKB_D * 512), BF16)
            wdt = [sb6(f"p5_wd{i}", (128, NKB_D * 128), BF16) for i in range(2)]
            hg = sb6("p5_hg", (86, 512), BF16)
            xf5 = sb6("p5_xf5", (128, 512), BF16)
            r2t = sb6("p5_r2", (128, 512), F32)
            r2bf = sb6("p5_r2bf", (128, 512), BF16)
            sq5 = sb6("p5_sq5", (128, 512), BF16)
            rowm5 = sb6("p5_rm", (1, TH), F32)
            rowv5 = sb6("p5_rv", (1, TH), F32)
            rowt5 = sb6("p5_rt", (1, TH), F32)
            scb5 = sb6("p5_scb", (128, TH), F32)
            mrb5 = sb6("p5_mrb", (128, TH), F32)
            outt = sb6("p5_out", (128, TH), F32)
            s_oneh86 = sb6("p5_oh86", (128, 86 * 86), BF16)
            s_zsd = [sb6(f"p5_zsd{i}", (86, 128), BF16) for i in range(2)]
            psD = [psum("p5_pD0", (128, 512)), psum("p5_pD1", (128, 512))]
            ps_hg = psum("p5_phg", (86, 512))
            ps_m5 = psum("p5_pm5", (1, 512))
            ps_v5 = psum("p5_pv5", (1, 512))

            @on(block.sync)
            def _(q):
                qS.inc(q.dma_start(out=s_oneh86[:], in_=oneh86), 16)
            p5b_consts = qS.n

            dn = {}
            for tt in range(4):
                c0 = tt * 512

                @on(block.sync)
                def _(q, tt=tt, c0=c0, h_done=h_done):
                    if tt == 0:
                        q.wait_ge(qS.sem, h_done)
                    else:
                        q.wait_ge(cT.sem, dn[(tt - 1, 31, "mm")])
                    qS.inc(q.dma_start(
                        out=hslab[:],
                        in_=h_s[:, c0:c0 + 512].rearrange("(kb p) c -> p kb c", p=128)), 16)
                hs_dma = qS.n

                @on(block.tensor)
                def _(t, tt=tt, hs_dma=hs_dma):
                    t.wait_ge(qS.sem, hs_dma)
                    if tt == 0:
                        t.wait_ge(qS.sem, p5b_consts)
                    if (tt - 1, "hg_ev") in dn:
                        t.wait_ge(cV.sem, dn[(tt - 1, "hg_ev")])
                    last = None
                    for kb in range(NKB_D):
                        last = t.matmul(ps_hg[:], s_oneh86[:, kb * 86:(kb + 1) * 86],
                                        hslab[:, kb * 512:(kb + 1) * 512],
                                        start=(kb == 0), stop=(kb == NKB_D - 1))
                    cT.inc(last)
                    dn[(tt, "hg_mm")] = cT.n

                @on(block.vector)
                def _(v, tt=tt):
                    v.wait_ge(cT.sem, dn[(tt, "hg_mm")])
                    cV.inc(v.tensor_copy(hg[:], ps_hg[:]))
                    dn[(tt, "hg_ev")] = cV.n

                for fb in range(32):
                    buf = fb % 2

                    @on(block.gpsimd)
                    def _(g, tt=tt, fb=fb, buf=buf):
                        if tt == 0 and fb < 2:
                            g.wait_ge(cT.sem, gu[(NKB_D - 1, "mmU")])
                        if (tt, fb - 2, "mm") in dn:
                            g.wait_ge(cT.sem, dn[(tt, fb - 2, "mm")])
                        elif tt > 0 and fb < 2:
                            g.wait_ge(cT.sem, dn[(tt - 1, 30 + fb, "mm")])
                        qG.inc(g.dma_start(
                            out=wdt[buf][:],
                            in_=wd_s[:, fb * 128:(fb + 1) * 128]
                            .rearrange("(kb p) c -> p kb c", p=128)), 16)
                        qG.inc(g.dma_start(out=s_zsd[buf][:],
                                           in_=zsd_s[:, fb * 128:(fb + 1) * 128]), 16)
                    wd_dma = qG.n

                    @on(block.sync)
                    def _(q, tt=tt, fb=fb, c0=c0):
                        if (tt, fb - 1, "r2") in dn:
                            q.wait_ge(cV.sem, dn[(tt, fb - 1, "r2")])
                        elif tt > 0 and fb == 0:
                            q.wait_ge(cV.sem, dn[(tt - 1, 31, "r2")])
                        qS.inc(q.dma_start(out=xf5[:],
                                           in_=xf_s[fb * 128:(fb + 1) * 128, c0:c0 + 512]), 16)
                        dn[(tt, fb, "xf")] = qS.n

                    @on(block.tensor)
                    def _(t, tt=tt, fb=fb, buf=buf, wd_dma=wd_dma):
                        t.wait_ge(qG.sem, wd_dma)
                        if fb == 0:
                            t.wait_ge(cV.sem, dn[(tt, "hg_ev")])
                        if (tt, fb - 2, "r2") in dn:
                            t.wait_ge(cV.sem, dn[(tt, fb - 2, "r2")])
                        elif tt > 0 and fb < 2:
                            t.wait_ge(cV.sem, dn[(tt - 1, 30 + fb, "r2")])
                        last = None
                        for kb in range(NKB_D):
                            last = t.matmul(psD[buf][:],
                                            wdt[buf][:, kb * 128:(kb + 1) * 128],
                                            hslab[:, kb * 512:(kb + 1) * 512],
                                            start=(kb == 0), stop=False)
                        last = t.matmul(psD[buf][:], s_zsd[buf][:],
                                        hg[:], start=False, stop=True)
                        cT.inc(last)
                        dn[(tt, fb, "mm")] = cT.n

                    @on(block.vector)
                    def _(v, tt=tt, fb=fb, buf=buf):
                        v.wait_ge(cT.sem, dn[(tt, fb, "mm")])
                        v.wait_ge(qS.sem, dn[(tt, fb, "xf")])
                        if (tt, fb - 1, "sq") in dn:
                            v.wait_ge(cA.sem, dn[(tt, fb - 1, "sq")])
                            v.wait_ge(qS.sem, dn[(tt, fb - 1, "r2st")])
                        elif tt > 0 and fb == 0:
                            v.wait_ge(cA.sem, dn[(tt - 1, 31, "sq")])
                            v.wait_ge(qS.sem, dn[(tt - 1, 31, "r2st")])
                        cV.inc(v.tensor_tensor(r2t[:], psD[buf][:], xf5[:], OP.add))
                        dn[(tt, fb, "r2")] = cV.n

                    @on(block.sync)
                    def _(q, tt=tt, fb=fb, c0=c0):
                        q.wait_ge(cV.sem, dn[(tt, fb, "r2")])
                        qS.inc(q.dma_start(out=r2_s[fb * 128:(fb + 1) * 128, c0:c0 + 512],
                                           in_=r2t[:]), 16)
                        dn[(tt, fb, "r2st")] = qS.n

                    @on(block.scalar)
                    def _(a, tt=tt, fb=fb):
                        a.wait_ge(cV.sem, dn[(tt, fb, "r2")])
                        if (tt, fb - 1, "stats") in dn:
                            a.wait_ge(cT.sem, dn[(tt, fb - 1, "stats")])
                        elif tt > 0 and fb == 0:
                            a.wait_ge(cT.sem, dn[(tt - 1, 31, "stats")])
                        a.copy(r2bf[:], r2t[:])
                        cA.inc(a.activation(sq5[:], r2t[:], AF.Square))
                        dn[(tt, fb, "sq")] = cA.n

                    @on(block.tensor)
                    def _(t, tt=tt, fb=fb, c0=c0):
                        t.wait_ge(cA.sem, dn[(tt, fb, "sq")])
                        if fb == 0 and (tt - 1, "strow") in dn:
                            t.wait_ge(cV.sem, dn[(tt - 1, "strow")])
                        t.matmul(ps_m5[:], s_ones[:], r2bf[:],
                                 start=(fb == 0), stop=(fb == 31))
                        last = t.matmul(ps_v5[:], s_ones[:], sq5[:],
                                        start=(fb == 0), stop=(fb == 31))
                        cT.inc(last)
                        dn[(tt, fb, "stats")] = cT.n

                @on(block.vector)
                def _(v, tt=tt, c0=c0):
                    v.wait_ge(cT.sem, dn[(tt, 31, "stats")])
                    v.tensor_copy(rowm5[0:1, c0:c0 + 512], ps_m5[:])
                    cV.inc(v.tensor_copy(rowv5[0:1, c0:c0 + 512], ps_v5[:]))
                    dn[(tt, "strow")] = cV.n

            @on(block.vector)
            def _(v):
                v.wait_ge(cV.sem, dn[(3, "strow")])
                v.tensor_scalar(rowm5[:], rowm5[:], 1.0 / H, None, OP.mult)
                v.tensor_scalar(rowv5[:], rowv5[:], 1.0 / H, None, OP.mult)
                v.tensor_tensor(rowt5[:], rowm5[:], rowm5[:], OP.mult)
                cV.inc(v.tensor_tensor(rowv5[:], rowv5[:], rowt5[:], OP.subtract))
            ln2_rows0 = cV.n

            @on(block.scalar)
            def _(a, ln2_rows0=ln2_rows0):
                a.wait_ge(cV.sem, ln2_rows0)
                a.activation(rowv5[:], rowv5[:], AF.Ln, bias=s_cst[0:1, 1:2])
                cA.inc(a.activation(rowv5[:], rowv5[:], AF.Exp, scale=s_cst[0:1, 4:5]))
            ln2_rstd = cA.n

            @on(block.vector)
            def _(v, ln2_rstd=ln2_rstd):
                v.wait_ge(cA.sem, ln2_rstd)
                v.tensor_tensor(rowm5[:], rowm5[:], rowv5[:], OP.mult)
                cV.inc(v.tensor_scalar(rowm5[:], rowm5[:], -1.0, None, OP.mult))
            ln2_rows = cV.n

            @on(block.sync)
            def _(q, ln2_rows=ln2_rows):
                q.wait_ge(cV.sem, ln2_rows)
                qS.inc(q.dma_start(out=row_bounce[0:1, 0:TCH], in_=rowv5[0:1, 0:TCH]), 16)
                qS.inc(q.dma_start(out=row_bounce[1:2, 0:TCH], in_=rowv5[0:1, TCH:2 * TCH]), 16)
                qS.inc(q.dma_start(out=row_bounce[2:3, 0:TCH], in_=rowm5[0:1, 0:TCH]), 16)
                qS.inc(q.dma_start(out=row_bounce[3:4, 0:TCH], in_=rowm5[0:1, TCH:2 * TCH]), 16)
            ln2_rows_st = qS.n

            @on(block.gpsimd)
            def _(g, ln2_rows_st=ln2_rows_st):
                g.wait_ge(qS.sem, ln2_rows_st)
                qG.inc(g.dma_start(out=scb5[:, 0:TCH],
                                   in_=row_bounce[0:1, :].partition_broadcast(128).squeeze(1)), 16)
                qG.inc(g.dma_start(out=scb5[:, TCH:2 * TCH],
                                   in_=row_bounce[1:2, :].partition_broadcast(128).squeeze(1)), 16)
                qG.inc(g.dma_start(out=mrb5[:, 0:TCH],
                                   in_=row_bounce[2:3, :].partition_broadcast(128).squeeze(1)), 16)
                qG.inc(g.dma_start(out=mrb5[:, TCH:2 * TCH],
                                   in_=row_bounce[3:4, :].partition_broadcast(128).squeeze(1)), 16)
            ln2_bc = qG.n

            fin = {}
            for fb in range(32):
                @on(block.sync)
                def _(q, fb=fb):
                    if fb >= 1:
                        q.wait_ge(cV.sem, fin[(fb - 1, "o")])
                    qS.inc(q.dma_start(out=outt[:],
                                       in_=r2_s[fb * 128:(fb + 1) * 128, :]), 16)
                    fin[(fb, "ld")] = qS.n

                @on(block.vector)
                def _(v, fb=fb, ln2_bc=ln2_bc):
                    if fb == 0:
                        v.wait_ge(qG.sem, ln2_bc)
                    v.wait_ge(qS.sem, fin[(fb, "ld")])
                    v.tensor_tensor(outt[:], outt[:], scb5[:], OP.mult)
                    cV.inc(v.tensor_tensor(outt[:], outt[:], mrb5[:], OP.add))
                    fin[(fb, "o")] = cV.n

                @on(block.sync)
                def _(q, fb=fb):
                    q.wait_ge(cV.sem, fin[(fb, "o")])
                    qS.inc(q.dma_start(out=out[fb * 128:(fb + 1) * 128, :], in_=outt[:]), 16)

            @on(block.sync)
            def _(q):
                q.wait_ge(qS.sem, qS.n)
                q.wait_ge(qG.sem, qG.n)

    return nc


def host_prep(inputs):
    x = inputs["L0_attn_hidden_states"]
    W_in = inputs["L0_attn_W_in"]
    conv_w = inputs["L0_attn_conv_w"]
    conv_b = inputs["L0_attn_conv_b"]
    W_x = inputs["L0_attn_W_x"]
    W_dt = inputs["L0_attn_W_dt"]
    b_dt = inputs["L0_attn_b_dt"]
    A_log = inputs["L0_attn_A_log"]
    Dp = inputs["L0_attn_D"]
    W_out = inputs["L0_attn_W_out"]

    def C(a):
        return np.ascontiguousarray(a)

    ones_bf = np.ones((128, 1), dtype=ml_dtypes.bfloat16)
    oneh32 = np.zeros((128, 32 * 32), dtype=ml_dtypes.bfloat16)
    for kb in range(32):
        oneh32[:, kb * 32 + kb] = 1
    oneh86 = np.zeros((128, 86 * 86), dtype=ml_dtypes.bfloat16)
    for kb in range(86):
        oneh86[:, kb * 86 + kb] = 1
    shifts = C(np.tile((np.arange(8, dtype=np.int32) * 4)[None, :], (128, 1)))
    mask15 = np.full((128, 1), 15, dtype=np.int32)

    maps = []
    for core in range(8):
        b, h = core // 2, core % 2
        dlo, dhi = h * DH, (h + 1) * DH
        xt = C(x[b].T.astype(np.float32))
        m = {
            "xT": xt,
            "xT_my": C(xt[:, h * TH:(h + 1) * TH]),
            "winT": C(np.concatenate([W_in[dlo:dhi], W_in[DIN + dlo:DIN + dhi]], 0).T),
            "convw": C(conv_w[dlo:dhi].reshape(NDB, 128, 4).transpose(1, 0, 2).reshape(128, NDB * 4)),
            "convb": C(conv_b[dlo:dhi].reshape(NDB, 128).T),
            "bdt": C(b_dt[dlo:dhi].reshape(NDB, 128).T),
            "Dv": C(Dp[dlo:dhi].reshape(NDB, 128).T),
            "alog": C(A_log[dlo:dhi].reshape(NDB, 128, NST).transpose(1, 0, 2).reshape(128, NDB * NST)),
            "wxT": C(W_x[:, dlo:dhi].T),
            "wdtT": C(W_dt[dlo:dhi].T),
            "woutT": C(W_out[:, dlo:dhi].T),
            "gq": np.asarray(inputs["L0_ffn_gate_qweight"]),
            "gs": np.asarray(inputs["L0_ffn_gate_scales"]),
            "gz": np.asarray(inputs["L0_ffn_gate_qzeros"]),
            "uq": np.asarray(inputs["L0_ffn_up_qweight"]),
            "us": np.asarray(inputs["L0_ffn_up_scales"]),
            "uz": np.asarray(inputs["L0_ffn_up_qzeros"]),
            "dq": np.asarray(inputs["L0_ffn_down_qweight"]),
            "ds": np.asarray(inputs["L0_ffn_down_scales"]),
            "dz": np.asarray(inputs["L0_ffn_down_qzeros"]),
            "ones_bf": ones_bf,
            "oneh32": oneh32,
            "oneh86": oneh86,
            "shifts": shifts,
            "mask15": mask15,
            "shifts32": C(shifts[:32]),
            "constv": C(np.tile(np.array([[EPS_RMS, EPS_LN, 1.0 / R, 1.0 / NST, -0.5, -1.0, 1.0, 0.0]], dtype=np.float32), (128, 1))),
            "mask15_32": C(mask15[:32]),
            "shifts86": C(shifts[:86]),
            "mask15_86": C(mask15[:86]),
        }
        maps.append(m)
    return maps


_RES = {}


def kernel(**inputs):
    maps = host_prep(inputs)
    nc = build_nc()
    res = run_bass_kernel_spmd(nc, maps, core_ids=list(range(8)))
    _RES["res"] = res
    out = np.empty((4, L, H), dtype=np.float32)
    for core in range(8):
        b, h = core // 2, core % 2
        out[b, h * TH:(h + 1) * TH, :] = np.asarray(res.results[core]["out"]).T
    return out


if __name__ == "__main__":
    print("building program ...")
    nc = build_nc()
    print("built ok")


# version 03
# speedup vs baseline: 1.1646x; 1.1383x over previous
import os
import numpy as np
import ml_dtypes
import concourse.bass as bass
import concourse.mybir as mybir
from concourse.alu_op_type import AluOpType
from concourse.bass_utils import run_bass_kernel_spmd
from contextlib import ExitStack

F32 = mybir.dt.float32
BF16 = mybir.dt.bfloat16
I32 = mybir.dt.int32
AF = mybir.ActivationFunctionType
OP = AluOpType

H = 4096
DIN = 8192
DH = 4096
NDB = 32
NST = 16
R = 256
L = 4096
TCH = 1024
NCH = L // TCH
THF = 512
TH = 2048
II = 11008
NKB_D = 86
EPS_RMS = 1e-6
EPS_LN = 1e-5

PAIRS = [[0, 1], [2, 3], [4, 5], [6, 7]]
DEBUG = os.environ.get("MK_DEBUG", "") == "1"


class Ctr:
    def __init__(self, stack, nc, name):
        self.sem = stack.enter_context(nc.semaphore(name))
        self.n = 0

    def inc(self, instr, k=1):
        instr.then_inc(self.sem, k)
        self.n += k
        return self.n


def build_nc():
    nc = bass.Bass("TRN2", target_bir_lowering=False)

    def din(name, shape, dt):
        return nc.dram_tensor(name, list(shape), dt, kind="ExternalInput").ap()

    xT = din("xT", (H, L), F32)
    xT_my = din("xT_my", (H, TH), F32)
    winT = din("winT", (H, DIN), F32)
    convw = din("convw", (128, NDB * 4), F32)
    convb = din("convb", (128, NDB), F32)
    bdt = din("bdt", (128, NDB), F32)
    Dv = din("Dv", (128, NDB), F32)
    alog = din("alog", (128, NDB * NST), F32)
    wxT = din("wxT", (DH, 288), F32)
    wdtT = din("wdtT", (R, DH), F32)
    woutT = din("woutT", (DH, H), F32)
    gq = din("gq", (H, II // 8), I32)
    gs = din("gs", (32, II), F32)
    gz = din("gz", (32, II // 8), I32)
    uq = din("uq", (H, II // 8), I32)
    us = din("us", (32, II), F32)
    uz = din("uz", (32, II // 8), I32)
    dq = din("dq", (II, H // 8), I32)
    ds = din("ds", (NKB_D, H), F32)
    dz = din("dz", (NKB_D, H // 8), I32)
    ones_bf = din("ones_bf", (128, 1), BF16)
    oneh32 = din("oneh32", (128, 32 * 32), BF16)
    oneh86 = din("oneh86", (128, 86 * 86), BF16)
    shifts = din("shifts", (128, 8), I32)
    mask15 = din("mask15", (128, 1), I32)
    shifts32 = din("shifts32", (32, 8), I32)
    mask15_32 = din("mask15_32", (32, 1), I32)
    shifts86 = din("shifts86", (86, 8), I32)
    mask15_86 = din("mask15_86", (86, 1), I32)
    constv = din("constv", (128, 8), F32)

    out = nc.dram_tensor("out", [H, TH], F32, kind="ExternalOutput").ap()

    def scratch(name, shape, dt):
        kind = "ExternalOutput" if DEBUG else "Internal"
        return nc.dram_tensor(name, list(shape), dt, kind=kind).ap()

    proj_s = scratch("proj_s", (DIN, L), BF16)
    u_s = scratch("u_s", (DH, L), BF16)
    sg_s = scratch("sg_s", (DH, L), BF16)
    y_s = scratch("y_s", (DH, L), BF16)
    wg_s = scratch("wg_s", (H, II), BF16)
    wu_s = scratch("wu_s", (H, II), BF16)
    wd_s = scratch("wd_s", (II, H), BF16)
    zsg_s = scratch("zsg_s", (32, II), BF16)
    zsu_s = scratch("zsu_s", (32, II), BF16)
    zsd_s = scratch("zsd_s", (86, H), BF16)
    xf_s = scratch("xf_s", (H, TH), BF16)
    r1_s = scratch("r1_s", (H, TH), F32)
    r2_s = scratch("r2_s", (H, TH), F32)
    h_s = scratch("h_s", (II, TH), BF16)
    ssm_dbg = scratch("ssm_dbg", (NCH * 288, TCH), F32) if DEBUG else None
    dt_dbg = scratch("dt_dbg", (DH, TCH), F32) if DEBUG else None
    tsn_dbg = scratch("tsn_dbg", (256, TCH), BF16) if DEBUG else None
    pre_dbg = scratch("pre_dbg", (128, TCH), F32) if DEBUG else None
    wdt0_dbg = scratch("wdt0_dbg", (128, 256), BF16) if DEBUG else None

    ar_in = nc.dram_tensor("ar_in", [288, TCH], F32, kind="Internal").ap()
    ar_out = nc.dram_tensor("ar_out", [288, TCH], F32, kind="Internal").ap()
    rs_in = nc.dram_tensor("rs_in", [4 * H, TCH], BF16, kind="Internal").ap()
    rs_out = nc.dram_tensor("rs_out", [2 * H, TCH], BF16, kind="Internal").ap()
    row_bounce = nc.dram_tensor("row_bounce", [64, TCH], F32, kind="Internal").ap()

    es = ExitStack()
    with es:
        def sb(name, shape, dt):
            return es.enter_context(nc.sbuf_tensor(name, list(shape), dt))

        _psum_stack = [es]

        def psum(name, shape):
            return _psum_stack[-1].enter_context(nc.psum_tensor(name, list(shape), F32))

        qS = Ctr(es, nc, "qS")
        qG = Ctr(es, nc, "qG")
        cT = Ctr(es, nc, "cT")
        cV = Ctr(es, nc, "cV")
        cA = Ctr(es, nc, "cA")
        cC = Ctr(es, nc, "cC")

        block = es.enter_context(nc.Block())

        def on(engine_section):
            def run(fn):
                engine_section(fn)
            return run

        s_convw = sb("s_convw", (128, NDB * 4), F32)
        s_convb = sb("s_convb", (128, NDB), F32)
        s_bdt = sb("s_bdt", (128, NDB), F32)
        s_D = sb("s_D", (128, NDB), F32)
        s_A = sb("s_A", (128, NDB * NST), F32)
        s_ones = sb("s_ones", (128, 1), BF16)
        s_shifts = sb("s_shifts", (128, 8), I32)
        s_mask = sb("s_mask", (128, 1), I32)
        s_carry = sb("s_carry", (128, NDB * NST), F32)
        s_cst = sb("s_cst", (128, 8), F32)

        p0s = ExitStack()
        with p0s:
            def sb0(n, s, d):
                return p0s.enter_context(nc.sbuf_tensor(n, list(s), d))
            s_alog = sb0("s_alog", (128, NDB * NST), F32)
            s_shifts32 = sb0("s_shifts32", (32, 8), I32)
            s_mask32 = sb0("s_mask32", (32, 1), I32)
            s_shifts86 = sb0("s_shifts86", (86, 8), I32)
            s_mask86 = sb0("s_mask86", (86, 1), I32)
            s_sflt = sb0("s_sflt", (86, II), F32)
            s_zq = sb0("s_zq", (86, II // 8), I32)
            s_zi = sb0("s_zi", (86, II), I32)
            s_zs = sb0("s_zs", (86, II), BF16)

            @on(block.sync)
            def _(q):
                for dst, src in [(s_convw, convw), (s_convb, convb), (s_bdt, bdt),
                                 (s_D, Dv), (s_ones, ones_bf),
                                 (s_shifts, shifts), (s_mask, mask15),
                                 (s_shifts32, shifts32), (s_mask32, mask15_32),
                                 (s_shifts86, shifts86), (s_mask86, mask15_86),
                                 (s_alog, alog), (s_cst, constv)]:
                    qS.inc(q.dma_start(out=dst[:], in_=src), 16)
            p0_loads = qS.n

            @on(block.scalar)
            def _(a):
                a.wait_ge(qS.sem, p0_loads)
                cA.inc(a.activation(s_A[:], s_alog[:], AF.Exp))
            p0_exp = cA.n

            @on(block.vector)
            def _(v):
                v.wait_ge(cA.sem, p0_exp)
                v.tensor_scalar(s_A[:], s_A[:], -1.0, None, OP.mult)
                cV.inc(v.memset(s_carry[:], 0.0))
            p0_A = cV.n

            def emit_zs(s_in, z_in, zdst, P, N, shp, msk, prev_st):
                @on(block.sync)
                def _(q):
                    if prev_st:
                        q.wait_ge(qS.sem, prev_st)
                    qS.inc(q.dma_start(out=s_sflt[0:P, 0:N], in_=s_in), 16)
                    qS.inc(q.dma_start(out=s_zq[0:P, 0:N // 8], in_=z_in), 16)
                marker = qS.n

                @on(block.vector)
                def _(v):
                    v.wait_ge(qS.sem, marker)
                    v.wait_ge(qS.sem, p0_loads)
                    for j in range(8):
                        v.tensor_scalar(s_zi[0:P, j:N:8], s_zq[0:P, 0:N // 8],
                                        shp[:, j:j + 1], msk[:],
                                        OP.logical_shift_right, OP.bitwise_and)
                    v.tensor_tensor(s_zs[0:P, 0:N], s_zi[0:P, 0:N],
                                    s_sflt[0:P, 0:N], OP.mult)
                    cV.inc(v.tensor_scalar(s_zs[0:P, 0:N], s_zs[0:P, 0:N],
                                           -1.0, None, OP.mult))
                zz = cV.n

                @on(block.sync)
                def _(q):
                    q.wait_ge(cV.sem, zz)
                    qS.inc(q.dma_start(out=zdst, in_=s_zs[0:P, 0:N]), 16)
                return qS.n

            st1 = emit_zs(gs, gz, zsg_s, 32, II, s_shifts32, s_mask32, None)
            st2 = emit_zs(us, uz, zsu_s, 32, II, s_shifts32, s_mask32, st1)
            zs_store = emit_zs(ds, dz, zsd_s, 86, H, s_shifts86, s_mask86, st2)
        p0_vec_done = cV.n

        p1 = ExitStack()
        with p1:
            _psum_stack.append(p1); p1.callback(_psum_stack.pop)
            def sb1(n, s, d):
                return p1.enter_context(nc.sbuf_tensor(n, list(s), d))
            xbf = sb1("p1_xbf", (128, 32 * TH), BF16)
            swin = [sb1(f"p1_swin{i}", (128, 32 * 128), BF16) for i in range(2)]
            pbf = [sb1(f"p1_pbf{i}", (128, TH), BF16) for i in range(2)]
            pp = [psum("p1_ps0", (128, TH)), psum("p1_ps1", (128, TH))]

            mm_done = {}
            evac_done = {}
            store_done = {}

            for th in range(2):
                t0 = th * TH

                @on(block.gpsimd)
                def _(g, th=th, t0=t0):
                    if th == 0:
                        g.wait_ge(qS.sem, zs_store)
                        g.wait_ge(cV.sem, p0_vec_done)
                        g.wait_ge(cA.sem, p0_exp)
                    else:
                        g.wait_ge(cT.sem, mm_done[(th - 1, 63)])
                    for kb in range(32):
                        qG.inc(g.dma_start(
                            out=xbf[:, kb * TH:(kb + 1) * TH],
                            in_=xT[kb * 128:(kb + 1) * 128, t0:t0 + TH]), 16)

                for fb in range(64):
                    buf = fb % 2

                    @on(block.gpsimd)
                    def _(g, th=th, fb=fb, buf=buf):
                        if fb >= 2:
                            g.wait_ge(cT.sem, mm_done[(th, fb - 2)])
                        qG.inc(g.dma_start(
                            out=swin[buf][:],
                            in_=winT[:, fb * 128:(fb + 1) * 128]
                            .rearrange("(kb p) c -> p kb c", p=128)), 16)
                    win_dma = qG.n

                    @on(block.tensor)
                    def _(t, th=th, fb=fb, buf=buf, win_dma=win_dma):
                        t.wait_ge(qG.sem, win_dma)
                        if fb >= 2:
                            t.wait_ge(cA.sem, evac_done[(th, fb - 2)])
                        last = None
                        for kb in range(32):
                            for nn in range(4):
                                last = t.matmul(
                                    pp[buf][:, nn * 512:(nn + 1) * 512],
                                    swin[buf][:, kb * 128:(kb + 1) * 128],
                                    xbf[:, kb * TH + nn * 512:kb * TH + (nn + 1) * 512],
                                    start=(kb == 0), stop=(kb == 31))
                        cT.inc(last)
                        mm_done[(th, fb)] = cT.n

                    @on(block.scalar)
                    def _(a, th=th, fb=fb, buf=buf):
                        a.wait_ge(cT.sem, mm_done[(th, fb)])
                        if fb >= 2:
                            a.wait_ge(qS.sem, store_done[(th, fb - 2)])
                        cA.inc(a.copy(pbf[buf][:], pp[buf][:]))
                        evac_done[(th, fb)] = cA.n

                    @on(block.sync)
                    def _(q, th=th, fb=fb, buf=buf, t0=t0):
                        q.wait_ge(cA.sem, evac_done[(th, fb)])
                        qS.inc(q.dma_start(
                            out=proj_s[fb * 128:(fb + 1) * 128, t0:t0 + TH],
                            in_=pbf[buf][:]), 16)
                        store_done[(th, fb)] = qS.n
            p1_done = qS.n

        p1b = ExitStack()
        with p1b:
            def sb2(n, s, d):
                return p1b.enter_context(nc.sbuf_tensor(n, list(s), d))
            qtile = [sb2(f"dq_q{i}", (128, II // 8), I32) for i in range(2)]
            sbc = [sb2(f"dq_s{i}", (128, II), BF16) for i in range(2)]
            wint = sb2("dq_wi", (128, II), I32)
            wbf = [sb2(f"dq_w{i}", (128, II), BF16) for i in range(2)]

            steps = ([("g", kb) for kb in range(32)] +
                     [("u", kb) for kb in range(32)] +
                     [("d", kb) for kb in range(NKB_D)])
            dq_v = {}
            dq_st = {}
            for i, (w, kb) in enumerate(steps):
                buf = i % 2
                src = {"g": gq, "u": uq, "d": dq}[w]
                srow = {"g": gs, "u": us, "d": ds}[w]
                dstt = {"g": wg_s, "u": wu_s, "d": wd_s}[w]
                ncols = II if w != "d" else H
                nq = ncols // 8

                @on(block.sync)
                def _(q, i=i, buf=buf, src=src, kb=kb, nq=nq):
                    if i == 0:
                        q.wait_ge(cT.sem, mm_done[(1, 63)])
                    if i >= 2:
                        q.wait_ge(cV.sem, dq_v[i - 2])
                    qS.inc(q.dma_start(out=qtile[buf][:, 0:nq],
                                       in_=src[kb * 128:(kb + 1) * 128, :]), 16)
                dq_q = qS.n

                @on(block.gpsimd)
                def _(g, i=i, buf=buf, srow=srow, kb=kb, ncols=ncols):
                    if i == 0:
                        g.wait_ge(cT.sem, mm_done[(1, 63)])
                    if i >= 2:
                        g.wait_ge(cV.sem, dq_v[i - 2])
                    qG.inc(g.dma_start(
                        out=sbc[buf][:, 0:ncols],
                        in_=srow[kb:kb + 1, :].partition_broadcast(128).squeeze(1)), 16)
                dq_s = qG.n

                @on(block.vector)
                def _(v, i=i, buf=buf, ncols=ncols, nq=nq, dq_q=dq_q, dq_s=dq_s):
                    if i == 0:
                        v.wait_ge(qS.sem, p0_loads)
                    v.wait_ge(qS.sem, dq_q)
                    v.wait_ge(qG.sem, dq_s)
                    if i >= 2:
                        v.wait_ge(qS.sem, dq_st[i - 2])
                    for j in range(8):
                        v.tensor_scalar(wint[:, j:ncols:8], qtile[buf][:, 0:nq],
                                        s_shifts[:, j:j + 1], s_mask[:],
                                        OP.logical_shift_right, OP.bitwise_and)
                    cV.inc(v.tensor_tensor(wbf[buf][:, 0:ncols], wint[:, 0:ncols],
                                           sbc[buf][:, 0:ncols], OP.mult))
                    dq_v[i] = cV.n

                @on(block.sync)
                def _(q, i=i, buf=buf, dstt=dstt, kb=kb, ncols=ncols):
                    q.wait_ge(cV.sem, dq_v[i])
                    qS.inc(q.dma_start(out=dstt[kb * 128:(kb + 1) * 128, :],
                                       in_=wbf[buf][:, 0:ncols]), 16)
                    dq_st[i] = qS.n
            p1b_vec_done = cV.n
            p1b_done = qS.n

        p2 = ExitStack()
        with p2:
            _psum_stack.append(p2); p2.callback(_psum_stack.pop)
            def sb3(n, s, d):
                return p2.enter_context(nc.sbuf_tensor(n, list(s), d))
            u_slab = sb3("p2_u", (128, NDB * THF), BF16)
            xp = [sb3(f"p2_xp{i}", (128, THF + 3), BF16) for i in range(2)]
            xc = [sb3(f"p2_xc{i}", (128, THF), F32) for i in range(2)]
            gt2 = sb3("p2_gt", (128, THF), BF16)
            sgt2 = sb3("p2_sgt", (128, THF), BF16)
            s_wx = [sb3(f"p2_wx{i}", (128, 288), BF16) for i in range(2)]
            s_wdtT = sb3("p2_wdtT", (128, 2 * DH), BF16)
            ts_sb = [sb3(f"p2_ts{i}", (128, TCH), F32) for i in range(2)]
            bc_sb = [sb3(f"p2_bc{i}", (16, TCH), F32) for i in range(2)]
            sq_bf = sb3("p2_sq", (128, TCH), BF16)
            row_a = sb3("p2_rowa", (1, TCH), F32)
            row_b = row_a
            rstd_b = sb3("p2_rstdb", (128, TCH), F32)
            tsn = [sb3(f"p2_tsn{i}", (128, TCH), BF16) for i in range(2)]
            bcn = [sb3(f"p2_bcn{i}", (16, TCH), F32) for i in range(2)]
            Bb = sb3("p2_Bb", (128, NST * TCH), BF16)
            Cb = sb3("p2_Cb", (128, NST * TCH), BF16)
            dt_f = sb3("p2_dt", (128, TCH), F32)
            sp_t = sb3("p2_spt", (128, TCH), F32)


            u_db = sb3("p2_udb", (128, TCH), BF16)

            sg_db = sb3("p2_sgdb", (128, TCH), BF16)
            dA = [sb3(f"p2_dA{i}", (128, 2 * TCH), F32) for i in range(2)]
            Xb = sb3("p2_X", (128, 2 * TCH), BF16)
            st_b = sb3("p2_st", (128, 2 * TCH), BF16)
            yacc = sb3("p2_ys", (128, TCH), BF16)
            y3b = [sb3(f"p2_y3{i}", (128, TCH), BF16) for i in range(2)]
            ps_ts0 = psum("p2_pts0", (128, THF))
            ps_ts1 = psum("p2_pts1", (128, THF))
            ps_B = psum("p2_pB", (16, THF))
            ps_C = psum("p2_pC", (16, THF))
            ps_row = psum("p2_prow", (1, TCH))
            ps_dt = psum("p2_pdt", (128, TCH))

            conv_v = {}
            silu_a = {}
            wx_mm = {}
            ssm_evac = {}
            scan_fin = {}
            scan_sp = {}
            udma = {}
            prev_fin = {}
            y_st = {}

            NT2 = TCH // THF
            dtu_bf = sp_t[:].bitcast(BF16)[:, 0:TCH]

            @on(block.gpsimd)
            def _(g):
                g.wait_ge(cV.sem, p1b_vec_done)
                qG.inc(g.dma_start(out=s_wdtT[:, 0:DH], in_=wdtT[0:128, :]), 16)
                qG.inc(g.dma_start(out=s_wdtT[:, DH:2 * DH], in_=wdtT[128:256, :]), 16)
            wdtT_res = qG.n

            for ch in range(NCH):
                t0 = ch * TCH
                for th2 in range(NT2):
                    tt0 = t0 + th2 * THF
                    for db in range(NDB):
                        def prev_key(d, ch=ch, th2=th2, db=db):
                            for k in [(ch, th2, db - 1), (ch, th2 - 1, NDB - 1),
                                      (ch - 1, NT2 - 1, NDB - 1)]:
                                if k in d:
                                    return k
                            return None

                        def prev_key2(d, ch=ch, th2=th2, db=db):
                            for k in [(ch, th2, db - 2), (ch, th2 - 1, NDB - 2 + db),
                                      (ch - 1, NT2 - 1, NDB - 2 + db)]:
                                if k in d:
                                    return k
                            return None

                        @on(block.sync)
                        def _(q, ch=ch, th2=th2, db=db, tt0=tt0):
                            if ch == 0 and th2 == 0 and db == 0:
                                q.wait_ge(qS.sem, p1_done)
                            pk2 = prev_key2(conv_v)
                            if pk2:
                                q.wait_ge(cV.sem, conv_v[pk2])
                            cb = db % 2
                            lo = tt0 - 3
                            if lo < 0:
                                qS.inc(q.dma_start(
                                    out=xp[cb][:, 3:THF + 3],
                                    in_=proj_s[db * 128:(db + 1) * 128, 0:THF]), 16)
                            else:
                                qS.inc(q.dma_start(
                                    out=xp[cb][:, 0:THF + 3],
                                    in_=proj_s[db * 128:(db + 1) * 128, lo:lo + THF + 3]), 16)
                            pk = prev_key(silu_a)
                            if pk:
                                q.wait_ge(cA.sem, silu_a[pk])
                            qS.inc(q.dma_start(
                                out=gt2[:],
                                in_=proj_s[DH + db * 128:DH + (db + 1) * 128,
                                           tt0:tt0 + THF]), 16)
                        conv_dma = qS.n

                        @on(block.vector)
                        def _(v, ch=ch, th2=th2, db=db, tt0=tt0, conv_dma=conv_dma):
                            v.wait_ge(qS.sem, conv_dma)
                            if ch == 0 and th2 == 0 and db == 0:
                                v.wait_ge(qS.sem, p0_loads)
                            pk = prev_key2(silu_a)
                            if pk:
                                v.wait_ge(cA.sem, silu_a[pk])
                            cb = db % 2
                            if tt0 - 3 < 0:
                                v.memset(xp[cb][:, 0:3], 0.0)
                            v.tensor_scalar(xc[cb][:], xp[cb][:, 0:THF],
                                            s_convw[:, db * 4:db * 4 + 1],
                                            s_convb[:, db:db + 1], OP.mult, OP.add)
                            for j in range(1, 3):
                                v.scalar_tensor_tensor(
                                    xc[cb][:], xp[cb][:, j:j + THF],
                                    s_convw[:, db * 4 + j:db * 4 + j + 1],
                                    xc[cb][:], OP.mult, OP.add)
                            cV.inc(v.scalar_tensor_tensor(
                                xc[cb][:], xp[cb][:, 3:3 + THF],
                                s_convw[:, db * 4 + 3:db * 4 + 4],
                                xc[cb][:], OP.mult, OP.add))
                            conv_v[(ch, th2, db)] = cV.n

                        @on(block.scalar)
                        def _(a, ch=ch, th2=th2, db=db):
                            a.wait_ge(cV.sem, conv_v[(ch, th2, db)])
                            for k in [(ch, th2 - 1, db), (ch - 1, NT2 - 1, db)]:
                                if k in wx_mm:
                                    a.wait_ge(cT.sem, wx_mm[k])
                                    break
                            a.activation(u_slab[:, db * THF:(db + 1) * THF],
                                         xc[db % 2][:], AF.Silu)
                            cA.inc(a.activation(sgt2[:], gt2[:], AF.Silu))
                            silu_a[(ch, th2, db)] = cA.n

                        @on(block.sync)
                        def _(q, ch=ch, th2=th2, db=db, tt0=tt0):
                            q.wait_ge(cA.sem, silu_a[(ch, th2, db)])
                            qS.inc(q.dma_start(
                                out=u_s[db * 128:(db + 1) * 128, tt0:tt0 + THF],
                                in_=u_slab[:, db * THF:(db + 1) * THF]), 16)
                            qS.inc(q.dma_start(
                                out=sg_s[db * 128:(db + 1) * 128, tt0:tt0 + THF],
                                in_=sgt2[:]), 16)
                            udma[(ch, th2, db)] = qS.n

                        wxbuf = db % 2

                        @on(block.gpsimd)
                        def _(g, ch=ch, th2=th2, db=db, wxbuf=wxbuf):
                            if ch == 0 and th2 == 0 and db == 0:
                                g.wait_ge(cV.sem, p1b_vec_done)
                            for k in [(ch, th2, db - 2), (ch, th2 - 1, NDB - 2 + db),
                                      (ch - 1, NT2 - 1, NDB - 2 + db)]:
                                if k in wx_mm:
                                    g.wait_ge(cT.sem, wx_mm[k])
                                    break
                            qG.inc(g.dma_start(
                                out=s_wx[wxbuf][:],
                                in_=wxT[db * 128:(db + 1) * 128, :]), 16)
                        wx_dma = qG.n

                        @on(block.tensor)
                        def _(t, ch=ch, th2=th2, db=db, wxbuf=wxbuf, wx_dma=wx_dma):
                            t.wait_ge(cA.sem, silu_a[(ch, th2, db)])
                            t.wait_ge(qG.sem, wx_dma)
                            if db == 0:
                                for k in [(ch, th2 - 1), (ch - 1, NT2 - 1)]:
                                    if k in ssm_evac:
                                        t.wait_ge(cV.sem, ssm_evac[k])
                                        break
                            st = (db == 0)
                            sp = (db == NDB - 1)
                            mv = u_slab[:, db * THF:(db + 1) * THF]
                            t.matmul(ps_ts0[:], s_wx[wxbuf][:, 0:128], mv, start=st, stop=sp)
                            t.matmul(ps_ts1[:], s_wx[wxbuf][:, 128:256], mv, start=st, stop=sp)
                            t.matmul(ps_B[:], s_wx[wxbuf][:, 256:272], mv, start=st, stop=sp)
                            last = t.matmul(ps_C[:], s_wx[wxbuf][:, 272:288], mv,
                                            start=st, stop=sp)
                            cT.inc(last)
                            wx_mm[(ch, th2, db)] = cT.n

                    @on(block.vector)
                    def _(v, ch=ch, th2=th2):
                        v.wait_ge(cT.sem, wx_mm[(ch, th2, NDB - 1)])
                        c0 = th2 * THF
                        v.tensor_copy(ts_sb[0][:, c0:c0 + THF], ps_ts0[:])
                        v.tensor_copy(ts_sb[1][:, c0:c0 + THF], ps_ts1[:])
                        v.tensor_copy(bc_sb[0][:, c0:c0 + THF], ps_B[:])
                        cV.inc(v.tensor_copy(bc_sb[1][:, c0:c0 + THF], ps_C[:]))
                        ssm_evac[(ch, th2)] = cV.n

                @on(block.sync)
                def _(q, ch=ch):
                    q.wait_ge(cV.sem, ssm_evac[(ch, NT2 - 1)])
                    qS.inc(q.dma_start(out=ar_in[0:128, :], in_=ts_sb[0][:]), 16)
                    qS.inc(q.dma_start(out=ar_in[128:256, :], in_=ts_sb[1][:]), 16)
                    qS.inc(q.dma_start(out=ar_in[256:272, :], in_=bc_sb[0][:]), 16)
                    qS.inc(q.dma_start(out=ar_in[272:288, :], in_=bc_sb[1][:]), 16)
                ar_staged = qS.n

                @on(block.gpsimd)
                def _(g, ch=ch, ar_staged=ar_staged):
                    g.wait_ge(qS.sem, ar_staged)
                    cC.inc(g.collective_compute(
                        "AllReduce", OP.add, replica_groups=PAIRS,
                        ins=[ar_in.opt()], outs=[ar_out.opt()]))
                ar_done = cC.n

                @on(block.sync)
                def _(q, ch=ch, ar_done=ar_done):
                    q.wait_ge(cC.sem, ar_done)
                    qS.inc(q.dma_start(out=ts_sb[0][:], in_=ar_out[0:128, :]), 16)
                    qS.inc(q.dma_start(out=ts_sb[1][:], in_=ar_out[128:256, :]), 16)
                    qS.inc(q.dma_start(out=bc_sb[0][:], in_=ar_out[256:272, :]), 16)
                    qS.inc(q.dma_start(out=bc_sb[1][:], in_=ar_out[272:288, :]), 16)
                    if DEBUG:
                        qS.inc(q.dma_start(out=ssm_dbg[ch * 288:(ch + 1) * 288, :],
                                           in_=ar_out), 16)
                ssm_full = qS.n

                @on(block.scalar)
                def _(a, ch=ch, ssm_full=ssm_full):
                    a.wait_ge(qS.sem, ssm_full)
                    cA.inc(a.activation(sq_bf[:], ts_sb[0][:], AF.Square))
                sq0 = cA.n

                @on(block.tensor)
                def _(t, ch=ch, sq0=sq0):
                    t.wait_ge(cA.sem, sq0)
                    last = None
                    for nn in range(2):
                        last = t.matmul(ps_row[:, nn * 512:(nn + 1) * 512], s_ones[:],
                                        sq_bf[:, nn * 512:(nn + 1) * 512],
                                        start=True, stop=False)
                    cT.inc(last)
                mm0 = cT.n

                @on(block.scalar)
                def _(a, ch=ch, mm0=mm0):
                    a.wait_ge(cT.sem, mm0)
                    cA.inc(a.activation(sq_bf[:], ts_sb[1][:], AF.Square))
                sq1 = cA.n

                @on(block.tensor)
                def _(t, ch=ch, sq1=sq1):
                    t.wait_ge(cA.sem, sq1)
                    last = None
                    for nn in range(2):
                        last = t.matmul(ps_row[:, nn * 512:(nn + 1) * 512], s_ones[:],
                                        sq_bf[:, nn * 512:(nn + 1) * 512],
                                        start=False, stop=True)
                    cT.inc(last)
                mm1 = cT.n

                @on(block.scalar)
                def _(a, ch=ch, mm1=mm1):
                    a.wait_ge(cT.sem, mm1)
                    a.activation(row_a[:], ps_row[:], AF.Ln, scale=s_cst[0:1, 2:3], bias=s_cst[0:1, 0:1])
                    cA.inc(a.activation(row_a[:], row_a[:], AF.Exp, scale=s_cst[0:1, 4:5]))
                rstd_ts = cA.n

                @on(block.sync)
                def _(q, rstd_ts=rstd_ts):
                    q.wait_ge(cA.sem, rstd_ts)
                    qS.inc(q.dma_start(out=row_bounce[0:1, :], in_=row_a[:]), 16)
                rb1 = qS.n

                @on(block.gpsimd)
                def _(g, rb1=rb1, ch=ch):
                    g.wait_ge(qS.sem, rb1)
                    qG.inc(g.dma_start(
                        out=rstd_b[:],
                        in_=row_bounce[0:1, :].partition_broadcast(128).squeeze(1)), 16)
                rstd_bc = qG.n

                @on(block.vector)
                def _(v, ch=ch, rstd_bc=rstd_bc):
                    v.wait_ge(qG.sem, rstd_bc)
                    if (ch - 1, NDB - 1) in scan_sp:
                        v.wait_ge(cA.sem, scan_sp[(ch - 1, NDB - 1)])
                    v.tensor_tensor(tsn[0][:], ts_sb[0][:], rstd_b[:], OP.mult)
                    cV.inc(v.tensor_tensor(tsn[1][:], ts_sb[1][:], rstd_b[:], OP.mult))
                tsn_done = cV.n

                bcn_rows = {}
                for bi in range(2):
                    @on(block.scalar)
                    def _(a, ch=ch, bi=bi, tsn_done=tsn_done):
                        if bi == 0:
                            a.wait_ge(cV.sem, tsn_done)
                        cA.inc(a.activation(sq_bf[0:16, :], bc_sb[bi][:], AF.Square))
                    sqb = cA.n

                    @on(block.tensor)
                    def _(t, ch=ch, bi=bi, sqb=sqb):
                        t.wait_ge(cA.sem, sqb)
                        last = None
                        for nn in range(2):
                            last = t.matmul(ps_row[:, nn * 512:(nn + 1) * 512],
                                            s_ones[0:16, :],
                                            sq_bf[0:16, nn * 512:(nn + 1) * 512],
                                            start=True, stop=True)
                        cT.inc(last)
                    mb = cT.n

                    @on(block.scalar)
                    def _(a, ch=ch, bi=bi, mb=mb):
                        a.wait_ge(cT.sem, mb)
                        a.activation(row_b[:], ps_row[:], AF.Ln, scale=s_cst[0:1, 3:4], bias=s_cst[0:1, 0:1])
                        cA.inc(a.activation(row_b[:], row_b[:], AF.Exp, scale=s_cst[0:1, 4:5]))
                    rstdb = cA.n

                    @on(block.sync)
                    def _(q, rstdb=rstdb):
                        q.wait_ge(cA.sem, rstdb)
                        qS.inc(q.dma_start(out=row_bounce[1:2, :], in_=row_b[:]), 16)
                    rbx = qS.n

                    @on(block.gpsimd)
                    def _(g, rbx=rbx, tsn_done=tsn_done):
                        g.wait_ge(qS.sem, rbx)
                        g.wait_ge(cV.sem, tsn_done)
                        qG.inc(g.dma_start(
                            out=rstd_b[:],
                            in_=row_bounce[1:2, :].partition_broadcast(128).squeeze(1)), 16)
                    rbv = qG.n

                    @on(block.vector)
                    def _(v, ch=ch, bi=bi, rbv=rbv):
                        v.wait_ge(qG.sem, rbv)
                        cV.inc(v.tensor_tensor(bcn[bi][:], bc_sb[bi][:],
                                               rstd_b[0:16, :], OP.mult))
                    bcnv = cV.n

                    @on(block.sync)
                    def _(q, bi=bi, bcnv=bcnv):
                        q.wait_ge(cV.sem, bcnv)
                        qS.inc(q.dma_start(
                            out=row_bounce[16 * (1 + bi):16 * (1 + bi) + 16, :],
                            in_=bcn[bi][:]), 16)
                        bcn_rows[bi] = qS.n

                @on(block.gpsimd)
                def _(g, ch=ch):
                    g.wait_ge(qS.sem, bcn_rows[0])
                    g.wait_ge(qS.sem, bcn_rows[1])
                    if (ch - 1, NDB - 1) in prev_fin:
                        g.wait_ge(cV.sem, prev_fin[(ch - 1, NDB - 1)])
                    for n in range(NST):
                        qG.inc(g.dma_start(
                            out=Bb[:, n * TCH:(n + 1) * TCH],
                            in_=row_bounce[16 + n:17 + n, :]
                            .partition_broadcast(128).squeeze(1)), 16)
                        qG.inc(g.dma_start(
                            out=Cb[:, n * TCH:(n + 1) * TCH],
                            in_=row_bounce[32 + n:33 + n, :]
                            .partition_broadcast(128).squeeze(1)), 16)
                bc_slabs = qG.n

                for db in range(NDB):
                    @on(block.sync)
                    def _(q, ch=ch, db=db, t0=t0):
                        for k in [(ch, db - 1), (ch - 1, NDB - 1)]:
                            if k in prev_fin:
                                q.wait_ge(cV.sem, prev_fin[k])
                                break
                        q.wait_ge(qS.sem, udma[(ch, NT2 - 1, db)])
                        qS.inc(q.dma_start(out=u_db[:],
                                           in_=u_s[db * 128:(db + 1) * 128, t0:t0 + TCH]), 16)
                        qS.inc(q.dma_start(out=sg_db[:],
                                           in_=sg_s[db * 128:(db + 1) * 128, t0:t0 + TCH]), 16)
                    db_dma = qS.n

                    @on(block.tensor)
                    def _(t, ch=ch, db=db):
                        t.wait_ge(cV.sem, tsn_done)
                        if ch == 0 and db == 0:
                            t.wait_ge(qG.sem, wdtT_res)
                        for k in [(ch, db - 1), (ch - 1, NDB - 1)]:
                            if k in scan_sp:
                                t.wait_ge(cA.sem, scan_sp[k])
                                break
                        last = None
                        for kt in range(2):
                            for nn in range(2):
                                last = t.matmul(
                                    ps_dt[:, nn * 512:(nn + 1) * 512],
                                    s_wdtT[:, kt * DH + db * 128:kt * DH + (db + 1) * 128],
                                    tsn[kt][:, nn * 512:(nn + 1) * 512],
                                    start=(kt == 0), stop=(kt == 1))
                        cT.inc(last)
                    dt_mm = cT.n

                    @on(block.scalar)
                    def _(a, ch=ch, db=db, dt_mm=dt_mm):
                        a.wait_ge(cT.sem, dt_mm)
                        if DEBUG and ch == 0 and db == 0:
                            cA.inc(a.copy(sp_t[:], ps_dt[:]))
                            _m = cA.n

                            @on(block.sync)
                            def _(q, _m=_m):
                                q.wait_ge(cA.sem, _m)
                                qS.inc(q.dma_start(out=pre_dbg, in_=sp_t[:]), 16)
                                qS.inc(q.dma_start(out=wdt0_dbg, in_=s_wdtT[:, 0:256]), 16)
                            a.wait_ge(qS.sem, qS.n)
                        if ch == 0 and db == 0:
                            a.wait_ge(cV.sem, p0_A)
                        for k in [(ch, db - 1, NST // 2 - 1), (ch - 1, NDB - 1, NST // 2 - 1)]:
                            if k in scan_fin:
                                a.wait_ge(cV.sem, scan_fin[k])
                                break
                        a.activation(dt_f[:], ps_dt[:], AF.Relu, bias=s_bdt[:, db:db + 1])
                        a.activation(sp_t[:], ps_dt[:], AF.Abs, bias=s_bdt[:, db:db + 1])
                        a.activation(sp_t[:], sp_t[:], AF.Exp, scale=s_cst[:, 5:6])
                        cA.inc(a.activation(sp_t[:], sp_t[:], AF.Ln, bias=s_cst[:, 6:7]))
                        scan_sp[(ch, db)] = cA.n

                    @on(block.vector)
                    def _(v, ch=ch, db=db, db_dma=db_dma):
                        v.wait_ge(cA.sem, scan_sp[(ch, db)])
                        v.tensor_tensor(dt_f[:], dt_f[:], sp_t[:], OP.add)
                        v.wait_ge(qS.sem, db_dma)
                        cV.inc(v.tensor_tensor(dtu_bf[:], dt_f[:], u_db[:], OP.mult))
                        scan_fin[(ch, db, "dtu")] = cV.n

                    if DEBUG and ch == 0:
                        @on(block.sync)
                        def _(q, db=db):
                            q.wait_ge(cV.sem, scan_fin[(0, db, "dtu")])
                            qS.inc(q.dma_start(out=dt_dbg[db * 128:(db + 1) * 128, :],
                                               in_=dt_f[:]), 16)

                    for np_i in range(NST // 2):
                        ab = np_i % 2

                        @on(block.scalar)
                        def _(a, ch=ch, db=db, np_i=np_i, ab=ab):
                            a.wait_ge(cV.sem, scan_fin[(ch, db, "dtu")])
                            for k in [(ch, db, np_i - 2), (ch, db - 1, NST // 2 - 2 + np_i),
                                      (ch - 1, NDB - 1, NST // 2 - 2 + np_i)]:
                                if k in scan_fin:
                                    a.wait_ge(cV.sem, scan_fin[k])
                                    break
                            n0 = db * NST + 2 * np_i
                            a.activation(dA[ab][:, 0:TCH], dt_f[:], AF.Exp,
                                         scale=s_A[:, n0:n0 + 1])
                            cA.inc(a.activation(dA[ab][:, TCH:2 * TCH], dt_f[:], AF.Exp,
                                                scale=s_A[:, n0 + 1:n0 + 2]))
                        dA_a = cA.n

                        @on(block.vector)
                        def _(v, ch=ch, db=db, np_i=np_i, ab=ab, dA_a=dA_a):
                            n0 = 2 * np_i
                            v.wait_ge(cA.sem, dA_a)
                            if np_i == 0 and db == 0:
                                v.wait_ge(qG.sem, bc_slabs)
                            cidx = db * NST + n0
                            v.tensor_tensor(
                                Xb[:].rearrange("p (r c) -> p r c", r=2),
                                dtu_bf[:].unsqueeze(1).broadcast_to((128, 2, TCH)),
                                Bb[:, n0 * TCH:(n0 + 2) * TCH]
                                .rearrange("p (r c) -> p r c", r=2), OP.mult)
                            v.tensor_tensor_scan(st_b[:, 0:TCH], dA[ab][:, 0:TCH],
                                                 Xb[:, 0:TCH],
                                                 s_carry[:, cidx:cidx + 1],
                                                 OP.mult, OP.add)
                            v.tensor_tensor_scan(st_b[:, TCH:2 * TCH],
                                                 dA[ab][:, TCH:2 * TCH],
                                                 Xb[:, TCH:2 * TCH],
                                                 s_carry[:, cidx + 1:cidx + 2],
                                                 OP.mult, OP.add)
                            st3 = st_b[:].rearrange("p (r c) -> p r c", r=2)
                            cs = s_carry[:, cidx:cidx + 2]
                            v.tensor_copy(cs.unsqueeze(2), st3[:, :, TCH - 1:TCH])
                            v.tensor_tensor(
                                st3, st3,
                                Cb[:, n0 * TCH:(n0 + 2) * TCH]
                                .rearrange("p (r c) -> p r c", r=2), OP.mult)
                            if np_i == 0:
                                cV.inc(v.tensor_tensor(yacc[:], st_b[:, 0:TCH],
                                                       st_b[:, TCH:2 * TCH], OP.add))
                            else:
                                v.tensor_tensor(Xb[:, 0:TCH], st_b[:, 0:TCH],
                                                st_b[:, TCH:2 * TCH], OP.add)
                                cV.inc(v.tensor_tensor(yacc[:], yacc[:],
                                                       Xb[:, 0:TCH], OP.add))
                            scan_fin[(ch, db, np_i)] = cV.n

                    @on(block.vector)
                    def _(v, ch=ch, db=db):
                        y2v = dt_f[:]
                        y3v = y3b[db % 2][:]
                        for k in [(ch, db - 2), (ch - 1, NDB - 2 + db)]:
                            if k in y_st:
                                v.wait_ge(qS.sem, y_st[k])
                                break
                        v.scalar_tensor_tensor(y2v, u_db[:], s_D[:, db:db + 1],
                                               yacc[:], OP.mult, OP.add)
                        cV.inc(v.tensor_tensor(y3v, y2v, sg_db[:], OP.mult))
                        prev_fin[(ch, db)] = cV.n
                        scan_fin[(ch, db, NST // 2 - 1)] = cV.n

                    @on(block.sync)
                    def _(q, ch=ch, db=db, t0=t0):
                        q.wait_ge(cV.sem, prev_fin[(ch, db)])
                        qS.inc(q.dma_start(out=y_s[db * 128:(db + 1) * 128, t0:t0 + TCH],
                                           in_=y3b[db % 2][:]), 16)
                        y_st[(ch, db)] = qS.n
                        if DEBUG and ch == 0 and db == NDB - 1:
                            qS.inc(q.dma_start(out=tsn_dbg[0:128, :], in_=tsn[0][:]), 16)
                            qS.inc(q.dma_start(out=tsn_dbg[128:256, :], in_=tsn[1][:]), 16)
            p2_vec_done = cV.n
            p2_y_done = qS.n

        p4 = ExitStack()
        with p4:
            _psum_stack.append(p4); p4.callback(_psum_stack.pop)
            def sb4(n, s, d):
                return p4.enter_context(nc.sbuf_tensor(n, list(s), d))
            p4a = ExitStack()
            p4a.callback(lambda: None)
            _psum_stack.append(p4a)
            yslab = p4a.enter_context(nc.sbuf_tensor("p4_y", [128, NDB * TH], BF16))
            swo = [p4a.enter_context(nc.sbuf_tensor(f"p4_w{i}", [128, 32 * 128], BF16)) for i in range(2)]
            pbf4 = [p4a.enter_context(nc.sbuf_tensor(f"p4_e{i}", [128, TH], BF16)) for i in range(2)]
            pp4 = [psum("p4_ps0", (128, TH)), psum("p4_ps1", (128, TH))]

            wo_mm = {}
            wo_ev = {}
            wo_st = {}
            for cp in range(2):
                t0 = cp * TH

                @on(block.sync)
                def _(q, cp=cp, t0=t0):
                    if cp == 0:
                        q.wait_ge(qS.sem, p2_y_done)
                    else:
                        q.wait_ge(cT.sem, wo_mm[(cp - 1, 31)])
                    for db in range(NDB):
                        qS.inc(q.dma_start(out=yslab[:, db * TH:(db + 1) * TH],
                                           in_=y_s[db * 128:(db + 1) * 128, t0:t0 + TH]), 16)
                yslab_dma = qS.n

                for fb in range(32):
                    buf = fb % 2

                    @on(block.gpsimd)
                    def _(g, cp=cp, fb=fb, buf=buf):
                        if cp == 0 and fb < 2:
                            g.wait_ge(cV.sem, p2_vec_done)
                        if (cp, fb - 2) in wo_mm:
                            g.wait_ge(cT.sem, wo_mm[(cp, fb - 2)])
                        elif cp > 0 and fb < 2:
                            g.wait_ge(cT.sem, wo_mm[(cp - 1, 30 + fb)])
                        qG.inc(g.dma_start(
                            out=swo[buf][:],
                            in_=woutT[:, fb * 128:(fb + 1) * 128]
                            .rearrange("(kb p) c -> p kb c", p=128)), 16)
                    wo_dma = qG.n

                    @on(block.tensor)
                    def _(t, cp=cp, fb=fb, buf=buf, wo_dma=wo_dma, yslab_dma=yslab_dma):
                        if fb == 0:
                            t.wait_ge(qS.sem, yslab_dma)
                        t.wait_ge(qG.sem, wo_dma)
                        if (cp, fb - 2) in wo_ev:
                            t.wait_ge(cA.sem, wo_ev[(cp, fb - 2)])
                        elif cp > 0 and fb < 2:
                            t.wait_ge(cA.sem, wo_ev[(cp - 1, 30 + fb)])
                        last = None
                        for kb in range(NDB):
                            for nn in range(4):
                                last = t.matmul(
                                    pp4[buf][:, nn * 512:(nn + 1) * 512],
                                    swo[buf][:, kb * 128:(kb + 1) * 128],
                                    yslab[:, kb * TH + nn * 512:kb * TH + (nn + 1) * 512],
                                    start=(kb == 0), stop=(kb == NDB - 1))
                        cT.inc(last)
                        wo_mm[(cp, fb)] = cT.n

                    @on(block.scalar)
                    def _(a, cp=cp, fb=fb, buf=buf):
                        a.wait_ge(cT.sem, wo_mm[(cp, fb)])
                        if (cp, fb - 2) in wo_st:
                            a.wait_ge(qS.sem, wo_st[(cp, fb - 2)])
                        elif cp > 0 and fb < 2:
                            a.wait_ge(qS.sem, wo_st[(cp - 1, 30 + fb)])
                        cA.inc(a.copy(pbf4[buf][:], pp4[buf][:]))
                        wo_ev[(cp, fb)] = cA.n

                    @on(block.sync)
                    def _(q, cp=cp, fb=fb, buf=buf):
                        q.wait_ge(cA.sem, wo_ev[(cp, fb)])
                        qS.inc(q.dma_start(
                            out=rs_in[cp * 2 * H + fb * 128:cp * 2 * H + (fb + 1) * 128, :],
                            in_=pbf4[buf][:, 0:TCH]), 16)
                        qS.inc(q.dma_start(
                            out=rs_in[cp * 2 * H + H + fb * 128:cp * 2 * H + H + (fb + 1) * 128, :],
                            in_=pbf4[buf][:, TCH:2 * TCH]), 16)
                        wo_st[(cp, fb)] = qS.n
            rs_staged = qS.n

            @on(block.gpsimd)
            def _(g):
                g.wait_ge(qS.sem, rs_staged)
                cC.inc(g.collective_compute(
                    "ReduceScatter", OP.add, replica_groups=PAIRS,
                    ins=[rs_in.opt()], outs=[rs_out.opt()]))
            rs_done = cC.n
            _psum_stack.pop()
            p4a.close()

            attn_bf = sb4("p4_attn", (128, TH), BF16)
            xres = sb4("p4_xres", (128, TH), F32)
            r1t = sb4("p4_r1", (128, TH), F32)
            r1bf = sb4("p4_r1bf", (128, TH), BF16)
            sq4 = sb4("p4_sq", (128, TH), BF16)
            rowm = sb4("p4_rowm", (1, TH), F32)
            rowv = sb4("p4_rowv", (1, TH), F32)
            rowt = sb4("p4_rowt", (1, TH), F32)
            sc_b = sb4("p4_scb", (128, TH), F32)
            mr_b = sb4("p4_mrb", (128, TH), F32)
            ps_m = psum("p4_pm", (1, TH))
            ps_v = psum("p4_pv", (1, TH))

            ln1 = {}
            for fb in range(32):
                @on(block.sync)
                def _(q, fb=fb, rs_done=rs_done):
                    if fb == 0:
                        q.wait_ge(cC.sem, rs_done)
                    if (fb - 1, "r1") in ln1:
                        q.wait_ge(cV.sem, ln1[(fb - 1, "r1")])
                    qS.inc(q.dma_start(out=attn_bf[:, 0:TCH],
                                       in_=rs_out[fb * 128:(fb + 1) * 128, :]), 16)
                    qS.inc(q.dma_start(out=attn_bf[:, TCH:2 * TCH],
                                       in_=rs_out[H + fb * 128:H + (fb + 1) * 128, :]), 16)
                    qS.inc(q.dma_start(out=xres[:],
                                       in_=xT_my[fb * 128:(fb + 1) * 128, :]), 16)
                    ln1[(fb, "dma")] = qS.n

                @on(block.vector)
                def _(v, fb=fb):
                    v.wait_ge(qS.sem, ln1[(fb, "dma")])
                    if (fb - 1, "cv") in ln1:
                        v.wait_ge(cA.sem, ln1[(fb - 1, "cv")])
                    cV.inc(v.tensor_tensor(r1t[:], attn_bf[:], xres[:], OP.add))
                    ln1[(fb, "r1")] = cV.n

                @on(block.sync)
                def _(q, fb=fb):
                    q.wait_ge(cV.sem, ln1[(fb, "r1")])
                    qS.inc(q.dma_start(out=r1_s[fb * 128:(fb + 1) * 128, :], in_=r1t[:]), 16)
                    ln1[(fb, "r1st")] = qS.n

                @on(block.scalar)
                def _(a, fb=fb):
                    a.wait_ge(cV.sem, ln1[(fb, "r1")])
                    if (fb - 1, "mm") in ln1:
                        a.wait_ge(cT.sem, ln1[(fb - 1, "mm")])
                    a.copy(r1bf[:], r1t[:])
                    cA.inc(a.activation(sq4[:], r1t[:], AF.Square))
                    ln1[(fb, "cv")] = cA.n

                @on(block.tensor)
                def _(t, fb=fb):
                    t.wait_ge(cA.sem, ln1[(fb, "cv")])
                    last = None
                    for nn in range(4):
                        t.matmul(ps_m[:, nn * 512:(nn + 1) * 512], s_ones[:],
                                 r1bf[:, nn * 512:(nn + 1) * 512],
                                 start=(fb == 0), stop=(fb == 31))
                        last = t.matmul(ps_v[:, nn * 512:(nn + 1) * 512], s_ones[:],
                                        sq4[:, nn * 512:(nn + 1) * 512],
                                        start=(fb == 0), stop=(fb == 31))
                    cT.inc(last)
                    ln1[(fb, "mm")] = cT.n

            @on(block.vector)
            def _(v):
                v.wait_ge(cT.sem, ln1[(31, "mm")])
                v.tensor_scalar(rowm[:], ps_m[:], 1.0 / H, None, OP.mult)
                v.tensor_scalar(rowv[:], ps_v[:], 1.0 / H, None, OP.mult)
                v.tensor_tensor(rowt[:], rowm[:], rowm[:], OP.mult)
                cV.inc(v.tensor_tensor(rowv[:], rowv[:], rowt[:], OP.subtract))
            ln1_rows0 = cV.n

            @on(block.scalar)
            def _(a, ln1_rows0=ln1_rows0):
                a.wait_ge(cV.sem, ln1_rows0)
                a.activation(rowv[:], rowv[:], AF.Ln, bias=s_cst[0:1, 1:2])
                cA.inc(a.activation(rowv[:], rowv[:], AF.Exp, scale=s_cst[0:1, 4:5]))
            ln1_rstd = cA.n

            @on(block.vector)
            def _(v, ln1_rstd=ln1_rstd):
                v.wait_ge(cA.sem, ln1_rstd)
                v.tensor_tensor(rowm[:], rowm[:], rowv[:], OP.mult)
                cV.inc(v.tensor_scalar(rowm[:], rowm[:], -1.0, None, OP.mult))
            ln1_rows = cV.n

            @on(block.sync)
            def _(q, ln1_rows=ln1_rows):
                q.wait_ge(cV.sem, ln1_rows)
                qS.inc(q.dma_start(out=row_bounce[0:1, 0:TCH], in_=rowv[0:1, 0:TCH]), 16)
                qS.inc(q.dma_start(out=row_bounce[1:2, 0:TCH], in_=rowv[0:1, TCH:2 * TCH]), 16)
                qS.inc(q.dma_start(out=row_bounce[2:3, 0:TCH], in_=rowm[0:1, 0:TCH]), 16)
                qS.inc(q.dma_start(out=row_bounce[3:4, 0:TCH], in_=rowm[0:1, TCH:2 * TCH]), 16)
            ln1_rows_st = qS.n

            @on(block.gpsimd)
            def _(g, ln1_rows_st=ln1_rows_st):
                g.wait_ge(qS.sem, ln1_rows_st)
                qG.inc(g.dma_start(out=sc_b[:, 0:TCH],
                                   in_=row_bounce[0:1, :].partition_broadcast(128).squeeze(1)), 16)
                qG.inc(g.dma_start(out=sc_b[:, TCH:2 * TCH],
                                   in_=row_bounce[1:2, :].partition_broadcast(128).squeeze(1)), 16)
                qG.inc(g.dma_start(out=mr_b[:, 0:TCH],
                                   in_=row_bounce[2:3, :].partition_broadcast(128).squeeze(1)), 16)
                qG.inc(g.dma_start(out=mr_b[:, TCH:2 * TCH],
                                   in_=row_bounce[3:4, :].partition_broadcast(128).squeeze(1)), 16)
            ln1_bc = qG.n

            for fb in range(32):
                @on(block.sync)
                def _(q, fb=fb):
                    if fb == 0:
                        q.wait_ge(cT.sem, ln1[(31, "mm")])
                    else:
                        q.wait_ge(cV.sem, ln1[(fb - 1, "xf")])
                    q.wait_ge(qS.sem, ln1[(fb, "r1st")])
                    qS.inc(q.dma_start(out=r1t[:], in_=r1_s[fb * 128:(fb + 1) * 128, :]), 16)
                    ln1[(fb, "r1ld")] = qS.n

                @on(block.vector)
                def _(v, fb=fb, ln1_bc=ln1_bc):
                    if fb == 0:
                        v.wait_ge(qG.sem, ln1_bc)
                    v.wait_ge(qS.sem, ln1[(fb, "r1ld")])
                    v.tensor_tensor(r1t[:], r1t[:], sc_b[:], OP.mult)
                    cV.inc(v.tensor_tensor(r1bf[:], r1t[:], mr_b[:], OP.add))
                    ln1[(fb, "xf")] = cV.n

                @on(block.sync)
                def _(q, fb=fb):
                    q.wait_ge(cV.sem, ln1[(fb, "xf")])
                    qS.inc(q.dma_start(out=xf_s[fb * 128:(fb + 1) * 128, :], in_=r1bf[:]), 16)
            p4_vec_done = cV.n
            xf_done = qS.n

        p5 = ExitStack()
        with p5:
            _psum_stack.append(p5); p5.callback(_psum_stack.pop)
            def sb5(n, s, d):
                return p5.enter_context(nc.sbuf_tensor(n, list(s), d))
            xf = sb5("p5_xf", (128, 32 * TH), BF16)
            wgt = [sb5(f"p5_wg{i}", (128, 32 * 128), BF16) for i in range(2)]
            wut = [sb5(f"p5_wu{i}", (128, 32 * 128), BF16) for i in range(2)]
            xfg = sb5("p5_xfg", (32, TH), BF16)
            sg5 = sb5("p5_sg", (128, TH), BF16)
            hbf = sb5("p5_h", (128, TH), BF16)
            s_oneh32 = sb5("p5_oh32", (128, 32 * 32), BF16)
            s_zsg = [sb5(f"p5_zsg{i}", (32, 128), BF16) for i in range(2)]
            s_zsu = [sb5(f"p5_zsu{i}", (32, 128), BF16) for i in range(2)]
            p5x = ExitStack()
            _psum_stack.append(p5x)
            ps_x = psum("p5_px", (32, TH))

            @on(block.sync)
            def _(q, xf_done=xf_done):
                q.wait_ge(qS.sem, xf_done)
                q.wait_ge(qS.sem, zs_store)
                qS.inc(q.dma_start(out=s_oneh32[:], in_=oneh32), 16)
                for kb in range(32):
                    qS.inc(q.dma_start(out=xf[:, kb * TH:(kb + 1) * TH],
                                       in_=xf_s[kb * 128:(kb + 1) * 128, :]), 16)
            xf_slab = qS.n

            @on(block.tensor)
            def _(t, xf_slab=xf_slab):
                t.wait_ge(qS.sem, xf_slab)
                last = None
                for kb in range(32):
                    for nn in range(4):
                        last = t.matmul(ps_x[:, nn * 512:(nn + 1) * 512],
                                        s_oneh32[:, kb * 32:(kb + 1) * 32],
                                        xf[:, kb * TH + nn * 512:kb * TH + (nn + 1) * 512],
                                        start=(kb == 0), stop=(kb == 31))
                cT.inc(last)
            xfg_mm = cT.n

            @on(block.vector)
            def _(v, xfg_mm=xfg_mm):
                v.wait_ge(cT.sem, xfg_mm)
                cV.inc(v.tensor_copy(xfg[:], ps_x[:]))
            xfg_ev = cV.n
            _psum_stack.pop()
            p5x.close()
            psG = psum("p5_pG", (128, TH))
            psU = psum("p5_pU", (128, TH))

            gu = {}
            for ib in range(NKB_D):
                buf = ib % 2

                @on(block.gpsimd)
                def _(g, ib=ib, buf=buf):
                    if (ib - 2, "mmU") in gu:
                        g.wait_ge(cT.sem, gu[(ib - 2, "mmU")])
                    if ib == 0:
                        g.wait_ge(qS.sem, p1b_done)
                        g.wait_ge(cV.sem, p4_vec_done)
                    qG.inc(g.dma_start(
                        out=wgt[buf][:],
                        in_=wg_s[:, ib * 128:(ib + 1) * 128]
                        .rearrange("(kb p) c -> p kb c", p=128)), 16)
                    qG.inc(g.dma_start(
                        out=wut[buf][:],
                        in_=wu_s[:, ib * 128:(ib + 1) * 128]
                        .rearrange("(kb p) c -> p kb c", p=128)), 16)
                    qG.inc(g.dma_start(out=s_zsg[buf][:],
                                       in_=zsg_s[:, ib * 128:(ib + 1) * 128]), 16)
                    qG.inc(g.dma_start(out=s_zsu[buf][:],
                                       in_=zsu_s[:, ib * 128:(ib + 1) * 128]), 16)
                wgu_dma = qG.n

                @on(block.tensor)
                def _(t, ib=ib, buf=buf, wgu_dma=wgu_dma):
                    t.wait_ge(qG.sem, wgu_dma)
                    if ib == 0:
                        t.wait_ge(cV.sem, xfg_ev)
                    if (ib - 1, "sg") in gu:
                        t.wait_ge(cA.sem, gu[(ib - 1, "sg")])
                    last = None
                    for kb in range(32):
                        for nn in range(4):
                            last = t.matmul(psG[:, nn * 512:(nn + 1) * 512],
                                            wgt[buf][:, kb * 128:(kb + 1) * 128],
                                            xf[:, kb * TH + nn * 512:kb * TH + (nn + 1) * 512],
                                            start=(kb == 0), stop=False)
                    for nn in range(4):
                        last = t.matmul(psG[:, nn * 512:(nn + 1) * 512],
                                        s_zsg[buf][:],
                                        xfg[:, nn * 512:(nn + 1) * 512],
                                        start=False, stop=True)
                    cT.inc(last)
                    gu[(ib, "mmG")] = cT.n
                    if (ib - 1, "h") in gu:
                        t.wait_ge(cV.sem, gu[(ib - 1, "h")])
                    for kb in range(32):
                        for nn in range(4):
                            last = t.matmul(psU[:, nn * 512:(nn + 1) * 512],
                                            wut[buf][:, kb * 128:(kb + 1) * 128],
                                            xf[:, kb * TH + nn * 512:kb * TH + (nn + 1) * 512],
                                            start=(kb == 0), stop=False)
                    for nn in range(4):
                        last = t.matmul(psU[:, nn * 512:(nn + 1) * 512],
                                        s_zsu[buf][:],
                                        xfg[:, nn * 512:(nn + 1) * 512],
                                        start=False, stop=True)
                    cT.inc(last)
                    gu[(ib, "mmU")] = cT.n

                @on(block.scalar)
                def _(a, ib=ib):
                    a.wait_ge(cT.sem, gu[(ib, "mmG")])
                    if (ib - 1, "h") in gu:
                        a.wait_ge(cV.sem, gu[(ib - 1, "h")])
                    cA.inc(a.activation(sg5[:], psG[:], AF.Silu))
                    gu[(ib, "sg")] = cA.n

                @on(block.vector)
                def _(v, ib=ib):
                    v.wait_ge(cA.sem, gu[(ib, "sg")])
                    v.wait_ge(cT.sem, gu[(ib, "mmU")])
                    if (ib - 1, "hst") in gu:
                        v.wait_ge(qS.sem, gu[(ib - 1, "hst")])
                    cV.inc(v.tensor_tensor(hbf[:], sg5[:], psU[:], OP.mult))
                    gu[(ib, "h")] = cV.n

                @on(block.sync)
                def _(q, ib=ib):
                    q.wait_ge(cV.sem, gu[(ib, "h")])
                    qS.inc(q.dma_start(out=h_s[ib * 128:(ib + 1) * 128, :], in_=hbf[:]), 16)
                    gu[(ib, "hst")] = qS.n
            h_done = qS.n

        p5b = ExitStack()
        with p5b:
            _psum_stack.append(p5b); p5b.callback(_psum_stack.pop)
            def sb6(n, s, d):
                return p5b.enter_context(nc.sbuf_tensor(n, list(s), d))
            hslab = sb6("p5_hs", (128, NKB_D * 512), BF16)
            wdt = [sb6(f"p5_wd{i}", (128, NKB_D * 128), BF16) for i in range(2)]
            hg = sb6("p5_hg", (86, 512), BF16)
            xf5 = sb6("p5_xf5", (128, 512), BF16)
            r2t = sb6("p5_r2", (128, 512), F32)
            r2bf = sb6("p5_r2bf", (128, 512), BF16)
            sq5 = sb6("p5_sq5", (128, 512), BF16)
            rowm5 = sb6("p5_rm", (1, TH), F32)
            rowv5 = sb6("p5_rv", (1, TH), F32)
            rowt5 = sb6("p5_rt", (1, TH), F32)
            scb5 = sb6("p5_scb", (128, TH), F32)
            mrb5 = sb6("p5_mrb", (128, TH), F32)
            outt = sb6("p5_out", (128, TH), F32)
            s_oneh86 = sb6("p5_oh86", (128, 86 * 86), BF16)
            s_zsd = [sb6(f"p5_zsd{i}", (86, 128), BF16) for i in range(2)]
            psD = [psum("p5_pD0", (128, 512)), psum("p5_pD1", (128, 512))]
            ps_hg = psum("p5_phg", (86, 512))
            ps_m5 = psum("p5_pm5", (1, 512))
            ps_v5 = psum("p5_pv5", (1, 512))

            @on(block.sync)
            def _(q):
                qS.inc(q.dma_start(out=s_oneh86[:], in_=oneh86), 16)
            p5b_consts = qS.n

            dn = {}
            for tt in range(4):
                c0 = tt * 512

                @on(block.sync)
                def _(q, tt=tt, c0=c0, h_done=h_done):
                    if tt == 0:
                        q.wait_ge(qS.sem, h_done)
                    else:
                        q.wait_ge(cT.sem, dn[(tt - 1, 31, "mm")])
                    qS.inc(q.dma_start(
                        out=hslab[:],
                        in_=h_s[:, c0:c0 + 512].rearrange("(kb p) c -> p kb c", p=128)), 16)
                hs_dma = qS.n

                @on(block.tensor)
                def _(t, tt=tt, hs_dma=hs_dma):
                    t.wait_ge(qS.sem, hs_dma)
                    if tt == 0:
                        t.wait_ge(qS.sem, p5b_consts)
                    if (tt - 1, "hg_ev") in dn:
                        t.wait_ge(cV.sem, dn[(tt - 1, "hg_ev")])
                    last = None
                    for kb in range(NKB_D):
                        last = t.matmul(ps_hg[:], s_oneh86[:, kb * 86:(kb + 1) * 86],
                                        hslab[:, kb * 512:(kb + 1) * 512],
                                        start=(kb == 0), stop=(kb == NKB_D - 1))
                    cT.inc(last)
                    dn[(tt, "hg_mm")] = cT.n

                @on(block.vector)
                def _(v, tt=tt):
                    v.wait_ge(cT.sem, dn[(tt, "hg_mm")])
                    cV.inc(v.tensor_copy(hg[:], ps_hg[:]))
                    dn[(tt, "hg_ev")] = cV.n

                for fb in range(32):
                    buf = fb % 2

                    @on(block.gpsimd)
                    def _(g, tt=tt, fb=fb, buf=buf):
                        if tt == 0 and fb < 2:
                            g.wait_ge(cT.sem, gu[(NKB_D - 1, "mmU")])
                        if (tt, fb - 2, "mm") in dn:
                            g.wait_ge(cT.sem, dn[(tt, fb - 2, "mm")])
                        elif tt > 0 and fb < 2:
                            g.wait_ge(cT.sem, dn[(tt - 1, 30 + fb, "mm")])
                        qG.inc(g.dma_start(
                            out=wdt[buf][:],
                            in_=wd_s[:, fb * 128:(fb + 1) * 128]
                            .rearrange("(kb p) c -> p kb c", p=128)), 16)
                        qG.inc(g.dma_start(out=s_zsd[buf][:],
                                           in_=zsd_s[:, fb * 128:(fb + 1) * 128]), 16)
                    wd_dma = qG.n

                    @on(block.sync)
                    def _(q, tt=tt, fb=fb, c0=c0):
                        if (tt, fb - 1, "r2") in dn:
                            q.wait_ge(cV.sem, dn[(tt, fb - 1, "r2")])
                        elif tt > 0 and fb == 0:
                            q.wait_ge(cV.sem, dn[(tt - 1, 31, "r2")])
                        qS.inc(q.dma_start(out=xf5[:],
                                           in_=xf_s[fb * 128:(fb + 1) * 128, c0:c0 + 512]), 16)
                        dn[(tt, fb, "xf")] = qS.n

                    @on(block.tensor)
                    def _(t, tt=tt, fb=fb, buf=buf, wd_dma=wd_dma):
                        t.wait_ge(qG.sem, wd_dma)
                        if fb == 0:
                            t.wait_ge(cV.sem, dn[(tt, "hg_ev")])
                        if (tt, fb - 2, "r2") in dn:
                            t.wait_ge(cV.sem, dn[(tt, fb - 2, "r2")])
                        elif tt > 0 and fb < 2:
                            t.wait_ge(cV.sem, dn[(tt - 1, 30 + fb, "r2")])
                        last = None
                        for kb in range(NKB_D):
                            last = t.matmul(psD[buf][:],
                                            wdt[buf][:, kb * 128:(kb + 1) * 128],
                                            hslab[:, kb * 512:(kb + 1) * 512],
                                            start=(kb == 0), stop=False)
                        last = t.matmul(psD[buf][:], s_zsd[buf][:],
                                        hg[:], start=False, stop=True)
                        cT.inc(last)
                        dn[(tt, fb, "mm")] = cT.n

                    @on(block.vector)
                    def _(v, tt=tt, fb=fb, buf=buf):
                        v.wait_ge(cT.sem, dn[(tt, fb, "mm")])
                        v.wait_ge(qS.sem, dn[(tt, fb, "xf")])
                        if (tt, fb - 1, "sq") in dn:
                            v.wait_ge(cA.sem, dn[(tt, fb - 1, "sq")])
                            v.wait_ge(qS.sem, dn[(tt, fb - 1, "r2st")])
                        elif tt > 0 and fb == 0:
                            v.wait_ge(cA.sem, dn[(tt - 1, 31, "sq")])
                            v.wait_ge(qS.sem, dn[(tt - 1, 31, "r2st")])
                        cV.inc(v.tensor_tensor(r2t[:], psD[buf][:], xf5[:], OP.add))
                        dn[(tt, fb, "r2")] = cV.n

                    @on(block.sync)
                    def _(q, tt=tt, fb=fb, c0=c0):
                        q.wait_ge(cV.sem, dn[(tt, fb, "r2")])
                        qS.inc(q.dma_start(out=r2_s[fb * 128:(fb + 1) * 128, c0:c0 + 512],
                                           in_=r2t[:]), 16)
                        dn[(tt, fb, "r2st")] = qS.n

                    @on(block.scalar)
                    def _(a, tt=tt, fb=fb):
                        a.wait_ge(cV.sem, dn[(tt, fb, "r2")])
                        if (tt, fb - 1, "stats") in dn:
                            a.wait_ge(cT.sem, dn[(tt, fb - 1, "stats")])
                        elif tt > 0 and fb == 0:
                            a.wait_ge(cT.sem, dn[(tt - 1, 31, "stats")])
                        a.copy(r2bf[:], r2t[:])
                        cA.inc(a.activation(sq5[:], r2t[:], AF.Square))
                        dn[(tt, fb, "sq")] = cA.n

                    @on(block.tensor)
                    def _(t, tt=tt, fb=fb, c0=c0):
                        t.wait_ge(cA.sem, dn[(tt, fb, "sq")])
                        if fb == 0 and (tt - 1, "strow") in dn:
                            t.wait_ge(cV.sem, dn[(tt - 1, "strow")])
                        t.matmul(ps_m5[:], s_ones[:], r2bf[:],
                                 start=(fb == 0), stop=(fb == 31))
                        last = t.matmul(ps_v5[:], s_ones[:], sq5[:],
                                        start=(fb == 0), stop=(fb == 31))
                        cT.inc(last)
                        dn[(tt, fb, "stats")] = cT.n

                @on(block.vector)
                def _(v, tt=tt, c0=c0):
                    v.wait_ge(cT.sem, dn[(tt, 31, "stats")])
                    v.tensor_copy(rowm5[0:1, c0:c0 + 512], ps_m5[:])
                    cV.inc(v.tensor_copy(rowv5[0:1, c0:c0 + 512], ps_v5[:]))
                    dn[(tt, "strow")] = cV.n

            @on(block.vector)
            def _(v):
                v.wait_ge(cV.sem, dn[(3, "strow")])
                v.tensor_scalar(rowm5[:], rowm5[:], 1.0 / H, None, OP.mult)
                v.tensor_scalar(rowv5[:], rowv5[:], 1.0 / H, None, OP.mult)
                v.tensor_tensor(rowt5[:], rowm5[:], rowm5[:], OP.mult)
                cV.inc(v.tensor_tensor(rowv5[:], rowv5[:], rowt5[:], OP.subtract))
            ln2_rows0 = cV.n

            @on(block.scalar)
            def _(a, ln2_rows0=ln2_rows0):
                a.wait_ge(cV.sem, ln2_rows0)
                a.activation(rowv5[:], rowv5[:], AF.Ln, bias=s_cst[0:1, 1:2])
                cA.inc(a.activation(rowv5[:], rowv5[:], AF.Exp, scale=s_cst[0:1, 4:5]))
            ln2_rstd = cA.n

            @on(block.vector)
            def _(v, ln2_rstd=ln2_rstd):
                v.wait_ge(cA.sem, ln2_rstd)
                v.tensor_tensor(rowm5[:], rowm5[:], rowv5[:], OP.mult)
                cV.inc(v.tensor_scalar(rowm5[:], rowm5[:], -1.0, None, OP.mult))
            ln2_rows = cV.n

            @on(block.sync)
            def _(q, ln2_rows=ln2_rows):
                q.wait_ge(cV.sem, ln2_rows)
                qS.inc(q.dma_start(out=row_bounce[0:1, 0:TCH], in_=rowv5[0:1, 0:TCH]), 16)
                qS.inc(q.dma_start(out=row_bounce[1:2, 0:TCH], in_=rowv5[0:1, TCH:2 * TCH]), 16)
                qS.inc(q.dma_start(out=row_bounce[2:3, 0:TCH], in_=rowm5[0:1, 0:TCH]), 16)
                qS.inc(q.dma_start(out=row_bounce[3:4, 0:TCH], in_=rowm5[0:1, TCH:2 * TCH]), 16)
            ln2_rows_st = qS.n

            @on(block.gpsimd)
            def _(g, ln2_rows_st=ln2_rows_st):
                g.wait_ge(qS.sem, ln2_rows_st)
                qG.inc(g.dma_start(out=scb5[:, 0:TCH],
                                   in_=row_bounce[0:1, :].partition_broadcast(128).squeeze(1)), 16)
                qG.inc(g.dma_start(out=scb5[:, TCH:2 * TCH],
                                   in_=row_bounce[1:2, :].partition_broadcast(128).squeeze(1)), 16)
                qG.inc(g.dma_start(out=mrb5[:, 0:TCH],
                                   in_=row_bounce[2:3, :].partition_broadcast(128).squeeze(1)), 16)
                qG.inc(g.dma_start(out=mrb5[:, TCH:2 * TCH],
                                   in_=row_bounce[3:4, :].partition_broadcast(128).squeeze(1)), 16)
            ln2_bc = qG.n

            fin = {}
            for fb in range(32):
                @on(block.sync)
                def _(q, fb=fb):
                    if fb >= 1:
                        q.wait_ge(cV.sem, fin[(fb - 1, "o")])
                    qS.inc(q.dma_start(out=outt[:],
                                       in_=r2_s[fb * 128:(fb + 1) * 128, :]), 16)
                    fin[(fb, "ld")] = qS.n

                @on(block.vector)
                def _(v, fb=fb, ln2_bc=ln2_bc):
                    if fb == 0:
                        v.wait_ge(qG.sem, ln2_bc)
                    v.wait_ge(qS.sem, fin[(fb, "ld")])
                    v.tensor_tensor(outt[:], outt[:], scb5[:], OP.mult)
                    cV.inc(v.tensor_tensor(outt[:], outt[:], mrb5[:], OP.add))
                    fin[(fb, "o")] = cV.n

                @on(block.sync)
                def _(q, fb=fb):
                    q.wait_ge(cV.sem, fin[(fb, "o")])
                    qS.inc(q.dma_start(out=out[fb * 128:(fb + 1) * 128, :], in_=outt[:]), 16)

            @on(block.sync)
            def _(q):
                q.wait_ge(qS.sem, qS.n)
                q.wait_ge(qG.sem, qG.n)

    return nc


def host_prep(inputs):
    x = inputs["L0_attn_hidden_states"]
    W_in = inputs["L0_attn_W_in"]
    conv_w = inputs["L0_attn_conv_w"]
    conv_b = inputs["L0_attn_conv_b"]
    W_x = inputs["L0_attn_W_x"]
    W_dt = inputs["L0_attn_W_dt"]
    b_dt = inputs["L0_attn_b_dt"]
    A_log = inputs["L0_attn_A_log"]
    Dp = inputs["L0_attn_D"]
    W_out = inputs["L0_attn_W_out"]

    def C(a):
        return np.ascontiguousarray(a)

    ones_bf = np.ones((128, 1), dtype=ml_dtypes.bfloat16)
    oneh32 = np.zeros((128, 32 * 32), dtype=ml_dtypes.bfloat16)
    for kb in range(32):
        oneh32[:, kb * 32 + kb] = 1
    oneh86 = np.zeros((128, 86 * 86), dtype=ml_dtypes.bfloat16)
    for kb in range(86):
        oneh86[:, kb * 86 + kb] = 1
    shifts = C(np.tile((np.arange(8, dtype=np.int32) * 4)[None, :], (128, 1)))
    mask15 = np.full((128, 1), 15, dtype=np.int32)

    maps = []
    for core in range(8):
        b, h = core // 2, core % 2
        dlo, dhi = h * DH, (h + 1) * DH
        xt = C(x[b].T.astype(np.float32))
        m = {
            "xT": xt,
            "xT_my": C(xt[:, h * TH:(h + 1) * TH]),
            "winT": C(np.concatenate([W_in[dlo:dhi], W_in[DIN + dlo:DIN + dhi]], 0).T),
            "convw": C(conv_w[dlo:dhi].reshape(NDB, 128, 4).transpose(1, 0, 2).reshape(128, NDB * 4)),
            "convb": C(conv_b[dlo:dhi].reshape(NDB, 128).T),
            "bdt": C(b_dt[dlo:dhi].reshape(NDB, 128).T),
            "Dv": C(Dp[dlo:dhi].reshape(NDB, 128).T),
            "alog": C(A_log[dlo:dhi].reshape(NDB, 128, NST).transpose(1, 0, 2).reshape(128, NDB * NST)),
            "wxT": C(W_x[:, dlo:dhi].T),
            "wdtT": C(W_dt[dlo:dhi].T),
            "woutT": C(W_out[:, dlo:dhi].T),
            "gq": np.asarray(inputs["L0_ffn_gate_qweight"]),
            "gs": np.asarray(inputs["L0_ffn_gate_scales"]),
            "gz": np.asarray(inputs["L0_ffn_gate_qzeros"]),
            "uq": np.asarray(inputs["L0_ffn_up_qweight"]),
            "us": np.asarray(inputs["L0_ffn_up_scales"]),
            "uz": np.asarray(inputs["L0_ffn_up_qzeros"]),
            "dq": np.asarray(inputs["L0_ffn_down_qweight"]),
            "ds": np.asarray(inputs["L0_ffn_down_scales"]),
            "dz": np.asarray(inputs["L0_ffn_down_qzeros"]),
            "ones_bf": ones_bf,
            "oneh32": oneh32,
            "oneh86": oneh86,
            "shifts": shifts,
            "mask15": mask15,
            "shifts32": C(shifts[:32]),
            "constv": C(np.tile(np.array([[EPS_RMS, EPS_LN, 1.0 / R, 1.0 / NST, -0.5, -1.0, 1.0, 0.0]], dtype=np.float32), (128, 1))),
            "mask15_32": C(mask15[:32]),
            "shifts86": C(shifts[:86]),
            "mask15_86": C(mask15[:86]),
        }
        maps.append(m)
    return maps


_RES = {}


def kernel(**inputs):
    maps = host_prep(inputs)
    nc = build_nc()
    res = run_bass_kernel_spmd(nc, maps, core_ids=list(range(8)))
    _RES["res"] = res
    out = np.empty((4, L, H), dtype=np.float32)
    for core in range(8):
        b, h = core // 2, core % 2
        out[b, h * TH:(h + 1) * TH, :] = np.asarray(res.results[core]["out"]).T
    return out


if __name__ == "__main__":
    print("building program ...")
    nc = build_nc()
    print("built ok")
